# version 09
# baseline (speedup 1.0000x reference)
import numpy as np
import ml_dtypes
import concourse.bass as bass
import concourse.mybir as mybir
from concourse.bass_utils import run_bass_kernel_spmd

F32 = mybir.dt.float32
BF16 = mybir.dt.bfloat16
I32 = mybir.dt.int32
AF = mybir.ActivationFunctionType
ALU = mybir.AluOpType
AX = mybir.AxisListType
bfdt = ml_dtypes.bfloat16

D = 2048
DC = D // 128
T = 2048
B, S = 4, 4096
NCORES = 8


class Sem:
    def __init__(self, h):
        self.h = h
        self.v = 0

    def dma(self, instr):
        instr.then_inc(self.h, 16)
        self.v += 16
        return self.v

    def tick(self, instr, n=1):
        instr.then_inc(self.h, n)
        self.v += n
        return self.v


ZEP = (0, 0)


def vmax(*vals):
    out = [0, 0]
    for v in vals:
        if v == 0:
            continue
        out[0] = max(out[0], v[0])
        out[1] = max(out[1], v[1])
    return (out[0], out[1])


class EpSem:
    def __init__(self, hv, hs):
        self.hv, self.hs = hv, hs
        self.nv = 0
        self.ns = 0

    def tick(self, instr, n=1):
        e = str(instr.ins.engine)
        if "DVE" in e or "Pool" in e:
            instr.then_inc(self.hv, n)
            self.nv += n
        else:
            instr.then_inc(self.hs, n)
            self.ns += n
        return self.v

    @property
    def v(self):
        return (self.nv, self.ns)

    def wait(self, eng, val):
        if val == 0:
            return
        nv, ns = val
        if nv:
            eng.wait_ge(self.hv, nv)
        if ns:
            eng.wait_ge(self.hs, ns)

    def reserve_s(self, n=1):
        self.ns += n
        return self.v


class ScalBatch:
    def __init__(self, block):
        self.block = block
        self.fns = []

    def add(self, fn):
        self.fns.append(fn)

    def flush(self):
        if self.fns:
            fns = tuple(self.fns)
            self.fns = []
            self.block.scalar(lambda s, fns=fns: [f(s) for f in fns])



def emit_layernorm(block, sems, xbf, psA, psB, ones1, onesc, sc, ntok=T):
    s_mm, s_ep = sems["mm"], sems["ep"]
    TGn = ntok // 512
    x2, mrow, vrow, trow = sc["x2"], sc["mrow"], sc["vrow"], sc["trow"]
    mbfr, rbfr, mbc, rbc = sc["mbfr"], sc["rbfr"], sc["mbc"], sc["rbc"]

    sb = ScalBatch(block)
    for tg in range(TGn):
        t0 = tg * 512
        for k in range(DC):
            wait_mm = s_mm.v

            def sq(s, k=k, t0=t0, wait_mm=wait_mm):
                s.wait_ge(s_mm.h, wait_mm)
                s.activation(out=x2[:], in_=xbf[:, k * ntok + t0:k * ntok + t0 + 512],
                             func=AF.Square).then_inc(s_ep.hs, 1)
            sb.add(sq)
            s_ep.reserve_s(1)
            sq_done = s_ep.v

            def mm(t, k=k, t0=t0, sq_done=sq_done):
                s_ep.wait(t, sq_done)
                t.matmul(psA[0:1, 0:512], onesc[:], xbf[:, k * ntok + t0:k * ntok + t0 + 512],
                         start=(k == 0), stop=(k == DC - 1))
                s_mm.tick(t.matmul(psB[0:1, 0:512], onesc[:], x2[:],
                                   start=(k == 0), stop=(k == DC - 1)))
            block.tensor(mm)
        stats_done = s_mm.v

        def rowmath(v, stats_done=stats_done):
            v.wait_ge(s_mm.h, stats_done)
            v.tensor_scalar_mul(mrow[:], psA[0:1, 0:512], 1.0 / D)
            v.tensor_scalar_mul(vrow[:], psB[0:1, 0:512], 1.0 / D)
            v.tensor_mul(trow[:], mrow[:], mrow[:])
            v.tensor_sub(vrow[:], vrow[:], trow[:])
            s_ep.tick(v.tensor_scalar_add(vrow[:], vrow[:], 1e-5))
        block.vector(rowmath)
        var_done = s_ep.v

        def rstd(s, var_done=var_done):
            s_ep.wait(s, var_done)
            s.activation(out=trow[:], in_=vrow[:], func=AF.Ln)
            s.activation(out=trow[:], in_=trow[:], func=AF.Exp, scale=-0.5).then_inc(s_ep.hs, 1)
        sb.add(rstd)
        s_ep.reserve_s(1)
        rstd_done = s_ep.v

        def tobf(v, rstd_done=rstd_done):
            s_ep.wait(v, rstd_done)
            v.tensor_copy(mbfr[:], mrow[:])
            s_ep.tick(v.tensor_copy(rbfr[:], trow[:]))
        block.vector(tobf)
        bf_done = s_ep.v

        def bcast(t, bf_done=bf_done):
            s_ep.wait(t, bf_done)
            t.matmul(psA[0:128, 0:512], ones1[:], mbfr[:], start=True, stop=True)
            s_mm.tick(t.matmul(psB[0:128, 0:512], ones1[:], rbfr[:], start=True, stop=True))
        block.tensor(bcast)
        bc_done = s_mm.v

        def xform(v, t0=t0, bc_done=bc_done):
            v.wait_ge(s_mm.h, bc_done)
            v.tensor_copy(mbc[:], psA[0:128, 0:512])
            v.tensor_copy(rbc[:], psB[0:128, 0:512])
            for k in range(DC):
                xk = xbf[:, k * ntok + t0:k * ntok + t0 + 512]
                v.tensor_sub(xk, xk, mbc[:])
                last = v.tensor_mul(xk, xk, rbc[:])
            s_ep.tick(last)
        block.vector(xform)
    sb.flush()



IFF = 8192
ICn = IFF // 128


def build_ffn(ctx=None, pfx="", uid=None):
    uid = pfx if uid is None else uid
    if ctx is None:
        nc = bass.Bass("TRN2", target_bir_lowering=False, debug=False)
        x = nc.dram_tensor("x", [D, T], F32, kind="ExternalInput").ap()
        xo = nc.dram_tensor("xo", [D, T], F32, kind="ExternalOutput").ap()
        xv = x.rearrange("(k p) t -> p k t", p=128)
        xov = xo.rearrange("(k p) t -> p k t", p=128)
        h_hbm = nc.dram_tensor("h_hbm", [IFF, T], BF16, kind="Internal").ap()
    else:
        nc = ctx.nc
        xv = ctx.x_own.rearrange("(k p) t -> p k t", p=128)
        xov = xv
        h_hbm = ctx.h_hbm
    def DT(name, shape, dt):
        if ctx is not None and name in ctx.dram_cache:
            return ctx.dram_cache[name]
        ap = nc.dram_tensor(name, shape, dt, kind="ExternalInput").ap()
        if ctx is not None:
            ctx.dram_cache[name] = ap
        return ap
    gw = DT(pfx + "gw", [D, IFF], I32)
    uw = DT(pfx + "uw", [D, IFF], I32)
    dw = DT(pfx + "dw", [IFF, D], I32)
    gsc = DT(pfx + "gsc", [128, ICn], F32)
    usc = DT(pfx + "usc", [128, ICn], F32)
    dsc = DT(pfx + "dsc", [128, DC], F32)

    gwv = gw.rearrange("(k p) i -> p k i", p=128)
    uwv = uw.rearrange("(k p) i -> p k i", p=128)
    dwv = dw.rearrange("(c p) d -> p c d", p=128)
    hv = h_hbm.rearrange("(c p) t -> p c t", p=128)

    import contextlib
    es = contextlib.ExitStack()
    def SB(name, shape, dt):
        return es.enter_context(nc.sbuf_tensor(uid + name, shape, dt))
    def PS(name, shape, dt):
        return es.enter_context(nc.psum_tensor(uid + name, shape, dt))
    def SEM(name):
        return es.enter_context(nc.semaphore(uid + name))
    with es:
        big = SB("big", [128, ICn * 1024], BF16)
        xbf = big
        arena = big
        wgA = SB("wgA", [128, DC * 128], BF16); wgB = SB("wgB", [128, DC * 128], BF16)
        wuA = SB("wuA", [128, DC * 128], BF16); wuB = SB("wuB", [128, DC * 128], BF16)
        gss = SB("gss", [128, ICn], F32); uss = SB("uss", [128, ICn], F32); dss = SB("dss", [128, DC], F32)
        gsil = SB("gsil", [128, T], BF16); usca = SB("usca", [128, T], BF16); htile = SB("htile", [128, T], BF16)
        wdintA = SB("wdintA", [128, 1024], I32); wdintB = SB("wdintB", [128, 1024], I32)
        hstrA = SB("hstrA", [128, 512], BF16); hstrB = SB("hstrB", [128, 512], BF16)
        xoldA = SB("xoldA", [128, 512], F32); xoldB = SB("xoldB", [128, 512], F32)
        xnewA = SB("xnewA", [128, 512], F32); xnewB = SB("xnewB", [128, 512], F32)
        ones1 = SB("ones1", [1, 128], BF16)
        onesc = SB("onesc", [128, 1], BF16)
        x2 = SB("x2", [128, 512], BF16)
        mrow = SB("mrow", [1, 512], F32); vrow = SB("vrow", [1, 512], F32); trow = SB("trow", [1, 512], F32)
        mbfr = SB("mbfr", [1, 512], BF16); rbfr = SB("rbfr", [1, 512], BF16)
        mbc = SB("mbc", [128, 512], BF16); rbc = SB("rbc", [128, 512], BF16)
        if ctx is None:
            psA = PS("psA", [128, 2048], F32); psB = PS("psB", [128, 2048], F32)
            _s_ld = SEM("s_ld"); _s_w = SEM("s_w"); _s_mm = SEM("s_mm")
            _s_epv = SEM("s_epv"); _s_eps = SEM("s_eps"); _s_st = SEM("s_st"); _s_h = SEM("s_h")
            _s_c = SEM("s_c"); _s_c1 = SEM("s_c1"); _s_hs = SEM("s_hs"); _s_xo = SEM("s_xo")
            _s_w1 = SEM("s_w1"); _s_h1 = SEM("s_h1"); _s_xo1 = SEM("s_xo1"); _s_st1 = SEM("s_st1")
            block = es.enter_context(nc.Block())
        if ctx is None:
            s_ld = Sem(_s_ld); s_w = Sem(_s_w); s_mm = Sem(_s_mm)
            s_ep = EpSem(_s_epv, _s_eps); s_st = Sem(_s_st); s_h = Sem(_s_h)
            s_hs = Sem(_s_hs)
            s_wp = [s_w, Sem(_s_w1)]
            s_hp = [s_h, Sem(_s_h1)]
            s_cp = [Sem(_s_c), Sem(_s_c1)]
            s_xop = [Sem(_s_xo), Sem(_s_xo1)]
            s_stp = [s_st, Sem(_s_st1)]
            s_g = Sem(es.enter_context(nc.semaphore("s_g")))
        else:
            block = ctx.block
            psA = ctx.psA; psB = ctx.psB
            s_ld = ctx.s_ld; s_mm = ctx.s_mm; s_ep = ctx.s_ep
            s_wp = ctx.s_wp; s_hp = ctx.s_hp; s_cp = ctx.s_cp
            s_xop = ctx.s_xop; s_stp = ctx.s_stp; s_hs = ctx.s_hs; s_g = ctx.s_g
        sems = dict(mm=s_mm, ep=s_ep)
        sc = dict(x2=x2, mrow=mrow, vrow=vrow, trow=trow, mbfr=mbfr, rbfr=rbfr, mbc=mbc, rbc=rbc)

        def ld0(g):
            for q in range(4):
                s_ld.dma(g.dma_start(out=xbf[:, 0:DC * T].rearrange("p (k t) -> p k t", k=DC)[:, 4 * q:4 * q + 4, :],
                                     in_=xv[:, 4 * q:4 * q + 4, :]))
            s_ld.dma(g.dma_start(out=gss[:], in_=gsc[:]))
            s_ld.dma(g.dma_start(out=uss[:], in_=usc[:]))
            s_ld.dma(g.dma_start(out=dss[:], in_=dsc[:]))
            g.memset(ones1[:], 1.0)
            s_g.tick(g.memset(onesc[:], 1.0))
        block.gpsimd(ld0)
        ld_done = s_ld.v
        g_done = s_g.v

        for sec in (block.tensor, block.vector, block.scalar):
            sec(lambda e, ld_done=ld_done, g_done=g_done: (
                e.wait_ge(s_ld.h, ld_done), e.wait_ge(s_g.h, g_done)))

        emit_layernorm(block, sems, xbf, psA, psB, ones1, onesc, sc)

        sb_gu = ScalBatch(block)
        wg = [wgA, wgB]
        wu = [wuA, wuB]
        mm_done = {}
        psum_free = s_ep.v
        gu_free = 0
        h_store = {}

        def wload(g, ic, need_mm):
            g.wait_ge(s_mm.h, need_mm)
            sw = s_wp[ic % 2]
            g.memset(wg[ic % 2][:], -128.0)
            g.memset(wu[ic % 2][:], -128.0)
            sw.dma(g.dma_start(out=wg[ic % 2][:].rearrange("p (k i) -> p k i", k=DC),
                               in_=gwv[:, :, ic * 128:(ic + 1) * 128], accum_op=ALU.add))
            sw.dma(g.dma_start(out=wu[ic % 2][:].rearrange("p (k i) -> p k i", k=DC),
                               in_=uwv[:, :, ic * 128:(ic + 1) * 128], accum_op=ALU.add))
            return sw.v

        w_arrv = {}
        block.gpsimd(lambda g: w_arrv.__setitem__(0, wload(g, 0, 0)))
        block.gpsimd(lambda g: w_arrv.__setitem__(1, wload(g, 1, 0)))

        for ic in range(ICn):
            buf = ic % 2

            def mmg(t, ic=ic, buf=buf, wt=w_arrv[ic], psum_free=psum_free):
                t.wait_ge(s_wp[buf].h, wt)
                s_ep.wait(t, psum_free)
                for k in range(DC):
                    for q in range(4):
                        mmi = t.matmul(psA[:, q * 512:(q + 1) * 512], wg[buf][:, k * 128:(k + 1) * 128],
                                       xbf[:, k * T + q * 512:k * T + (q + 1) * 512],
                                       start=(k == 0), stop=(k == DC - 1))
                s_mm.tick(mmi)
                for k in range(DC):
                    for q in range(4):
                        mmi = t.matmul(psB[:, q * 512:(q + 1) * 512], wu[buf][:, k * 128:(k + 1) * 128],
                                       xbf[:, k * T + q * 512:k * T + (q + 1) * 512],
                                       start=(k == 0), stop=(k == DC - 1))
                s_mm.tick(mmi)
            block.tensor(mmg)
            g_done, u_done = s_mm.v - 1, s_mm.v
            mm_done[ic] = s_mm.v

            if ic + 2 < ICn:
                block.gpsimd(lambda g, ic=ic: w_arrv.__setitem__(
                    ic + 2, wload(g, ic + 2, mm_done[ic])))

            def epi_s(s, ic=ic, g_done=g_done, gu_free=gu_free):
                s.wait_ge(s_mm.h, g_done)
                s_ep.wait(s, gu_free)
                s.activation(out=gsil[:], in_=psA[:, 0:2048], func=AF.Silu,
                             scale=gss[:, ic:ic + 1]).then_inc(s_ep.hs, 1)
            sb_gu.add(epi_s)
            s_ep.reserve_s(1)
            sil_done = s_ep.v

            def epi_v(v, ic=ic, sil_done=sil_done, u_done=u_done, h_st=h_store.get(ic - 1, 0)):
                s_ep.wait(v, sil_done)
                v.wait_ge(s_mm.h, u_done)
                v.wait_ge(s_hs.h, h_st)
                s_ep.tick(v.scalar_tensor_tensor(htile[:], psB[:, 0:2048], uss[:, ic:ic + 1],
                                                 gsil[:], ALU.mult, ALU.mult))
            block.vector(epi_v)
            psum_free = s_ep.v
            gu_free = s_ep.v
            h_done = s_ep.v

            def hstore(sy, ic=ic, h_done=h_done):
                s_ep.wait(sy, h_done)
                s_hs.dma(sy.dma_start(out=hv[:, ic, :], in_=htile[:]))
            block.sync(hstore)
            h_store[ic] = s_hs.v

        sb_gu.flush()
        gateup_st = s_hs.v
        down_psum_free = psum_free

        wdint = [wdintA, wdintB]
        hstr = [hstrA, hstrB]
        xold = [xoldA, xoldB]
        xnew = [xnewA, xnewB]
        conv_tick = {}
        dmm_tick = {}
        epi_tick = {}
        xst_tick = {}
        arena_free = s_mm.v

        for dmh in range(2):
            for icc in range(ICn):
                bufw = icc % 2

                def wdload(g, icc=icc, dmh=dmh, bufw=bufw, arena_free=arena_free):
                    g.wait_ge(s_mm.h, arena_free)
                    g.memset(arena[:, icc * 1024:(icc + 1) * 1024], -128.0)
                    s_cp[bufw].dma(g.dma_start(out=arena[:, icc * 1024:(icc + 1) * 1024],
                                               in_=dwv[:, icc, dmh * 1024:(dmh + 1) * 1024],
                                               accum_op=ALU.add))
                block.gpsimd(wdload)
            wd_done = (s_cp[0].v, s_cp[1].v)

            for tp in range(4):
                for icc in range(ICn):
                    bufh = icc % 2
                    h_free = dmm_tick.get((dmh, tp, icc - 2),
                                          dmm_tick.get((dmh, tp - 1, ICn - 2 + icc),
                                                       dmm_tick.get((dmh - 1, 3, ICn - 2 + icc), 0)))

                    def hload(g, icc=icc, tp=tp, bufh=bufh, h_free=h_free, need=gateup_st):
                        g.wait_ge(s_hs.h, need)
                        g.wait_ge(s_mm.h, h_free)
                        s_hp[bufh].dma(g.dma_start(out=hstr[bufh][:], in_=hv[:, icc, tp * 512:(tp + 1) * 512]))
                    block.gpsimd(hload)
                    h_arr = s_hp[bufh].v

                    dpsum_free = epi_tick.get((dmh, tp - 1), epi_tick.get((dmh - 1, 3), down_psum_free))

                    def dmm(t, icc=icc, bufh=bufh, h_arr=h_arr, wd_done=wd_done, dpsum_free=dpsum_free):
                        t.wait_ge(s_cp[0].h, wd_done[0])
                        t.wait_ge(s_cp[1].h, wd_done[1])
                        t.wait_ge(s_hp[bufh].h, h_arr)
                        if icc == 0:
                            s_ep.wait(t, dpsum_free)
                        for j in range(8):
                            ps = psA if j < 4 else psB
                            jj = j % 4
                            mmi = t.matmul(ps[:, jj * 512:(jj + 1) * 512],
                                           arena[:, icc * 1024 + j * 128:icc * 1024 + (j + 1) * 128],
                                           hstr[bufh][:], start=(icc == 0), stop=(icc == ICn - 1))
                        s_mm.tick(mmi)
                    block.tensor(dmm)
                    dmm_tick[(dmh, tp, icc)] = s_mm.v
                dmm_all = s_mm.v

                for j in range(8):
                    dmc = dmh * 8 + j
                    bufe = j % 2
                    xold_free = epi_tick.get((dmh, tp, j - 2), epi_tick.get((dmh, tp - 1, 6 + j),
                                             epi_tick.get((dmh - 1, 3, 6 + j), 0)))

                    def xload(sy, dmc=dmc, tp=tp, bufe=bufe, xold_free=xold_free):
                        s_ep.wait(sy, xold_free)
                        s_xop[bufe].dma(sy.dma_start(out=xold[bufe][:], in_=xv[:, dmc, tp * 512:(tp + 1) * 512]))
                    block.sync(xload)
                    x_arr = s_xop[bufe].v

                    xnew_free = xst_tick.get((dmh, tp, j - 2), xst_tick.get((dmh, tp - 1, 6 + j),
                                             xst_tick.get((dmh - 1, 3, 6 + j), 0)))

                    def epi(v, j=j, dmc=dmc, bufe=bufe, x_arr=x_arr, dmm_all=dmm_all, xnew_free=xnew_free):
                        v.wait_ge(s_mm.h, dmm_all)
                        v.wait_ge(s_xop[bufe].h, x_arr)
                        v.wait_ge(s_stp[bufe].h, xnew_free)
                        ps = psA if j < 4 else psB
                        jj = j % 4
                        s_ep.tick(v.scalar_tensor_tensor(
                            xnew[bufe][:], ps[:, jj * 512:(jj + 1) * 512], dss[:, dmc:dmc + 1],
                            xold[bufe][:], ALU.mult, ALU.add))
                    block.vector(epi)
                    epi_tick[(dmh, tp, j)] = s_ep.v
                    e_done = s_ep.v

                    def xstore(sy, dmc=dmc, tp=tp, bufe=bufe, e_done=e_done):
                        s_ep.wait(sy, e_done)
                        s_stp[bufe].dma(sy.dma_start(out=xov[:, dmc, tp * 512:(tp + 1) * 512], in_=xnew[bufe][:]))
                    block.sync(xstore)
                    xst_tick[(dmh, tp, j)] = s_stp[bufe].v
                epi_tick[(dmh, tp)] = s_ep.v
            arena_free = s_mm.v

        if ctx is None:
            f0, f1 = s_stp[0].v, s_stp[1].v

            def fin(g):
                g.wait_ge(s_stp[0].h, f0)
                g.wait_ge(s_stp[1].h, f1)
            block.gpsimd(fin)

    return nc



_PROGS = {}


def _prog(name, builder):
    if name not in _PROGS:
        _PROGS[name] = builder()
    return _PROGS[name]


def _ffn_weight_maps(pfx, inputs):
    gw = np.ascontiguousarray(np.asarray(inputs[pfx + "_gate_w"]).T)
    uw = np.ascontiguousarray(np.asarray(inputs[pfx + "_up_w"]).T)
    dw = np.ascontiguousarray(np.asarray(inputs[pfx + "_down_w"]).T)
    gsc = np.ascontiguousarray(np.asarray(inputs[pfx + "_gate_scale"]).reshape(ICn, 128).T)
    usc = np.ascontiguousarray(np.asarray(inputs[pfx + "_up_scale"]).reshape(ICn, 128).T)
    dsc = np.ascontiguousarray(np.asarray(inputs[pfx + "_down_scale"]).reshape(DC, 128).T)
    return dict(gw=gw, uw=uw, dw=dw, gsc=gsc, usc=usc, dsc=dsc)


def run_ffn(x_full, wmap):
    nc = _prog("ffn", build_ffn)
    flat = x_full.reshape(B * S, D)
    in_maps = []
    for c in range(NCORES):
        xc = np.ascontiguousarray(flat[c * T:(c + 1) * T].T)
        in_maps.append(dict(x=xc, **wmap))
    res = run_bass_kernel_spmd(nc, in_maps, core_ids=list(range(NCORES)))
    out = np.empty_like(flat)
    for c in range(NCORES):
        out[c * T:(c + 1) * T] = res.results[c]["xo"].T
    return out.reshape(B, S, D)


def kernel_unfused(**inputs):
    x = np.asarray(inputs["L0_attn_a_x"], dtype=np.float32)
    w_ffn0 = _ffn_weight_maps("L0_ffn", inputs)
    w_ffn1 = _ffn_weight_maps("L1_ffn", inputs)
    m_a0 = _attn_maps("a0", inputs)
    m_b0 = _attn_maps("b0", inputs)
    m_b1 = _attn_maps("b1", inputs)
    m_mb = _mamba_maps(inputs)
    x = run_attn("a0", x, m_a0)
    x = run_ffn(x, w_ffn0)
    x = run_attn("b0", x, m_b0)
    x = run_ffn(x, w_ffn0)
    x = run_mamba(x, m_mb)
    x = run_ffn(x, w_ffn1)
    x = run_attn("b1", x, m_b1)
    x = run_ffn(x, w_ffn1)
    return x



SC128 = float(1.0 / np.sqrt(128.0))

ATTN_CFG = {
    "a0": dict(EK=2048, EV=2048, NQH=16, NKV=16, HPG=1, power=False, bias=True, fused_qkv=True),
    "b0": dict(EK=512, EV=512, NQH=16, NKV=4, HPG=4, power=False, bias=False, fused_qkv=False),
    "b1": dict(EK=1024, EV=1024, NQH=16, NKV=8, HPG=2, power=True, bias=True, fused_qkv=False),
}


def build_attn(kind, ctx=None, pfx=""):
    cfg = ATTN_CFG[kind]
    EK, EV, NQH, NKV, HPG = cfg["EK"], cfg["EV"], cfg["NQH"], cfg["NKV"], cfg["HPG"]
    power, has_bias = cfg["power"], cfg["bias"]
    EKC = EK // 128
    T2 = 2 * T

    if ctx is None:
        nc = bass.Bass("TRN2", target_bir_lowering=False, debug=False)
        x = nc.dram_tensor("x", [D, T2], F32, kind="ExternalInput").ap()
        gates = nc.dram_tensor("gates", [128, 4], F32, kind="ExternalInput").ap()
        masks = nc.dram_tensor("masks", [128, 4 * 512], BF16, kind="ExternalInput").ap()
        ident = nc.dram_tensor("ident", [128, 128], BF16, kind="ExternalInput").ap()
        xo = nc.dram_tensor("xo", [D, T], F32, kind="ExternalOutput").ap()
        xv = x.rearrange("(k p) t -> p k t", p=128)
        xov = xo.rearrange("(k p) t -> p k t", p=128)

        def xsrc(sl, xcol0):
            return xv[:, sl, xcol0:xcol0 + T]

        def xown_cols(sl, c0, c1):
            return xv[:, sl, T + c0:T + c1]
    else:
        nc = ctx.nc
        gates = ctx.gates; masks = ctx.masks; ident = ctx.ident
        xv_own = ctx.x_own.rearrange("(k p) t -> p k t", p=128)
        xv_past = ctx.x_past.rearrange("(k p) t -> p k t", p=128)
        xov = xv_own

        def xsrc(sl, xcol0):
            return (xv_past if xcol0 == 0 else xv_own)[:, sl, :]

        def xown_cols(sl, c0, c1):
            return xv_own[:, sl, c0:c1]
    if kind == "a0":
        wqkv = nc.dram_tensor(pfx + "wq", [D, 6144], F32, kind="ExternalInput").ap()
        bqr = nc.dram_tensor(pfx + "bqr", [1, 6144], F32, kind="ExternalInput").ap()
        bqc = nc.dram_tensor(pfx + "bqc", [128, 48], F32, kind="ExternalInput").ap()
    elif kind == "b0":
        wqm = nc.dram_tensor(pfx + "wq", [D, 2048], F32, kind="ExternalInput").ap()
        wkv = nc.dram_tensor(pfx + "wkv", [D, 1024], F32, kind="ExternalInput").ap()
    else:
        wqm = nc.dram_tensor(pfx + "wq", [D, 2048], F32, kind="ExternalInput").ap()
        wk = nc.dram_tensor(pfx + "wk", [D, 1024], F32, kind="ExternalInput").ap()
        wvm = nc.dram_tensor(pfx + "wv", [D, 1024], F32, kind="ExternalInput").ap()
        wgg = nc.dram_tensor(pfx + "wgg", [D, 8], F32, kind="ExternalInput").ap()
    wo = nc.dram_tensor(pfx + "wo", [D, D], F32, kind="ExternalInput").ap()
    if has_bias:
        boc = nc.dram_tensor(pfx + "boc", [128, DC], F32, kind="ExternalInput").ap()

    q_hbm = nc.dram_tensor(pfx + "q_hbm", [D, T], BF16, kind="Internal").ap()
    k_hbm = nc.dram_tensor(pfx + "k_hbm", [EK, T2], BF16, kind="Internal").ap()
    v_hbm = nc.dram_tensor(pfx + "v_hbm", [T2, EV], BF16, kind="Internal").ap()
    qv = q_hbm.rearrange("(h p) t -> p h t", p=128)
    kv_ = k_hbm.rearrange("(h p) t -> p h t", p=128)
    vv = v_hbm.rearrange("(c p) e -> p c e", p=128)
    wov = wo.rearrange("(e p) d -> p e d", p=128)

    import contextlib
    es = contextlib.ExitStack()
    SB = lambda n, s, d: es.enter_context(nc.sbuf_tensor(pfx + n, s, d))
    PS = lambda n, s, d: es.enter_context(nc.psum_tensor(pfx + n, s, d))
    SEM = lambda n: es.enter_context(nc.semaphore(pfx + n))
    with es:
        big = SB("big", [128, 32768], BF16)
        wtA = SB("wtA", [128, DC * 128], BF16); wtB = SB("wtB", [128, DC * 128], BF16)
        vwA = SB("vwA", [128, DC * 512], BF16)
        ptA = SB("ptA", [128, T], BF16); ptB = SB("ptB", [128, T], BF16)
        vtA = SB("vtA", [128, 512], BF16); vtB = SB("vtB", [128, 512], BF16)
        khA = SB("khA", [128, T2], BF16); khB = SB("khB", [128, T2], BF16)
        vhA = SB("vhA", [128, 32 * 128], BF16); vhB = SB("vhB", [128, 32 * 128], BF16)
        qtA = SB("qtA", [128, 512], BF16); qtB = SB("qtB", [128, 512], BF16)
        pT0 = SB("pT0", [128, 512], BF16); pT1 = SB("pT1", [128, 512], BF16)
        pT2 = SB("pT2", [128, 512], BF16); pT3 = SB("pT3", [128, 512], BF16)
        msk = SB("msk", [128, 4 * 512], BF16)
        idn = SB("idn", [128, 128], BF16)
        gat = SB("gat", [128, 4], F32)
        woT = SB("woT", [128, DC * 128], BF16); woU = SB("woU", [128, DC * 128], BF16)
        xoldA = SB("xoldA", [128, T], F32)
        xnewA = SB("xnewA", [128, T], F32)
        ones1 = SB("ones1", [1, 128], BF16)
        onesc = SB("onesc", [128, 1], BF16)
        lrow = SB("lrow", [1, 512], F32)
        linv = SB("linv", [1, 512], F32)
        linb = SB("linb", [1, 512], BF16)
        lbc = SB("lbc", [128, 512], F32)
        x2 = SB("x2", [128, 512], BF16)
        mrow = SB("mrow", [1, 512], F32); vrow = SB("vrow", [1, 512], F32); trow = SB("trow", [1, 512], F32)
        mbfr = SB("mbfr", [1, 512], BF16); rbfr = SB("rbfr", [1, 512], BF16)
        mbc = SB("mbc", [128, 512], BF16); rbc = SB("rbc", [128, 512], BF16)
        if has_bias and kind == "a0":
            bqrs = SB("bqrs", [1, 6144], BF16)
            bqcs = SB("bqcs", [128, 48], F32)
        if has_bias:
            bocs = SB("bocs", [128, DC], F32)
        if power:
            g_sb = SB("g_sb", [8, T2], F32)
            ga = big[0:8, 0:2 * T2].bitcast(F32)
            gb2 = big[0:8, 2 * T2:4 * T2].bitcast(F32)
            gones = big[0:8, 4 * T2:6 * T2].bitcast(F32)
            a_bf = SB("a_bf", [8, T2], BF16)
            aT = SB("aT", [128, 8 * 32], F32)
        if ctx is None:
            psA = PS("psA", [128, 2048], F32)
            psB = PS("psB", [128, 2048], F32)
            s_ld = Sem(SEM("s_ld")); s_w = Sem(SEM("s_w")); s_mm = Sem(SEM("s_mm"))
            s_ep = EpSem(SEM("s_epv"), SEM("s_eps")); s_st = Sem(SEM("s_st")); s_h = Sem(SEM("s_h"))
            s_c = Sem(SEM("s_c"))
            s_g = Sem(SEM("s_g"))
            s_fs = Sem(SEM("s_fs"))
            s_vs = Sem(SEM("s_vs"))
            s_xo = Sem(SEM("s_xo"))
            s_wp = [s_w, Sem(SEM("s_w1"))]
            s_vw = Sem(SEM("s_vw"))
            s_wo = [Sem(SEM("s_wo0")), Sem(SEM("s_wo1"))]
            s_kv = [Sem(SEM("s_kv0")), Sem(SEM("s_kv1"))]
            s_q = [Sem(SEM("s_q0")), Sem(SEM("s_q1"))]
            block = es.enter_context(nc.Block())
        else:
            psA = ctx.psA; psB = ctx.psB; block = ctx.block
            s_ld = ctx.s_ld; s_w = ctx.s_w; s_mm = ctx.s_mm; s_ep = ctx.s_ep
            s_st = ctx.s_st; s_h = ctx.s_h; s_c = ctx.s_c; s_g = ctx.s_g
            s_fs = ctx.s_fs; s_vs = ctx.s_vs; s_xo = ctx.s_xo
            s_wp = ctx.s_wp; s_vw = ctx.s_vw; s_wo = ctx.s_wo
            s_kv = ctx.s_kv; s_q = ctx.s_q
        sems = dict(mm=s_mm, ep=s_ep)
        sc = dict(x2=x2, mrow=mrow, vrow=vrow, trow=trow, mbfr=mbfr, rbfr=rbfr, mbc=mbc, rbc=rbc)
        xbf = big

        def ld0(sy):
            s_ld.dma(sy.dma_start(out=msk[:], in_=masks[:]))
            s_ld.dma(sy.dma_start(out=idn[:], in_=ident[:]))
            s_ld.dma(sy.dma_start(out=gat[:], in_=gates[:]))
            if has_bias:
                s_ld.dma(sy.dma_start(out=bocs[:], in_=boc[:]))
            if has_bias and kind == "a0":
                s_ld.dma(sy.dma_start(out=bqcs[:], in_=bqc[:]))
        block.sync(ld0)

        def ld0g(g):
            if has_bias and kind == "a0":
                s_ld.dma(g.dma_start(out=bqrs[:], in_=bqr[:]))
            g.memset(ones1[:], 1.0)
            s_ld.tick(g.memset(onesc[:], 1.0))
        block.gpsimd(ld0g)
        consts_done = s_ld.v

        for sec in (block.tensor, block.vector, block.scalar):
            sec(lambda e: e.wait_ge(s_ld.h, consts_done))

        if kind == "a0":
            wqv = wqkv.rearrange("(k p) e -> p k e", p=128)
        else:
            wqv = wqm.rearrange("(k p) e -> p k e", p=128)
            if kind == "b0":
                wkvv = wkv.rearrange("(k p) e -> p k e", p=128)
            else:
                wkv_b1 = wk.rearrange("(k p) e -> p k e", p=128)
                wvv_b1 = wvm.rearrange("(k p) e -> p k e", p=128)
                wgv = wgg.rearrange("(k p) e -> p k e", p=128)

        wt = [wtA, wtB]
        vw = [vwA, vwA]
        pt = [ptA, ptB]
        vt = [vtA, vtB]
        fm_mm = {}
        fm_ep = {}
        fm_st = {}
        vt_mm = {}
        vt_ep = {}
        vt_st = {}
        jglob = [0]
        vglob = [0]
        half_mm_done = 0

        def fm_wload(g, wsrc, ec, j):
            g.wait_ge(s_mm.h, fm_mm.get(j - 2, 0))
            s_wp[j % 2].dma(g.dma_start(out=wt[j % 2][:].rearrange("p (k i) -> p k i", k=DC),
                                        in_=wsrc[:, :, ec * 128:(ec + 1) * 128]))

        def emit_fm_job(wsrc, ec, dst_view, dst_chunk, dst_col0, bias_ec, xln_ready):
            j = jglob[0]; jglob[0] += 1
            pp = j % 2
            ps = psA if pp == 0 else psB
            block.gpsimd(lambda g: fm_wload(g, wsrc, ec, j))
            w_arr = s_wp[j % 2].v

            def mmj(t, j=j, pp=pp, ps=ps, w_arr=w_arr, xln_ready=xln_ready):
                t.wait_ge(s_wp[j % 2].h, w_arr)
                s_ep.wait(t, vmax(fm_ep.get(j - 2, 0), xln_ready))
                for k in range(DC):
                    for q in range(4):
                        mmi = t.matmul(ps[:, q * 512:(q + 1) * 512], wt[pp][:, k * 128:(k + 1) * 128],
                                       xbf[:, k * T + q * 512:k * T + (q + 1) * 512],
                                       start=(k == 0), stop=(k == DC - 1))
                s_mm.tick(mmi)
            block.tensor(mmj)
            fm_mm[j] = s_mm.v
            mmv = s_mm.v

            def epij(s, j=j, pp=pp, ps=ps, bias_ec=bias_ec, mmv=mmv, fst=fm_st.get(j - 2, 0)):
                s.wait_ge(s_mm.h, mmv)
                s.wait_ge(s_fs.h, fst)
                if bias_ec is not None:
                    s.activation(out=pt[pp][:], in_=ps[:, 0:2048], func=AF.Identity,
                                 bias=bqcs[:, bias_ec:bias_ec + 1]).then_inc(s_ep.hs, 1)
                else:
                    s.activation(out=pt[pp][:], in_=ps[:, 0:2048],
                                 func=AF.Copy).then_inc(s_ep.hs, 1)
            sb_fm.add(epij)
            s_ep.reserve_s(1)
            fm_ep[j] = s_ep.v
            epv = s_ep.v

            def stj(sy, j=j, pp=pp, dst_view=dst_view, dst_chunk=dst_chunk, dst_col0=dst_col0, epv=epv):
                s_ep.wait(sy, epv)
                s_fs.dma(sy.dma_start(out=dst_view[:, dst_chunk, dst_col0:dst_col0 + T], in_=pt[pp][:]))
            block.sync(stj)
            fm_st[j] = s_fs.v

        def emit_tm_vjob(wsrc, src_col0, ncols, dst_row_chunk0, dst_col0, bias_row_col0, xln_ready):
            jv = vglob[0]; vglob[0] += 1
            pp = jv % 2
            ep_now = s_ep.v
            block.gpsimd(lambda g, jv=jv, pp=pp: (
                g.wait_ge(s_mm.h, vt_mm.get(jv - 1, 0)),
                s_vw.dma(g.dma_start(out=vw[pp][:, 0:DC * ncols].rearrange("p (k i) -> p k i", k=DC),
                                     in_=wsrc[:, :, src_col0:src_col0 + ncols]))))
            w_arr = s_vw.v
            for tc in range(16):
                def mmv(t, jv=jv, pp=pp, tc=tc, w_arr=w_arr, ncols=ncols,
                        bias_row_col0=bias_row_col0, xln_ready=xln_ready, ep_now=ep_now):
                    ps = psA if tc % 2 == 0 else psB
                    t.wait_ge(s_vw.h, w_arr)
                    s_ep.wait(t, vmax(vt_ep.get((jv, tc - 2), vt_ep.get((jv - 1, 14 + tc), 0)),
                                          xln_ready, ep_now if tc < 2 else 0))
                    has_b = bias_row_col0 is not None
                    for k in range(DC):
                        mmi = t.matmul(ps[:, 0:ncols], xbf[:, k * T + tc * 128:k * T + (tc + 1) * 128],
                                       vw[pp][:, k * ncols:(k + 1) * ncols],
                                       start=(k == 0), stop=(k == DC - 1 and not has_b))
                    if has_b:
                        mmi = t.matmul(ps[:, 0:ncols], ones1[:],
                                       bqrs[0:1, bias_row_col0:bias_row_col0 + ncols],
                                       start=False, stop=True)
                    s_mm.tick(mmi)
                block.tensor(mmv)
                vt_mm[jv] = s_mm.v
                mmvv = s_mm.v

                def epiv(v, jv=jv, tc=tc, ncols=ncols, mmvv=mmvv):
                    ps = psA if tc % 2 == 0 else psB
                    pe = (jv * 16 + tc) % 2
                    v.wait_ge(s_mm.h, mmvv)
                    v.wait_ge(s_vs.h, vt_st.get((jv, tc - 2), vt_st.get((jv - 1, 14 + tc), 0)))
                    s_ep.tick(v.tensor_copy(vt[pe][:, 0:ncols], ps[:, 0:ncols]))
                block.vector(epiv)
                vt_ep[(jv, tc)] = s_ep.v
                epvv = s_ep.v

                def stv(sy, jv=jv, tc=tc, ncols=ncols, dst_row_chunk0=dst_row_chunk0,
                        dst_col0=dst_col0, epvv=epvv):
                    pe = (jv * 16 + tc) % 2
                    s_ep.wait(sy, epvv)
                    s_vs.dma(sy.dma_start(out=vv[:, dst_row_chunk0 + tc, dst_col0:dst_col0 + ncols],
                                          in_=vt[pe][:, 0:ncols]))
                block.sync(stv)
                vt_st[(jv, tc)] = s_vs.v

        sb_fm = ScalBatch(block)
        power_g_ep = {}
        for half in range(2):
            xcol0 = half * T
            xgate = s_mm.v

            def xload(g, xcol0=xcol0, xgate=xgate):
                g.wait_ge(s_mm.h, xgate)
                for qq in range(4):
                    s_ld.dma(g.dma_start(
                        out=xbf[:, 0:DC * T].rearrange("p (k t) -> p k t", k=DC)[:, 4 * qq:4 * qq + 4, :],
                        in_=xsrc(slice(4 * qq, 4 * qq + 4), xcol0)))
            block.gpsimd(xload)
            xld = s_ld.v
            for sec in (block.tensor, block.vector, block.scalar):
                sec(lambda e, xld=xld: e.wait_ge(s_ld.h, xld))
            emit_layernorm(block, sems, xbf, psA, psB, ones1, onesc, sc)
            xln_ready = s_ep.v

            if kind == "a0":
                if half == 1:
                    for ec in range(16):
                        emit_fm_job(wqv, ec, qv, ec, 0, ec, xln_ready)
                for ec in range(16):
                    emit_fm_job(wqv, 16 + ec, kv_, ec, xcol0, 16 + ec, xln_ready)
                for gvi in range(4):
                    emit_tm_vjob(wqv, 4096 + gvi * 512, 512, 16 * half, gvi * 512, 4096 + gvi * 512, xln_ready)
            elif kind == "b0":
                if half == 1:
                    for ec in range(16):
                        emit_fm_job(wqv, ec, qv, ec, 0, None, xln_ready)
                for ec in range(4):
                    emit_fm_job(wkvv, ec, kv_, ec, xcol0, None, xln_ready)
                emit_tm_vjob(wkvv, 512, 512, 16 * half, 0, None, xln_ready)
            else:
                if half == 1:
                    for ec in range(16):
                        emit_fm_job(wqv, ec, qv, ec, 0, None, xln_ready)
                for ec in range(8):
                    emit_fm_job(wkv_b1, ec, kv_, ec, xcol0, None, xln_ready)
                for gvi in range(2):
                    emit_tm_vjob(wvv_b1, gvi * 512, 512, 16 * half, gvi * 512, None, xln_ready)
                block.gpsimd(lambda g: (
                    g.wait_ge(s_mm.h, fm_mm.get(jglob[0] - 1, 0)),
                    s_vw.dma(g.dma_start(out=wt[0][:, 0:DC * 8].rearrange("p (k i) -> p k i", k=DC),
                                         in_=wgv[:, :, 0:8]))))
                gw_arr = s_vw.v

                for tgc in range(4):
                    def gmm(t, tgc=tgc, gw_arr=gw_arr, xln_ready=xln_ready):
                        t.wait_ge(s_vw.h, gw_arr)
                        s_ep.wait(t, vmax(power_g_ep.get((half, tgc - 1), power_g_ep.get((half - 1, 3), 0)), xln_ready))
                        for k in range(DC):
                            mmi = t.matmul(psB[0:8, 1024:1536], wt[0][:, k * 8:(k + 1) * 8],
                                           xbf[:, k * T + tgc * 512:k * T + (tgc + 1) * 512],
                                           start=(k == 0), stop=(k == DC - 1))
                        s_mm.tick(mmi)
                    block.tensor(gmm)
                    gm = s_mm.v

                    def gev(v, half=half, tgc=tgc, gm=gm):
                        v.wait_ge(s_mm.h, gm)
                        s_ep.tick(v.tensor_copy(g_sb[0:8, half * T + tgc * 512:half * T + (tgc + 1) * 512],
                                                psB[0:8, 1024:1536]))
                    block.vector(gev)
                    power_g_ep[(half, tgc)] = s_ep.v
            half_mm_done = s_mm.v
            sb_fm.flush()

        proj_done_mm = s_mm.v
        proj_fs = s_fs.v
        proj_vs = s_vs.v
        proj_done_ep = s_ep.v

        p2_ep = 0
        if power:
            GO = 6.906768

            def p2v(v):
                s_ep.wait(v, proj_done_ep)
                v.wait_ge(s_mm.h, proj_done_mm)
                v.memset(gones[:, :], 1.0)
                v.tensor_scalar_mul(g_sb[0:8, 0:T], g_sb[0:8, 0:T], gat[0:8, 0:1])
                s_ep.tick(v.tensor_scalar(ga[:, :], g_sb[0:8, :], -1.0, -GO, ALU.mult, ALU.add))
            block.vector(p2v)
            t_done = s_ep.v

            def p2s(s, t_done=t_done):
                s_ep.wait(s, t_done)
                s.activation(out=gb2[:, :], in_=ga[:, :], func=AF.Abs)
                s.activation(out=gb2[:, :], in_=gb2[:, :], func=AF.Exp, scale=-1.0)
                s_ep.tick(s.activation(out=gb2[:, :], in_=gb2[:, :], func=AF.Ln, bias=1.0))
            block.scalar(p2s)
            sp_done = s_ep.v

            def p2v2(v, sp_done=sp_done):
                s_ep.wait(v, sp_done)
                v.tensor_relu(ga[:, :], ga[:, :])
                v.tensor_add(ga[:, :], ga[:, :], gb2[:, :])
                v.tensor_scalar_mul(ga[:, :], ga[:, :], -1.0)
                s_ep.tick(v.tensor_tensor_scan(g_sb[0:8, :], gones[:, :], ga[:, :], 0.0,
                                               ALU.mult, ALU.add))
            block.vector(p2v2)
            cs_done = s_ep.v

            def p2s2(s, cs_done=cs_done):
                s_ep.wait(s, cs_done)
                s_ep.tick(s.activation(out=ga[:, :], in_=g_sb[0:8, :], func=AF.Exp, scale=-1.0,
                                       bias=g_sb[0:8, T2 - 1:T2]))
            block.scalar(p2s2)
            a_done = s_ep.v

            def p2v3(v, a_done=a_done):
                s_ep.wait(v, a_done)
                v.tensor_scalar_mul(ga[:, 0:T], ga[:, 0:T], gat[0:8, 0:1])
                s_ep.tick(v.tensor_copy(a_bf[0:8, :], ga[:, :]))
            block.vector(p2v3)
            abf_done = s_ep.v

            pstr = [psA[:, 1536:1540].bitcast(BF16), psB[:, 1536:1540].bitcast(BF16)]
            trc = {}
            for kc in range(32):
                def ptr2(t, kc=kc, abf_done=abf_done):
                    s_ep.wait(t, abf_done if kc < 2 else trc[kc - 2])
                    s_mm.tick(t.transpose(pstr[kc % 2], a_bf[0:8, kc * 128:(kc + 1) * 128], idn[0:8, 0:8]))
                block.tensor(ptr2)
                trm = s_mm.v

                def ptrc(v, kc=kc, trm=trm):
                    v.wait_ge(s_mm.h, trm)
                    s_ep.tick(v.tensor_copy(aT[:, kc * 8:(kc + 1) * 8], pstr[kc % 2]))
                block.vector(ptrc)
                trc[kc] = s_ep.v
            p2_ep = s_ep.v

        kh = [khA, khB]
        vh = [vhA, vhB]
        qt = [qtA, qtB]
        pT = [pT0, pT1, pT2, pT3]
        sc_scale = SC128
        st_ps = [psA[:, 0:512], psA[:, 512:1024], psB[:, 512:1024], psB[:, 1536:2048]]
        po_ps = psA[:, 1024:1536]
        sum_ps = psA[0:1, 1536:2048]
        eps_ps = psB[0:1, 1024:1536]
        pb_ps = psB[:, 0:512]

        sb_p3 = ScalBatch(block)
        kv_arr = {}
        kv_mm = {}
        st_free = {}
        pt_free = {}
        q_arr = {}
        norm_ep = {}
        norm_eps_free = [0]

        attn_fs = proj_fs
        attn_vs = proj_vs
        p3gate_ep = vmax(proj_done_ep, p2_ep)

        for kvg in range(NKV):
            def kvload(sy, kvg=kvg):
                sy.wait_ge(s_fs.h, attn_fs)
                sy.wait_ge(s_vs.h, attn_vs)
                sy.wait_ge(s_mm.h, kv_mm.get(kvg - 2, 0))
                skv = s_kv[kvg % 2]
                skv.dma(sy.dma_start(out=kh[kvg % 2][:], in_=kv_[:, kvg, :]))
                for qq in range(4):
                    skv.dma(sy.dma_start(
                        out=vh[kvg % 2][:].rearrange("p (c e) -> p c e", e=128)[:, 8 * qq:8 * qq + 8, :],
                        in_=vv[:, 8 * qq:8 * qq + 8, kvg * 128:(kvg + 1) * 128]))
            block.sync(kvload)
            kv_arr[kvg] = s_kv[kvg % 2].v

            for qh in range(kvg * HPG, (kvg + 1) * HPG):
                for tg in range(4):
                    def qload(sy, qh=qh, tg=tg):
                        sy.wait_ge(s_fs.h, attn_fs)
                        sy.wait_ge(s_mm.h, q_arr.get((qh, tg - 2), q_arr.get((qh - 1, 2 + tg), 0)))
                        s_q[tg % 2].dma(sy.dma_start(out=qt[tg % 2][:], in_=qv[:, qh, tg * 512:(tg + 1) * 512]))
                    block.sync(qload)
                    q_here = s_q[tg % 2].v

                    nown = 4 * (tg + 1)
                    kcs = list(range(16 + nown))
                    po_gate = norm_ep.get((qh, tg - 1), norm_ep.get((qh - 1, 3), 0))
                    for i, kc in enumerate(kcs):
                        pp = i % 4
                        stf = st_free.get(pp, 0)

                        def smm(t, i=i, kc=kc, pp=pp, kvg=kvg, tg=tg, q_here=q_here,
                                po_gate=po_gate, stf=stf):
                            if i == 0:
                                t.wait_ge(s_kv[kvg % 2].h, kv_arr[kvg])
                                t.wait_ge(s_q[tg % 2].h, q_here)
                                s_ep.wait(t, vmax(stf, p3gate_ep, po_gate))
                            else:
                                s_ep.wait(t, stf)
                            s_mm.tick(t.matmul(st_ps[pp], kh[kvg % 2][:, kc * 128:(kc + 1) * 128],
                                               qt[tg % 2][:], start=True, stop=True))
                        block.tensor(smm)
                        st_here = s_mm.v
                        ptf_gate = pt_free.get(pp, 0)

                        def sexp(s, i=i, kc=kc, pp=pp, st_here=st_here, ptf_gate=ptf_gate):
                            s.wait_ge(s_mm.h, max(st_here, ptf_gate))
                            if power:
                                s.activation(out=pT[pp][:], in_=st_ps[pp], func=AF.Square,
                                             scale=sc_scale).then_inc(s_ep.hs, 1)
                            elif kc < 16:
                                s.activation(out=pT[pp][:], in_=st_ps[pp], func=AF.Exp,
                                             scale=sc_scale, bias=gat[:, 1:2]).then_inc(s_ep.hs, 1)
                            else:
                                s.activation(out=pT[pp][:], in_=st_ps[pp], func=AF.Exp,
                                             scale=sc_scale).then_inc(s_ep.hs, 1)
                        sb_p3.add(sexp)
                        s_ep.reserve_s(1)
                        st_free[pp] = s_ep.v
                        exp_here = s_ep.v

                        jdiag = kc - 16 - 4 * tg
                        need_mask = 0 <= jdiag < 4
                        need_pow = power
                        if need_mask or need_pow:
                            def ptf(v, pp=pp, kc=kc, qh=qh, jdiag=jdiag, exp_here=exp_here,
                                    need_mask=need_mask, need_pow=need_pow):
                                s_ep.wait(v, exp_here)
                                last = None
                                if need_pow:
                                    last = v.tensor_scalar_mul(pT[pp][:], pT[pp][:],
                                                               aT[:, kc * 8 + qh // HPG:kc * 8 + qh // HPG + 1])
                                if need_mask:
                                    last = v.tensor_mul(pT[pp][:], pT[pp][:],
                                                        msk[:, jdiag * 512:(jdiag + 1) * 512])
                                s_ep.tick(last)
                            block.vector(ptf)
                        pt_ready = s_ep.v

                        def avmm(t, i=i, kc=kc, pp=pp, kvg=kvg, nkc=len(kcs), pt_ready=pt_ready):
                            s_ep.wait(t, pt_ready)
                            t.matmul(po_ps, vh[kvg % 2][:, kc * 128:(kc + 1) * 128], pT[pp][:],
                                     start=(i == 0), stop=(i == nkc - 1))
                            s_mm.tick(t.matmul(sum_ps, onesc[:], pT[pp][:],
                                               start=(i == 0), stop=(i == nkc - 1)))
                        block.tensor(avmm)
                        pt_free[pp] = s_mm.v
                    sb_p3.flush()
                    kv_mm[kvg] = s_mm.v
                    q_arr[(qh, tg)] = s_mm.v
                    av_done = s_mm.v

                    if power:
                        def emm(t, qh=qh, tg=tg, nef=norm_eps_free[0]):
                            s_ep.wait(t, nef)
                            s_mm.tick(t.matmul(eps_ps, idn[0:8, (qh // HPG):(qh // HPG) + 1],
                                               a_bf[0:8, T + tg * 512:T + (tg + 1) * 512],
                                               start=True, stop=True))
                        block.tensor(emm)
                        eps_here = s_mm.v

                    def nrm(v, qh=qh, tg=tg, av_done=av_done):
                        v.wait_ge(s_mm.h, av_done)
                        if power:
                            v.wait_ge(s_mm.h, eps_here)
                            v.tensor_scalar_mul(lrow[:], eps_ps, 1e-6)
                            v.tensor_add(lrow[:], lrow[:], sum_ps)
                        else:
                            v.tensor_copy(lrow[:], sum_ps)
                        v.reciprocal(linv[:], lrow[:])
                        s_ep.tick(v.tensor_copy(linb[:], linv[:]))
                    block.vector(nrm)
                    if power:
                        norm_eps_free[0] = s_ep.v
                    li_done = s_ep.v

                    def nbc(t, li_done=li_done):
                        s_ep.wait(t, li_done)
                        s_mm.tick(t.matmul(pb_ps, ones1[:], linb[:], start=True, stop=True))
                    block.tensor(nbc)
                    bc_done = s_mm.v

                    def nctx(v, qh=qh, tg=tg, bc_done=bc_done):
                        v.wait_ge(s_mm.h, bc_done)
                        v.tensor_copy(lbc[:], pb_ps)
                        s_ep.tick(v.tensor_mul(big[:, qh * 2048 + tg * 512:qh * 2048 + (tg + 1) * 512],
                                               po_ps, lbc[:]))
                    block.vector(nctx)
                    norm_ep[(qh, tg)] = s_ep.v

        attn_done_ep = s_ep.v

        woB = [woT, woU]
        xold = [xoldA, xoldA]
        xnew = [xnewA, xnewA]
        od_mm = {}
        od_ep = {}
        od_st = {}
        for dm in range(DC):
            def woload(g, dm=dm):
                g.wait_ge(s_mm.h, od_mm.get(dm - 2, 0))
                s_wo[dm % 2].dma(g.dma_start(out=woB[dm % 2][:].rearrange("p (e i) -> p e i", e=DC),
                                             in_=wov[:, :, dm * 128:(dm + 1) * 128]))
            block.gpsimd(woload)
            wo_arr = s_wo[dm % 2].v

            def xoload(sy, dm=dm):
                s_ep.wait(sy, od_ep.get(dm - 1, 0))
                s_xo.dma(sy.dma_start(out=xold[dm % 2][:], in_=xown_cols(dm, 0, T)))
            block.sync(xoload)
            xo_arr = s_xo.v

            def omm(t, dm=dm, wo_arr=wo_arr, attn_done_ep=attn_done_ep):
                t.wait_ge(s_wo[dm % 2].h, wo_arr)
                s_ep.wait(t, vmax(attn_done_ep, od_ep.get(dm - 2, 0)))
                ps = psA if dm % 2 == 0 else psB
                for ec in range(DC):
                    for tgq in range(4):
                        mmi = t.matmul(ps[:, tgq * 512:(tgq + 1) * 512], woB[dm % 2][:, ec * 128:(ec + 1) * 128],
                                       big[:, ec * 2048 + tgq * 512:ec * 2048 + (tgq + 1) * 512],
                                       start=(ec == 0), stop=(ec == DC - 1))
                s_mm.tick(mmi)
            block.tensor(omm)
            od_mm[dm] = s_mm.v
            om = s_mm.v

            def oepi(v, dm=dm, om=om, xo_arr=xo_arr):
                v.wait_ge(s_mm.h, om)
                v.wait_ge(s_xo.h, xo_arr)
                v.wait_ge(s_st.h, od_st.get(dm - 1, 0))
                ps = psA if dm % 2 == 0 else psB
                if has_bias:
                    s_ep.tick(v.scalar_tensor_tensor(xnew[dm % 2][:], ps[:, 0:2048],
                                                     bocs[:, dm:dm + 1], xold[dm % 2][:],
                                                     ALU.add, ALU.add))
                else:
                    s_ep.tick(v.scalar_tensor_tensor(xnew[dm % 2][:], ps[:, 0:2048],
                                                     1.0, xold[dm % 2][:], ALU.mult, ALU.add))
            block.vector(oepi)
            od_ep[dm] = s_ep.v
            oe = s_ep.v

            def ost(sy, dm=dm, oe=oe):
                s_ep.wait(sy, oe)
                s_st.dma(sy.dma_start(out=xov[:, dm, :], in_=xnew[dm % 2][:]))
            block.sync(ost)
            od_st[dm] = s_st.v

        if ctx is None:
            final_st = s_st.v

            def fin(g):
                g.wait_ge(s_st.h, final_st)
            block.gpsimd(fin)

    return nc


def _attn_maps(kind, inputs):
    m = {}
    if kind == "a0":
        m["wq"] = np.ascontiguousarray(np.asarray(inputs["L0_attn_a_W_qkv"], dtype=np.float32))
        b = np.asarray(inputs["L0_attn_a_b_qkv"], dtype=np.float32)
        m["bqr"] = np.ascontiguousarray(b.reshape(1, 6144))
        m["bqc"] = np.ascontiguousarray(b.reshape(48, 128).T)
        m["wo"] = np.ascontiguousarray(np.asarray(inputs["L0_attn_a_W_out"], dtype=np.float32))
        m["boc"] = np.ascontiguousarray(np.asarray(inputs["L0_attn_a_b_out"], dtype=np.float32).reshape(16, 128).T)
    elif kind == "b0":
        m["wq"] = np.ascontiguousarray(np.asarray(inputs["L0_attn_b_w_q"], dtype=np.float32))
        m["wkv"] = np.ascontiguousarray(np.asarray(inputs["L0_attn_b_w_kv"], dtype=np.float32))
        m["wo"] = np.ascontiguousarray(np.asarray(inputs["L0_attn_b_w_dense"], dtype=np.float32))
    else:
        m["wq"] = np.ascontiguousarray(np.asarray(inputs["L1_attn_b_Wq"], dtype=np.float32))
        m["wk"] = np.ascontiguousarray(np.asarray(inputs["L1_attn_b_Wk"], dtype=np.float32))
        m["wv"] = np.ascontiguousarray(np.asarray(inputs["L1_attn_b_Wv"], dtype=np.float32))
        m["wgg"] = np.ascontiguousarray(np.asarray(inputs["L1_attn_b_Wg"], dtype=np.float32))
        m["wo"] = np.ascontiguousarray(np.asarray(inputs["L1_attn_b_Wc"], dtype=np.float32))
        m["boc"] = np.ascontiguousarray(np.asarray(inputs["L1_attn_b_bc"], dtype=np.float32).reshape(16, 128).T)
    jm = np.arange(512)[None, :] >= (np.arange(128)[:, None] + np.arange(4)[:, None, None] * 128)
    m["masks"] = np.ascontiguousarray(jm.transpose(1, 0, 2).reshape(128, 4 * 512).astype(bfdt))
    m["ident"] = np.eye(128, dtype=bfdt)
    return m


def run_attn(kind, x_full, wmap):
    nc = _prog("attn_" + kind, lambda: build_attn(kind))
    in_maps = []
    for c in range(NCORES):
        b, h = c // 2, c % 2
        own = x_full[b, h * T:(h + 1) * T]
        other = x_full[b, (1 - h) * T:(2 - h) * T]
        xc = np.ascontiguousarray(np.concatenate([other.T, own.T], axis=1))
        gates = np.zeros((128, 4), dtype=np.float32)
        gates[:, 0] = float(h)
        gates[:, 1] = 0.0 if h == 1 else -60.0
        in_maps.append(dict(x=xc, gates=gates, **wmap))
    res = run_bass_kernel_spmd(nc, in_maps, core_ids=list(range(NCORES)))
    out = x_full.copy()
    for c in range(NCORES):
        b, h = c // 2, c % 2
        out[b, h * T:(h + 1) * T] = res.results[c]["xo"].T
    return out



MI = 4096
MN = 128
MNH = 64
MP = 64
MCH = 256
NCH = 16
HPC = 32


def build_mamba(phases=99, dump=None, ctx=None, pfx=""):
    if ctx is None:
        nc = bass.Bass("TRN2", target_bir_lowering=False, debug=False)
        x = nc.dram_tensor("x", [D, 2 * T], F32, kind="ExternalInput").ap()
        gates = nc.dram_tensor("gates", [128, 4], F32, kind="ExternalInput").ap()
        masks = nc.dram_tensor("masks", [128, 4 * 512], BF16, kind="ExternalInput").ap()
        ident = nc.dram_tensor("ident", [128, 128], BF16, kind="ExternalInput").ap()
        identf = nc.dram_tensor("identf", [128, 128], F32, kind="ExternalInput").ap()
        xo = nc.dram_tensor("xo", [D, T], F32, kind="ExternalOutput").ap()
        xv = x.rearrange("(k p) t -> p k t", p=128)
        xov = xo.rearrange("(k p) t -> p k t", p=128)

        def xsrc(sl, xcol0):
            return xv[:, sl, xcol0:xcol0 + T]

        def xown_cols(sl, c0, c1):
            return xv[:, sl, T + c0:T + c1]
    else:
        nc = ctx.nc
        gates = ctx.gates; masks = ctx.masks; ident = ctx.ident; identf = ctx.identf
        xv_own = ctx.x_own.rearrange("(k p) t -> p k t", p=128)
        xv_past = ctx.x_past.rearrange("(k p) t -> p k t", p=128)
        xov = xv_own

        def xsrc(sl, xcol0):
            return (xv_past if xcol0 == 0 else xv_own)[:, sl, :]

        def xown_cols(sl, c0, c1):
            return xv_own[:, sl, c0:c1]
    winT = nc.dram_tensor(pfx + "winT", [D, 8512], F32, kind="ExternalInput").ap()
    woutT = nc.dram_tensor(pfx + "woutT", [MI, D], F32, kind="ExternalInput").ap()
    convw = nc.dram_tensor(pfx + "convw", [128, 34 * 4], F32, kind="ExternalInput").ap()
    convb = nc.dram_tensor(pfx + "convb", [128, 34], F32, kind="ExternalInput").ap()
    dtb = nc.dram_tensor(pfx + "dtb", [64, 1], F32, kind="ExternalInput").ap()
    alog = nc.dram_tensor(pfx + "alog", [64, 1], F32, kind="ExternalInput").ap()
    dcolv = nc.dram_tensor(pfx + "dcol", [128, HPC], F32, kind="ExternalInput").ap()
    nwv = nc.dram_tensor(pfx + "nw", [128, HPC], F32, kind="ExternalInput").ap()

    dbg = None
    if dump == "xbc":
        dbg = nc.dram_tensor("dbg", [4352, 2 * T], BF16, kind="ExternalOutput").ap()
    elif dump == "x4f":
        dbg = nc.dram_tensor("dbg", [MI, T], BF16, kind="ExternalOutput").ap()
    elif dump == "yn":
        dbg = nc.dram_tensor("dbg", [MI, T], BF16, kind="ExternalOutput").ap()
    elif dump == "dt":
        dbg = nc.dram_tensor("dbg", [64, 2 * T], F32, kind="ExternalOutput").ap()
    elif dump == "acs":
        dbg = nc.dram_tensor("dbg", [64, 2 * T], F32, kind="ExternalOutput").ap()
    elif dump == "y":
        dbg = nc.dram_tensor("dbg", [MI, T], BF16, kind="ExternalOutput").ap()
    xbc_hbm = nc.dram_tensor(pfx + "xbc_hbm", [4352, 2 * T], BF16, kind="Internal").ap()
    z_hbm = nc.dram_tensor(pfx + "z_hbm", [MI, T], BF16, kind="Internal").ap()
    x4t_hbm = nc.dram_tensor(pfx + "x4t_hbm", [2 * T, MI], BF16, kind="Internal").ap()
    x4f_hbm = nc.dram_tensor(pfx + "x4f_hbm", [MI, T], BF16, kind="Internal").ap()
    yn_hbm = nc.dram_tensor(pfx + "yn_hbm", [MI, T], BF16, kind="Internal").ap()

    winv = winT.rearrange("(k p) e -> p k e", p=128)
    woutv = woutT.rearrange("(c p) d -> p c d", p=128)
    xbcv = xbc_hbm.rearrange("(c p) t -> p c t", p=128)
    zv = z_hbm.rearrange("(c p) t -> p c t", p=128)
    x4tv = x4t_hbm.rearrange("(tc p) hp -> p tc hp", p=128)
    x4fv = x4f_hbm.rearrange("(c p) t -> p c t", p=128)
    ynv = yn_hbm.rearrange("(c p) t -> p c t", p=128)

    import contextlib
    es = contextlib.ExitStack()
    SB = lambda n, s, d: es.enter_context(nc.sbuf_tensor(pfx + n, s, d))
    PS = lambda n, s, d: es.enter_context(nc.psum_tensor(pfx + n, s, d))
    SEM = lambda n: es.enter_context(nc.semaphore(pfx + n))
    with es:
        big = SB("big", [128, 32768], BF16)
        Ysb = SB("Ysb", [128, HPC * MCH], BF16)
        x4ft = SB("x4ft", [128, HPC * MCH], BF16)
        zsil = big[:, 24576:32768]
        wtA = x4ft[:, 0:DC * 128]; wtB = x4ft[:, DC * 128:2 * DC * 128]
        ptA = Ysb[:, 0:T]; ptB = Ysb[:, T:2 * T]
        dt_sb = SB("dt_sb", [64, 2 * T], BF16)
        acs = SB("acs", [64, 2 * T], F32)
        acs_bf = SB("acs_bf", [64, 2 * T], BF16)
        acsT = SB("acsT", [128, 32 * 64], F32)
        lam_bf = SB("lam_bf", [64, NCH], BF16)
        lamhp = SB("lamhp", [128, HPC * NCH], F32)
        Bfm = SB("Bfm", [128, 2 * T], BF16)
        Cfm = SB("Cfm", [128, 2 * T], BF16)
        Sfp = SB("Sfp", [128, HPC * 128], F32)
        Sin = SB("Sin", [128, HPC * 128], BF16)
        slbf = SB("slbf", [128, MI], BF16)
        xpad = Ysb[:, 4093:8192]
        cvb = slbf
        x4s = Sin
        convws = SB("convws", [128, 34 * 4], F32)
        convbs = SB("convbs", [128, 34], F32)
        dtbs = SB("dtbs", [64, 1], F32)
        alogs = SB("alogs", [64, 1], F32)
        acol = SB("acol", [64, 1], F32)
        dcols = SB("dcols", [128, HPC], F32)
        nws = SB("nws", [128, HPC], F32)
        gat = SB("gat", [128, 4], F32)
        msk = SB("msk", [128, 4 * 512], BF16)
        idn = SB("idn", [128, 128], BF16)
        idnf = SB("idnf", [128, 128], F32)
        onesf = SB("onesf", [1, 128], F32)
        erowf = SB("erowf", [1, MCH], F32)
        w2 = SB("w2", [64, MCH], F32)
        w2b = SB("w2b", [64, MCH], BF16)
        wT1 = SB("wT1", [128, 128], F32)
        wT2 = SB("wT2", [128, 128], F32)
        eacs = SB("eacs", [64, MCH], BF16)
        erow = SB("erow", [1, MCH], BF16)
        GTsb = SB("GTsb", [128, 2 * MCH], BF16)
        wtsb = SB("wtsb", [128, MCH], BF16)
        decL = SB("decL", [128, MCH], BF16)
        diffs = SB("diffs", [128, MCH], F32)
        zeros2 = SB("zeros2", [128, MCH], F32)
        tmpf = SB("tmpf", [128, MCH], F32)
        Btm = SB("Btm", [128, 2 * 128], BF16)
        hstrA = x4ft[:, 4096:4608]; hstrB = x4ft[:, 4608:5120]
        xoldA = x4ft[:, 5120:6144].bitcast(F32)
        xnewA = x4ft[:, 6144:7168].bitcast(F32)
        ones1 = SB("ones1", [1, 128], BF16)
        onesc = SB("onesc", [128, 1], BF16)
        x2 = x4ft[:, 4096:4608]
        mbc = Sin[:, 0:512]; rbc = Sin[:, 512:1024]
        mrow = Sin[0:1, 1024:2048].bitcast(F32)
        vrow = Sin[0:1, 2048:3072].bitcast(F32)
        trow = Sin[0:1, 3072:4096].bitcast(F32)
        mbfr = slbf[0:1, 0:512]; rbfr = slbf[0:1, 512:1024]
        rrow = x4ft[0:1, 4608:5120].bitcast(F32)
        if ctx is None:
            psA = PS("psA", [128, 2048], F32)
            psB = PS("psB", [128, 2048], F32)
            s_ld = Sem(SEM("s_ld")); s_w = Sem(SEM("s_w")); s_mm = Sem(SEM("s_mm"))
            s_ep = EpSem(SEM("s_epv"), SEM("s_eps")); s_st = Sem(SEM("s_st")); s_h = Sem(SEM("s_h"))
            s_c = Sem(SEM("s_c"))
            s_g = Sem(SEM("s_g"))
            s_fs = Sem(SEM("s_fs"))
            s_x4f = Sem(SEM("s_x4f"))
            s_x4t = Sem(SEM("s_x4t"))
            s_yn = Sem(SEM("s_yn"))
            s_xo = Sem(SEM("s_xo"))
            s_cl = Sem(SEM("s_cl"))
            s_x4l = Sem(SEM("s_x4l"))
            s_zx = Sem(SEM("s_zx"))
            s_hp = [Sem(SEM("s_hl0")), Sem(SEM("s_hl1"))]
            block = es.enter_context(nc.Block())
        else:
            psA = ctx.psA; psB = ctx.psB; block = ctx.block
            s_ld = ctx.s_ld; s_w = ctx.s_w; s_mm = ctx.s_mm; s_ep = ctx.s_ep
            s_st = ctx.s_st; s_h = ctx.s_h; s_c = ctx.s_c; s_g = ctx.s_g
            s_fs = ctx.s_fs; s_x4f = ctx.s_x4f; s_x4t = ctx.s_x4t; s_yn = ctx.s_yn
            s_xo = ctx.s_xo; s_cl = ctx.s_cl; s_x4l = ctx.s_x4l; s_zx = ctx.s_zx
            s_hp = ctx.s_hp
        sems = dict(mm=s_mm, ep=s_ep)
        sc = dict(x2=x2, mrow=mrow, vrow=vrow, trow=trow, mbfr=mbfr, rbfr=rbfr, mbc=mbc, rbc=rbc)
        xbf = big

        def ld0(sy):
            s_c.dma(sy.dma_start(out=convws[:], in_=convw[:]))
            s_c.dma(sy.dma_start(out=convbs[:], in_=convb[:]))
            s_c.dma(sy.dma_start(out=dtbs[:], in_=dtb[:]))
            s_c.dma(sy.dma_start(out=alogs[:], in_=alog[:]))
            s_c.dma(sy.dma_start(out=dcols[:], in_=dcolv[:]))
            s_c.dma(sy.dma_start(out=nws[:], in_=nwv[:]))
            s_c.dma(sy.dma_start(out=gat[:], in_=gates[:]))
            s_c.dma(sy.dma_start(out=msk[:], in_=masks[:]))
            s_c.dma(sy.dma_start(out=idn[:], in_=ident[:]))
            s_c.dma(sy.dma_start(out=idnf[:], in_=identf[:]))
        block.sync(ld0)

        def ld0g(g):
            g.memset(ones1[:], 1.0)
            g.memset(onesc[:], 1.0)
            g.memset(onesf[:], 1.0)
            g.memset(zeros2[:], 0.0)
            g.memset(Sfp[:], 0.0)
            s_g.tick(g.memset(xpad[:, 0:3], 0.0))
        block.gpsimd(ld0g)
        consts_c = s_c.v
        consts_g = s_g.v
        for sec in (block.tensor, block.vector, block.scalar):
            sec(lambda e: (e.wait_ge(s_c.h, consts_c), e.wait_ge(s_g.h, consts_g)))

        wt = [wtA, wtB]
        pt = [ptA, ptB]
        fm_mm = {}
        fm_ep = {}
        fm_st = {}
        jg = [0]

        def emit_proj(dst_view, dst_chunk, dst_col0, src_ec, xln_ready, to_dt=None):
            j = jg[0]; jg[0] += 1
            pp = j % 2
            ps = psA if pp == 0 else psB
            ncols = 64 if to_dt is not None else 128

            def wl(g, j=j, pp=pp, src_ec=src_ec, ncols=ncols):
                g.wait_ge(s_mm.h, fm_mm.get(j - 2, 0))
                s_w.dma(g.dma_start(out=wt[pp][:, 0:DC * ncols].rearrange("p (k i) -> p k i", k=DC),
                                    in_=winv[:, :, src_ec * 128:src_ec * 128 + ncols]))
            block.gpsimd(wl)
            w_arr = s_w.v

            def mmj(t, j=j, pp=pp, ps=ps, w_arr=w_arr, xln_ready=xln_ready, ncols=ncols):
                t.wait_ge(s_w.h, w_arr)
                s_ep.wait(t, vmax(fm_ep.get(j - 2, 0), xln_ready))
                for k in range(DC):
                    for q in range(4):
                        mmi = t.matmul(ps[0:ncols, q * 512:(q + 1) * 512],
                                       wt[pp][:, k * ncols:(k + 1) * ncols],
                                       xbf[:, k * T + q * 512:k * T + (q + 1) * 512],
                                       start=(k == 0), stop=(k == DC - 1))
                s_mm.tick(mmi)
            block.tensor(mmj)
            fm_mm[j] = s_mm.v
            mmv = s_mm.v

            if to_dt is not None:
                def epid(v, ps=ps, mmv=mmv, to_dt=to_dt):
                    v.wait_ge(s_mm.h, mmv)
                    s_ep.tick(v.tensor_copy(dt_sb[0:64, to_dt:to_dt + T], ps[0:64, 0:2048]))
                block.vector(epid)
                fm_ep[j] = s_ep.v
            else:
                def epij(s, pp=pp, ps=ps, mmv=mmv, fst=fm_st.get(j - 2, 0)):
                    s.wait_ge(s_mm.h, mmv)
                    s.wait_ge(s_fs.h, fst)
                    s.activation(out=pt[pp][:], in_=ps[:, 0:2048], func=AF.Copy).then_inc(s_ep.hs, 1)
                sb1m.add(epij)
                s_ep.reserve_s(1)
                fm_ep[j] = s_ep.v
                epv = s_ep.v

                def stj(sy, pp=pp, dst_view=dst_view, dst_chunk=dst_chunk, dst_col0=dst_col0, epv=epv):
                    s_ep.wait(sy, epv)
                    s_fs.dma(sy.dma_start(out=dst_view[:, dst_chunk, dst_col0:dst_col0 + T], in_=pt[pp][:]))
                block.sync(stj)
                fm_st[j] = s_fs.v

        sb1m = ScalBatch(block)
        for half in range(2):
            xcol0 = half * T
            xgate = s_mm.v

            def xload(g, xcol0=xcol0, xgate=xgate):
                g.wait_ge(s_mm.h, xgate)
                for qq in range(4):
                    s_ld.dma(g.dma_start(
                        out=xbf[:, 0:DC * T].rearrange("p (k t) -> p k t", k=DC)[:, 4 * qq:4 * qq + 4, :],
                        in_=xsrc(slice(4 * qq, 4 * qq + 4), xcol0)))
            block.gpsimd(xload)
            xld = s_ld.v
            for sec in (block.tensor, block.vector, block.scalar):
                sec(lambda e, xld=xld: e.wait_ge(s_ld.h, xld))
            emit_layernorm(block, sems, xbf, psA, psB, ones1, onesc, sc)
            xln_ready = s_ep.v

            if half == 1:
                for ec in range(32):
                    emit_proj(zv, ec, 0, ec, xln_ready)
            for ec in range(34):
                emit_proj(xbcv, ec, xcol0, 32 + ec, xln_ready)
            emit_proj(None, 0, 0, 66, xln_ready, to_dt=xcol0)
            sb1m.flush()

        proj_mm = s_mm.v
        proj_ep = s_ep.v
        proj_st = s_fs.v

        if phases < 2:
            ff = s_st.v
            block.gpsimd(lambda g: g.wait_ge(s_st.h, ff))
            return nc
        def m2a(v):
            s_ep.wait(v, proj_ep)
            s_ep.tick(v.tensor_scalar_add(acs[0:64, :], dt_sb[0:64, :], dtbs[0:64, 0:1]))
        block.vector(m2a)
        t1 = s_ep.v
        if phases < 22:
            ff = s_st.v; fe = s_ep.v
            block.gpsimd(lambda g: (g.wait_ge(s_st.h, ff), s_ep.wait(g, fe)))
            return nc

        def m2b(s, t1=t1):
            s_ep.wait(s, t1)
            s.activation(out=dt_sb[0:64, :], in_=acs[0:64, :], func=AF.Abs)
            s.activation(out=dt_sb[0:64, :], in_=dt_sb[0:64, :], func=AF.Exp, scale=-1.0)
            s.activation(out=dt_sb[0:64, :], in_=dt_sb[0:64, :], func=AF.Ln, bias=1.0)
            s.activation(out=acol[0:64, :], in_=alogs[0:64, :], func=AF.Exp)
            s_ep.tick(s.mul(acol[0:64, :], acol[0:64, :], -1.0))
        block.scalar(m2b)
        t2 = s_ep.v

        def m2c(v, t2=t2):
            s_ep.wait(v, t2)
            v.tensor_relu(acs[0:64, :], acs[0:64, :])
            v.tensor_add(dt_sb[0:64, :], dt_sb[0:64, :], acs[0:64, :])
            v.tensor_scalar_min(dt_sb[0:64, :], dt_sb[0:64, :], 1.0e10)
            s_ep.tick(v.tensor_scalar_mul(acs[0:64, :], dt_sb[0:64, :], acol[0:64, 0:1]))
        block.vector(m2c)
        t3 = s_ep.v
        if phases < 23:
            ff = s_st.v; fe = s_ep.v
            block.gpsimd(lambda g: (g.wait_ge(s_st.h, ff), s_ep.wait(g, fe)))
            return nc

        def m2d(v, t3=t3):
            s_ep.wait(v, t3)
            gones = big[0:64, 0:512].bitcast(F32)
            v.wait_ge(s_mm.h, proj_mm)
            v.memset(gones, 1.0)
            for c in range(NCH):
                la = v.tensor_tensor_scan(acs[0:64, c * MCH:(c + 1) * MCH], gones,
                                          acs[0:64, c * MCH:(c + 1) * MCH], 0.0, ALU.mult, ALU.add)
            v.tensor_copy(acs_bf[0:64, :], acs[0:64, :])
            s_ep.tick(la)
        block.vector(m2d)
        t4 = s_ep.v
        if phases < 24:
            ff = s_st.v; fe = s_ep.v
            block.gpsimd(lambda g: (g.wait_ge(s_st.h, ff), s_ep.wait(g, fe)))
            return nc

        def m2e0(v, t4=t4):
            s_ep.wait(v, t4)
            s_ep.tick(v.tensor_copy(w2[0:64, 0:NCH],
                                    acs[0:64, :].rearrange("h (c l) -> h c l", l=MCH)[:, :, MCH - 1]))
        block.vector(m2e0)
        t4b = s_ep.v

        def m2e(s, t4b=t4b):
            s_ep.wait(s, t4b)
            s_ep.tick(s.activation(out=lam_bf[0:64, :], in_=w2[0:64, 0:NCH], func=AF.Exp))
        block.scalar(m2e)
        t5 = s_ep.v
        if phases < 25:
            ff = s_st.v; fe = s_ep.v
            block.gpsimd(lambda g: (g.wait_ge(s_st.h, ff), s_ep.wait(g, fe)))
            return nc

        trc = {}
        for kc in range(32):
            def ptr(t, kc=kc, t4=t4):
                s_ep.wait(t, t4 if kc < 2 else trc[kc - 2])
                pst = (psA if kc % 2 == 0 else psB)[:, 1536:1600]
                s_mm.tick(t.transpose(pst, acs[0:64, kc * 128:(kc + 1) * 128], idnf[0:64, 0:64]))
            block.tensor(ptr)
            trm = s_mm.v

            def ptc(v, kc=kc, trm=trm):
                v.wait_ge(s_mm.h, trm)
                pst = (psA if kc % 2 == 0 else psB)[:, 1536:1600]
                s_ep.tick(v.tensor_copy(acsT[:, kc * 64:(kc + 1) * 64], pst))
            block.vector(ptc)
            trc[kc] = s_ep.v
        acst_done = s_ep.v
        if phases < 26:
            ff = s_st.v; fe = s_ep.v
            block.gpsimd(lambda g: (g.wait_ge(s_st.h, ff), s_ep.wait(g, fe)))
            return nc

        lhp = {}
        lrowm = {}
        for h in range(64):
            base = 64 * (h % 2)

            def lex(t, h=h, base=base):
                s_ep.wait(t, vmax(acst_done, lrowm.get(h - 2, 0)))
                s_mm.tick(t.matmul(psA[0:1, 1536 + (h % 2) * 16:1552 + (h % 2) * 16],
                                   idn[0:64, h:h + 1], lam_bf[0:64, :], start=True, stop=True))
            block.tensor(lex)
            lem = s_mm.v

            def lrowc(v, h=h, lem=lem):
                v.wait_ge(s_mm.h, lem)
                s_ep.tick(v.tensor_copy(erow[0:1, (h % 2) * 16:(h % 2) * 16 + 16].bitcast(BF16),
                                        psA[0:1, 1536 + (h % 2) * 16:1552 + (h % 2) * 16]))
            block.vector(lrowc)
            lrowm[h] = s_ep.v
            lrw = s_ep.v

            def lbc(t, h=h, base=base, lrw=lrw):
                s_ep.wait(t, vmax(lrw, lhp.get(h - 2, 0)))
                s_mm.tick(t.matmul(psB[base:base + 64, 1536 + (h % 2) * 16:1552 + (h % 2) * 16],
                                   ones1[0:1, 0:64],
                                   erow[0:1, (h % 2) * 16:(h % 2) * 16 + 16].bitcast(BF16),
                                   start=True, stop=True))
            block.tensor(lbc)
            lbm = s_mm.v

            def lco(v, h=h, base=base, lbm=lbm):
                v.wait_ge(s_mm.h, lbm)
                hpc = h // 2
                s_ep.tick(v.tensor_copy(lamhp[base:base + 64, hpc * NCH:(hpc + 1) * NCH],
                                        psB[base:base + 64, 1536 + (h % 2) * 16:1552 + (h % 2) * 16]))
            block.vector(lco)
            lhp[h] = s_ep.v
        m2_done = s_ep.v

        if phases < 30:
            ff = s_st.v
            block.gpsimd(lambda g: (g.wait_ge(s_st.h, ff), s_ep.wait(g, m2_done)))
            return nc
        sb3m = ScalBatch(block)
        cv_ep = {}
        cv_mm = {}
        cvst = {}
        x4st = {}
        for ic in range(34):
            def cload(g, ic=ic):
                g.wait_ge(s_fs.h, proj_st)
                s_ep.wait(g, cv_ep.get(ic - 1, 0))
                s_cl.dma(g.dma_start(out=xpad[:, 3:3 + 2 * T], in_=xbcv[:, ic, :]))
            block.gpsimd(cload)
            c_arr = s_cl.v

            def conv(v, ic=ic, c_arr=c_arr):
                v.wait_ge(s_cl.h, c_arr)
                s_ep.wait(v, vmax(m2_done, cv_ep.get(ic - 1, 0)))
                v.wait_ge(s_mm.h, cv_mm.get(ic - 1, 0))
                v.wait_ge(s_x4f.h, cvst.get(ic - 1, 0))
                v.tensor_scalar_mul(xpad[:, 3:3 + T], xpad[:, 3:3 + T], gat[:, 0:1])
                for hf in range(2):
                    ps = (psA if hf == 0 else psB)[:, 0:2048]
                    v.tensor_scalar(ps, xpad[:, hf * 2048:hf * 2048 + 2048],
                                    convws[:, ic * 4:ic * 4 + 1], None, ALU.mult)
                    for j in range(1, 4):
                        la = v.scalar_tensor_tensor(ps, xpad[:, hf * 2048 + j:hf * 2048 + j + 2048],
                                                    convws[:, ic * 4 + j:ic * 4 + j + 1], ps,
                                                    ALU.mult, ALU.add)
                s_ep.tick(la)
            block.vector(conv)
            cvd = s_ep.v

            def csil(s, ic=ic, cvd=cvd):
                s_ep.wait(s, cvd)
                s.activation(out=cvb[:, 0:2048], in_=psA[:, 0:2048], func=AF.Silu,
                             bias=convbs[:, ic:ic + 1])
                s.activation(out=cvb[:, 2048:4096], in_=psB[:, 0:2048], func=AF.Silu,
                             bias=convbs[:, ic:ic + 1]).then_inc(s_ep.hs, 1)
            sb3m.add(csil)
            s_ep.reserve_s(1)
            cv_ep[ic] = s_ep.v
            sil_done = s_ep.v

            if ic < 32:
                def x4fst(sy, ic=ic, sil_done=sil_done):
                    s_ep.wait(sy, sil_done)
                    s_x4f.dma(sy.dma_start(out=x4fv[:, ic, :], in_=cvb[:, T:2 * T]))
                block.sync(x4fst)
                cvst[ic] = s_x4f.v
                for j in range(32):
                    def ctr(t, ic=ic, j=j, sil_done=sil_done):
                        s_ep.wait(t, sil_done if j < 2 else x4st[j - 2])
                        pst = (psA if j % 2 == 0 else psB)[:, 1792:1856].bitcast(BF16)
                        s_mm.tick(t.transpose(pst, cvb[:, j * 128:(j + 1) * 128], idn[:]))
                    block.tensor(ctr)
                    trm = s_mm.v

                    def ctc(v, j=j, trm=trm):
                        v.wait_ge(s_mm.h, trm)
                        pst = (psA if j % 2 == 0 else psB)[:, 1792:1856].bitcast(BF16)
                        s_ep.tick(v.tensor_copy(x4s[:, j * 128:(j + 1) * 128], pst))
                    block.vector(ctc)
                    x4st[j] = s_ep.v
                cv_mm[ic] = s_mm.v
                stg_done = s_ep.v

                def x4tst(sy, ic=ic, stg_done=stg_done):
                    s_ep.wait(sy, stg_done)
                    s_x4t.dma(sy.dma_start(out=x4tv[:, :, ic * 128:(ic + 1) * 128],
                                          in_=x4s[:].rearrange("p (tc i) -> p tc i", i=128)))
                block.sync(x4tst)
            elif ic == 32:
                def bcp(v, sil_done=sil_done):
                    s_ep.wait(v, sil_done)
                    s_ep.tick(v.tensor_copy(Bfm[:], cvb[:]))
                block.vector(bcp)
                cv_ep[ic] = s_ep.v
                cv_mm[ic] = s_mm.v
            else:
                def ccp(v, sil_done=sil_done):
                    s_ep.wait(v, sil_done)
                    s_ep.tick(v.tensor_copy(Cfm[:], cvb[:]))
                block.vector(ccp)
                cv_ep[ic] = s_ep.v
                cv_mm[ic] = s_mm.v
        m3_stf = s_x4f.v
        sb3m.flush()
        m3_stt = s_x4t.v
        m3_ep = s_ep.v

        if phases < 40:
            ff = s_st.v
            block.gpsimd(lambda g: (g.wait_ge(s_st.h, ff), s_ep.wait(g, m3_ep)))
            return nc
        x4t_t = big[:, 0:8192]
        xs1v = big[:, 8192:16384]
        xs2v = big[:, 16384:24576]
        decb = wtsb
        wts2 = SB("wts2", [128, MCH], BF16)
        w1b = SB("w1b", [64, MCH], BF16)
        w1T = SB("w1T", [128, 128], F32)
        sfpbA = SB("sfpbA", [128, 128], BF16)
        sfpbB = SB("sfpbB", [128, 128], BF16)
        sfpb = [sfpbA, sfpbB]
        tmpf2 = SB("tmpf2", [128, MCH], F32)
        erow2 = SB("erow2", [1, MCH], BF16)

        sbm = ScalBatch(block)
        ch_ep = m3_ep
        ch_mm = s_mm.v
        ch_st = m3_stt
        zx_gate = [0]
        yst_gate = [0]
        prev_wde = [0]
        ydst_hold = [0]
        for ci in range(NCH):
            own = ci >= 8
            def x4l(g, ci=ci, ch_ep=ch_ep, ch_mm=ch_mm):
                g.wait_ge(s_x4t.h, m3_stt)
                g.wait_ge(s_mm.h, ch_mm)
                s_ep.wait(g, ch_ep)
                s_x4l.dma(g.dma_start(out=x4t_t.rearrange("p (s hp) -> p s hp", s=2),
                                      in_=x4tv[:, 2 * ci:2 * ci + 2, :]))
            block.gpsimd(x4l)
            x4a = s_x4l.v

            def wrow_s(s, ci=ci, ch_ep=ch_ep):
                s_ep.wait(s, ch_ep)
                s.activation(out=w2[0:64, :], in_=acs[0:64, ci * MCH:(ci + 1) * MCH],
                             func=AF.Exp, scale=-1.0,
                             bias=acs[0:64, (ci + 1) * MCH - 1:(ci + 1) * MCH]).then_inc(s_ep.hs, 1)
            sbm.add(wrow_s)
            s_ep.reserve_s(1)
            w2a = s_ep.v

            def wrow_v(v, ci=ci, own=own, w2a=w2a):
                s_ep.wait(v, w2a)
                v.tensor_mul(w2[0:64, :], w2[0:64, :], dt_sb[0:64, ci * MCH:(ci + 1) * MCH])
                if ci < 8:
                    v.tensor_scalar_mul(w2[0:64, :], w2[0:64, :], gat[0:64, 0:1])
                if own:
                    v.tensor_copy(w1b[0:64, :], dt_sb[0:64, ci * MCH:(ci + 1) * MCH])
                s_ep.tick(v.tensor_copy(w2b[0:64, :], w2[0:64, :]))
            block.vector(wrow_v)
            w2c = s_ep.v


            def wtr(t, ci=ci, own=own, w2c=w2c):
                s_ep.wait(t, w2c)
                t.transpose(psA[:, 0:32].bitcast(BF16), w2b[0:64, 0:128], idn[0:64, 0:64])
                t.transpose(psA[:, 512:544].bitcast(BF16), w2b[0:64, 128:256], idn[0:64, 0:64])
                if own:
                    t.transpose(psA[:, 1024:1056].bitcast(BF16), w1b[0:64, 0:128], idn[0:64, 0:64])
                    t.transpose(psB[:, 1024:1056].bitcast(BF16), w1b[0:64, 128:256], idn[0:64, 0:64])
                t.transpose(psB[:, 0:64].bitcast(BF16), Bfm[:, ci * MCH:ci * MCH + 128], idn[:])
                s_mm.tick(t.transpose(psB[:, 512:576].bitcast(BF16),
                                      Bfm[:, ci * MCH + 128:ci * MCH + 256], idn[:]))
            block.tensor(wtr)
            wtm = s_mm.v

            def wtc(v, own=own, wtm=wtm):
                v.wait_ge(s_mm.h, wtm)
                v.tensor_copy(wT2[:, 0:64], psA[:, 0:32].bitcast(BF16))
                v.tensor_copy(wT2[:, 64:128], psA[:, 512:544].bitcast(BF16))
                if own:
                    v.tensor_copy(w1T[:, 0:64], psA[:, 1024:1056].bitcast(BF16))
                    v.tensor_copy(w1T[:, 64:128], psB[:, 1024:1056].bitcast(BF16))
                v.tensor_copy(Btm[:, 0:128], psB[:, 0:64].bitcast(BF16))
                s_ep.tick(v.tensor_copy(Btm[:, 128:256], psB[:, 512:576].bitcast(BF16)))
            block.vector(wtc)
            wtcd = s_ep.v

            def xsb(v, own=own, wtcd=wtcd, x4a=x4a):
                s_ep.wait(v, wtcd)
                v.wait_ge(s_x4l.h, x4a)
                for sub in range(2):
                    src3 = x4t_t[:, sub * 4096:(sub + 1) * 4096].rearrange("p (h q) -> p h q", h=64)
                    w2s = wT2[:, sub * 64:(sub + 1) * 64].broadcast_to([128, 64, 64])
                    v.tensor_tensor(xs2v[:, sub * 4096:(sub + 1) * 4096].rearrange("p (h q) -> p h q", h=64),
                                    src3, w2s, ALU.mult)
                    if own:
                        w1s = w1T[:, sub * 64:(sub + 1) * 64].broadcast_to([128, 64, 64])
                        la = v.tensor_tensor(xs1v[:, sub * 4096:(sub + 1) * 4096].rearrange("p (h q) -> p h q", h=64),
                                             src3, w1s, ALU.mult)
                    else:
                        la = None
                s_ep.tick(la if la is not None else v.tensor_copy(wts2[:, 0:1], wT2[:, 0:1]))
            block.vector(xsb)
            xsd = s_ep.v

            def slmm(t, xsd=xsd, slgate=ch_ep):
                s_ep.wait(t, xsd)
                for sub in range(2):
                    for nq in range(8):
                        ps = psA if nq < 4 else psB
                        q = nq % 4
                        mmi = t.matmul(ps[:, q * 512:(q + 1) * 512], Btm[:, sub * 128:(sub + 1) * 128],
                                       xs2v[:, sub * 4096 + nq * 512:sub * 4096 + (nq + 1) * 512],
                                       start=(sub == 0), stop=(sub == 1))
                s_mm.tick(mmi)
            block.tensor(slmm)
            slm = s_mm.v

            def slev(v, slm=slm):
                v.wait_ge(s_mm.h, slm)
                v.tensor_copy(slbf[:, 0:2048], psA[:, 0:2048])
                s_ep.tick(v.tensor_copy(slbf[:, 2048:4096], psB[:, 0:2048]))
            block.vector(slev)
            sld = s_ep.v

            if own:
                co = ci - 8
                sintr = {}
                strm = {}
                for hpc in range(HPC):
                    def sbc(v, hpc=hpc, sld=sld):
                        v.wait_ge(s_mm.h, strm.get(hpc - 2, 0))
                        if hpc == 0:
                            s_ep.wait(v, sld)
                        s_ep.tick(v.tensor_copy(sfpb[hpc % 2][:], Sfp[:, hpc * 128:(hpc + 1) * 128]))
                    block.vector(sbc)
                    sbd = s_ep.v

                    def str_(t, hpc=hpc, sbd=sbd, sld=sld):
                        s_ep.wait(t, vmax(sbd, sintr.get(hpc - 2, 0)))
                        pst = (psA if hpc % 2 == 0 else psB)[:, 1088:1152].bitcast(BF16)
                        s_mm.tick(t.transpose(pst, sfpb[hpc % 2][:], idn[:]))
                    block.tensor(str_)
                    strm[hpc] = s_mm.v
                    stm = s_mm.v

                    def stc(v, hpc=hpc, stm=stm):
                        v.wait_ge(s_mm.h, stm)
                        pst = (psA if hpc % 2 == 0 else psB)[:, 1088:1152].bitcast(BF16)
                        s_ep.tick(v.tensor_copy(Sin[:, hpc * 128:(hpc + 1) * 128], pst))
                    block.vector(stc)
                    sintr[hpc] = s_ep.v
                sin_done = s_ep.v

                def gtm(t, ci=ci, sin_done=sin_done):
                    s_ep.wait(t, sin_done)
                    t.matmul(psA[:, 0:256], Bfm[:, ci * MCH:ci * MCH + 128],
                             Cfm[:, ci * MCH:(ci + 1) * MCH], start=True, stop=True)
                    s_mm.tick(t.matmul(psA[:, 256:512], Bfm[:, ci * MCH + 128:ci * MCH + 256],
                                       Cfm[:, ci * MCH:(ci + 1) * MCH], start=True, stop=True))
                block.tensor(gtm)
                gtd = s_mm.v

                def gte(v, gtd=gtd):
                    v.wait_ge(s_mm.h, gtd)
                    v.tensor_mul(GTsb[:, 0:256], psA[:, 0:256], msk[:, 0:256])
                    s_ep.tick(v.tensor_mul(GTsb[:, 256:512], psA[:, 256:512], msk[:, 512:768]))
                block.vector(gte)
                gted = s_ep.v

                def eacs_s(s, ci=ci, gted=gted):
                    s_ep.wait(s, gted)
                    s.activation(out=eacs[0:64, :], in_=acs[0:64, ci * MCH:(ci + 1) * MCH],
                                 func=AF.Exp).then_inc(s_ep.hs, 1)
                sbm.add(eacs_s)
                s_ep.reserve_s(1)
                eacd = s_ep.v

                def zx4l(g, co=co, sld=sld, m3f=m3_stf):
                    g.wait_ge(s_x4f.h, m3f)
                    s_ep.wait(g, zx_gate[0])
                    for qq in range(4):
                        s_zx.dma(g.dma_start(
                            out=zsil[:].rearrange("p (c l) -> p c l", l=MCH)[:, 8 * qq:8 * qq + 8, :],
                            in_=zv[:, 8 * qq:8 * qq + 8, co * MCH:(co + 1) * MCH]))
                    for qq in range(4):
                        s_zx.dma(g.dma_start(
                            out=x4ft[:].rearrange("p (c l) -> p c l", l=MCH)[:, 8 * qq:8 * qq + 8, :],
                            in_=x4fv[:, 8 * qq:8 * qq + 8, co * MCH:(co + 1) * MCH]))
                block.gpsimd(zx4l)
                zx4a = s_zx.v

                hpc_ep = {}
                for hpc in range(HPC):
                    ypp = hpc % 2
                    ytile = psA[:, 512 + ypp * 256:512 + ypp * 256 + 256]
                    y2tile = psA[:, 1024 + ypp * 256:1024 + ypp * 256 + 256]
                    def dle(t, hpc=hpc, eacd=eacd, prv=None):
                        s_ep.wait(t, vmax(eacd, hpc_ep.get(hpc - 1, 0)))
                        s_mm.tick(t.matmul(psA[0:1, 1792:1792 + 256], idn[0:64, 2 * hpc:2 * hpc + 1],
                                           eacs[0:64, :], start=True, stop=True))
                    block.tensor(dle)
                    d1 = s_mm.v

                    def dlc(v, d1=d1):
                        v.wait_ge(s_mm.h, d1)
                        s_ep.tick(v.tensor_copy(erow[0:1, :], psA[0:1, 1792:1792 + 256]))
                    block.vector(dlc)
                    d2 = s_ep.v

                    def dle2(t, hpc=hpc, d2=d2):
                        s_ep.wait(t, d2)
                        t.matmul(psB[0:64, 0:256], ones1[0:1, 0:64], erow[0:1, :], start=True, stop=True)
                        s_mm.tick(t.matmul(psA[0:1, 1792:1792 + 256], idn[0:64, 2 * hpc + 1:2 * hpc + 2],
                                           eacs[0:64, :], start=True, stop=True))
                    block.tensor(dle2)
                    d3 = s_mm.v

                    def dlc2(v, d3=d3):
                        v.wait_ge(s_mm.h, d3)
                        s_ep.tick(v.tensor_copy(erow2[0:1, :], psA[0:1, 1792:1792 + 256]))
                    block.vector(dlc2)
                    d4 = s_ep.v

                    def dle3(t, d4=d4):
                        s_ep.wait(t, d4)
                        s_mm.tick(t.matmul(psB[64:128, 0:256], ones1[0:1, 0:64], erow2[0:1, :],
                                           start=True, stop=True))
                    block.tensor(dle3)
                    d5 = s_mm.v

                    def dlc3(v, d5=d5):
                        v.wait_ge(s_mm.h, d5)
                        s_ep.tick(v.tensor_copy(decL[:, 0:256], psB[:, 0:256]))
                    block.vector(dlc3)
                    dld = s_ep.v

                    def i2m(t, hpc=hpc, ci=ci, dld=dld, sin_done=sin_done):
                        s_ep.wait(t, vmax(hpc_ep.get(hpc - 2, 0), dld))
                        s_mm.tick(t.matmul(y2tile, Sin[:, hpc * 128:(hpc + 1) * 128],
                                           Cfm[:, ci * MCH:(ci + 1) * MCH], start=True, stop=True))
                    block.tensor(i2m)
                    i2d = s_mm.v

                    y1t = {}
                    for hh in range(2):
                        h = 2 * hpc + hh
                        def wex(t, h=h, ci=ci, i2d=i2d):
                            t.wait_ge(s_mm.h, i2d)
                            s_mm.tick(t.matmul(psA[0:1, 1792:1792 + 256], idnf[0:64, h:h + 1],
                                               acs[0:64, ci * MCH:(ci + 1) * MCH],
                                               start=True, stop=True))
                        block.tensor(wex)
                        we1 = s_mm.v

                        def wec(v, we1=we1):
                            v.wait_ge(s_mm.h, we1)
                            s_ep.tick(v.tensor_copy(erowf[0:1, :], psA[0:1, 1792:1792 + 256]))
                        block.vector(wec)
                        we2 = s_ep.v

                        def web(t, we2=we2):
                            s_ep.wait(t, we2)
                            s_mm.tick(t.matmul(psA[:, 1536:1792], onesf[:], erowf[0:1, :],
                                               start=True, stop=True))
                        block.tensor(web)
                        we3 = s_mm.v

                        for sub in range(2):
                            prev_y1 = s_mm.v

                            def wdf(v, ci=ci, h=h, sub=sub, we3=we3, pwde=prev_wde[0]):
                                v.wait_ge(s_mm.h, we3)
                                s_ep.wait(v, pwde)
                                s_ep.tick(v.scalar_tensor_tensor(
                                    diffs[:], psA[:, 1536:1792],
                                    acsT[:, (2 * ci + sub) * 64 + h:(2 * ci + sub) * 64 + h + 1],
                                    zeros2[:], ALU.subtract, ALU.min))
                            block.vector(wdf)
                            wd1 = s_ep.v

                            def wde(s, wd1=wd1):
                                s_ep.wait(s, wd1)
                                s.activation(out=wtsb[:], in_=diffs[:], func=AF.Exp).then_inc(s_ep.hs, 1)
                            sbm.add(wde)
                            s_ep.reserve_s(1)
                            wd2 = s_ep.v
                            prev_wde[0] = wd2

                            def wdm(v, sub=sub, wd2=wd2, prev_y1=prev_y1):
                                s_ep.wait(v, wd2)
                                v.wait_ge(s_mm.h, prev_y1)
                                s_ep.tick(v.tensor_mul(wts2[:], GTsb[:, sub * 256:(sub + 1) * 256], wtsb[:]))
                            block.vector(wdm)
                            wd3 = s_ep.v

                            def y1m(t, h=h, hh=hh, sub=sub, wd3=wd3, ytile=ytile):
                                s_ep.wait(t, wd3)
                                s_mm.tick(t.matmul(ytile[hh * 64:hh * 64 + 64, :],
                                                   xs1v[:, sub * 4096 + h * 64:sub * 4096 + (h + 1) * 64],
                                                   wts2[:], start=(sub == 0), stop=(sub == 1)))
                            block.tensor(y1m)
                            y1t[(hh, sub)] = s_mm.v
                    y1_done = s_mm.v

                    def yev(v, hpc=hpc, co=co, y1_done=y1_done, zx4a=zx4a, ytile=ytile, y2tile=y2tile,
                            ysg=yst_gate[0]):
                        v.wait_ge(s_mm.h, y1_done)
                        v.wait_ge(s_zx.h, zx4a)
                        v.wait_ge(s_yn.h, ysg)
                        v.tensor_mul(tmpf[:], y2tile, decL[:, 0:256])
                        v.scalar_tensor_tensor(tmpf2[:], x4ft[:, hpc * MCH:(hpc + 1) * MCH],
                                               dcols[:, hpc:hpc + 1], ytile, ALU.mult, ALU.add)
                        s_ep.tick(v.tensor_add(Ysb[:, hpc * MCH:(hpc + 1) * MCH], tmpf[:], tmpf2[:]))
                    block.vector(yev)
                    hpc_ep[hpc] = s_ep.v
                y_done = s_ep.v

                if dump == "y":
                    dbgv = dbg.rearrange("(c p) t -> p c t", p=128)
                    def ydmp(sy, co=co, y_done=y_done):
                        s_ep.wait(sy, y_done)
                        for qq in range(4):
                            s_st.dma(sy.dma_start(
                                out=dbgv[:, 8 * qq:8 * qq + 8, co * MCH:(co + 1) * MCH],
                                in_=Ysb[:].rearrange("p (c l) -> p c l", l=MCH)[:, 8 * qq:8 * qq + 8, :]))
                    block.sync(ydmp)
                    ydst_hold[0] = s_st.v

                def gat_s(s, y_done=y_done, zx4a=zx4a, _yd=(dump == "y")):
                    if _yd:
                        s.wait_ge(s_st.h, ydst_hold[0])
                    s_ep.wait(s, y_done)
                    s.wait_ge(s_zx.h, zx4a)
                    s.activation(out=zsil[:], in_=zsil[:], func=AF.Silu).then_inc(s_ep.hs, 1)
                sbm.add(gat_s)
                s_ep.reserve_s(1)
                zs1 = s_ep.v

                def gat_v(v, zs1=zs1):
                    s_ep.wait(v, zs1)
                    s_ep.tick(v.tensor_mul(Ysb[:], Ysb[:], zsil[:]))
                block.vector(gat_v)
                yf1 = s_ep.v

                sq_mm = {}
                for hpc in range(HPC):
                    def sqs(s, hpc=hpc, yf1=yf1, sqg=sq_mm.get(hpc - 1, 0)):
                        s_ep.wait(s, yf1)
                        s.wait_ge(s_mm.h, sqg)
                        s.activation(out=x2[:, 0:256], in_=Ysb[:, hpc * MCH:(hpc + 1) * MCH],
                                     func=AF.Square).then_inc(s_ep.hs, 1)
                    sbm.add(sqs)
                    s_ep.reserve_s(1)
                    sqd = s_ep.v

                    def sqm(t, hpc=hpc, sqd=sqd):
                        s_ep.wait(t, sqd)
                        s_mm.tick(t.matmul(psB[0:1, 256:512], onesc[:], x2[:, 0:256],
                                           start=(hpc == 0), stop=(hpc == HPC - 1)))
                    block.tensor(sqm)
                    sq_mm[hpc] = s_mm.v
                ssqd = s_mm.v

                def rstd_v(v, ssqd=ssqd):
                    v.wait_ge(s_mm.h, ssqd)
                    v.tensor_scalar_mul(rrow[:], psB[0:1, 256:512], 1.0 / MI)
                    s_ep.tick(v.tensor_scalar_add(rrow[:], rrow[:], 1e-5))
                block.vector(rstd_v)
                rs1 = s_ep.v

                def rstd_s(s, rs1=rs1):
                    s_ep.wait(s, rs1)
                    s.activation(out=rrow[:], in_=rrow[:], func=AF.Ln)
                    s.activation(out=rrow[:], in_=rrow[:], func=AF.Exp, scale=-0.5).then_inc(s_ep.hs, 1)
                sbm.add(rstd_s)
                s_ep.reserve_s(1)
                rs2 = s_ep.v

                def rstd_b(v, rs2=rs2):
                    s_ep.wait(v, rs2)
                    s_ep.tick(v.tensor_copy(erow[0:1, :], rrow[:]))
                block.vector(rstd_b)
                rs3 = s_ep.v

                def rstd_m(t, rs3=rs3):
                    s_ep.wait(t, rs3)
                    s_mm.tick(t.matmul(psB[:, 512:768], ones1[:], erow[0:1, :], start=True, stop=True))
                block.tensor(rstd_m)
                rs4 = s_mm.v

                def rstd_c(v, rs4=rs4):
                    v.wait_ge(s_mm.h, rs4)
                    s_ep.tick(v.tensor_copy(tmpf[:], psB[:, 512:768]))
                block.vector(rstd_c)
                rs5 = s_ep.v

                yn_ep = {}
                for hpc in range(HPC):
                    def ynv_(v, hpc=hpc, rs5=rs5):
                        s_ep.wait(v, rs5)
                        s_ep.tick(v.scalar_tensor_tensor(
                            Ysb[:, hpc * MCH:(hpc + 1) * MCH], Ysb[:, hpc * MCH:(hpc + 1) * MCH],
                            nws[:, hpc:hpc + 1], tmpf[:], ALU.mult, ALU.mult))
                    block.vector(ynv_)
                    yn_ep[hpc] = s_ep.v
                ynd = s_ep.v

                def ynst(sy, co=co, ynd=ynd):
                    s_ep.wait(sy, ynd)
                    for qq in range(4):
                        s_yn.dma(sy.dma_start(
                            out=ynv[:, 8 * qq:8 * qq + 8, co * MCH:(co + 1) * MCH],
                            in_=Ysb[:].rearrange("p (c l) -> p c l", l=MCH)[:, 8 * qq:8 * qq + 8, :]))
                block.sync(ynst)
                zx_gate[0] = s_ep.v
                yst_gate[0] = s_yn.v

            sbm.flush()
            rec_mm = {}
            rec_ep = {}
            base_ep = y_done if own else sld
            for hpc in range(HPC):
                def rtr(t, hpc=hpc, ci=ci, base_ep=base_ep):
                    s_ep.wait(t, vmax(base_ep, rec_ep.get(hpc - 2, 0)))
                    pst = (psA if hpc % 2 == 0 else psB)[:, 1792:1856].bitcast(BF16)
                    s_mm.tick(t.transpose(pst, slbf[:, hpc * 128:(hpc + 1) * 128], idn[:]))
                block.tensor(rtr)
                rtm = s_mm.v

                def rup(v, hpc=hpc, ci=ci, rtm=rtm):
                    v.wait_ge(s_mm.h, rtm)
                    pst = (psA if hpc % 2 == 0 else psB)[:, 1792:1856].bitcast(BF16)
                    s_ep.tick(v.scalar_tensor_tensor(
                        Sfp[:, hpc * 128:(hpc + 1) * 128], Sfp[:, hpc * 128:(hpc + 1) * 128],
                        lamhp[:, hpc * NCH + ci:hpc * NCH + ci + 1],
                        pst, ALU.mult, ALU.add))
                block.vector(rup)
                rec_ep[hpc] = s_ep.v
                rec_mm[hpc] = s_mm.v
            ch_ep = s_ep.v
            ch_mm = s_mm.v

        mamba_st = s_yn.v
        mamba_ep = s_ep.v
        mamba_mm = s_mm.v

        if phases < 50:
            ff = s_st.v
            fe = s_ep.v
            block.gpsimd(lambda g: (g.wait_ge(s_st.h, ff), s_ep.wait(g, fe)))
            return nc
        woh = big[:, 0:32768]
        hstr = [hstrA, hstrB]
        m6_mm = {}
        m6_ep = {}
        m6_st = {}
        hl_mm = {}
        for dmh in range(2):
            def wol(g, dmh=dmh, mamba_mm=mamba_mm):
                g.wait_ge(s_mm.h, max(mamba_mm, m6_mm.get((dmh - 1, 3), 0)))
                s_ep.wait(g, mamba_ep)
                for qq in range(4):
                    s_w.dma(g.dma_start(
                        out=woh.rearrange("p (c d) -> p c d", d=1024)[:, 8 * qq:8 * qq + 8, :],
                        in_=woutv[:, 8 * qq:8 * qq + 8, dmh * 1024:(dmh + 1) * 1024]))
            block.gpsimd(wol)
            wo_arr = s_w.v

            for tp in range(4):
                for icc in range(HPC):
                    bufh = icc % 2
                    def hload(g, icc=icc, tp=tp, bufh=bufh, mamba_st=mamba_st):
                        g.wait_ge(s_yn.h, mamba_st)
                        g.wait_ge(s_mm.h, hl_mm.get((dmh, tp, icc - 2),
                                                    hl_mm.get((dmh, tp - 1, HPC - 2 + icc),
                                                              hl_mm.get((dmh - 1, 3, HPC - 2 + icc), 0))))
                        s_hp[bufh].dma(g.dma_start(out=hstr[bufh][:], in_=ynv[:, icc, tp * 512:(tp + 1) * 512]))
                    block.gpsimd(hload)
                    h_arr = s_hp[bufh].v

                    dpf = m6_ep.get((dmh, tp - 1), m6_ep.get((dmh - 1, 3), 0))

                    def dmm(t, icc=icc, bufh=bufh, h_arr=h_arr, wo_arr=wo_arr, dpf=dpf):
                        t.wait_ge(s_w.h, wo_arr)
                        t.wait_ge(s_hp[bufh].h, h_arr)
                        if icc == 0:
                            s_ep.wait(t, dpf)
                        for j in range(8):
                            ps = psA if j < 4 else psB
                            jj = j % 4
                            mmi = t.matmul(ps[:, jj * 512:(jj + 1) * 512],
                                           woh[:, icc * 1024 + j * 128:icc * 1024 + (j + 1) * 128],
                                           hstr[bufh][:], start=(icc == 0), stop=(icc == HPC - 1))
                        s_mm.tick(mmi)
                    block.tensor(dmm)
                    hl_mm[(dmh, tp, icc)] = s_mm.v
                dmm_all = s_mm.v

                for j in range(8):
                    dmc = dmh * 8 + j
                    xof = m6_ep.get((dmh, tp, j - 1),
                                    m6_ep.get((dmh, tp - 1, 7),
                                              m6_ep.get((dmh - 1, 3, 7), 0)))

                    def xload(sy, dmc=dmc, tp=tp, xof=xof):
                        s_ep.wait(sy, xof)
                        s_xo.dma(sy.dma_start(out=xoldA[:], in_=xown_cols(dmc, tp * 512, (tp + 1) * 512)))
                    block.sync(xload)
                    x_arr = s_xo.v

                    xnf = m6_st.get((dmh, tp, j - 1), m6_st.get((dmh, tp - 1, 7), m6_st.get((dmh - 1, 3, 7), 0)))

                    def epi(v, j=j, x_arr=x_arr, dmm_all=dmm_all, xnf=xnf):
                        v.wait_ge(s_mm.h, dmm_all)
                        v.wait_ge(s_xo.h, x_arr)
                        v.wait_ge(s_st.h, xnf)
                        ps = psA if j < 4 else psB
                        jj = j % 4
                        s_ep.tick(v.scalar_tensor_tensor(xnewA[:], ps[:, jj * 512:(jj + 1) * 512],
                                                         1.0, xoldA[:], ALU.mult, ALU.add))
                    block.vector(epi)
                    m6_ep[(dmh, tp, j)] = s_ep.v
                    e_done = s_ep.v

                    def xst(sy, dmc=dmc, tp=tp, e_done=e_done):
                        s_ep.wait(sy, e_done)
                        s_st.dma(sy.dma_start(out=xov[:, dmc, tp * 512:(tp + 1) * 512], in_=xnewA[:]))
                    block.sync(xst)
                    m6_st[(dmh, tp, j)] = s_st.v
                m6_ep[(dmh, tp)] = s_ep.v
            m6_mm[(dmh, 3)] = s_mm.v

        if dump is not None and dump != "y":
            fe = s_ep.v; fs = s_st.v
            def dmp(g, fe=fe, fs=fs):
                s_ep.wait(g, fe)
                g.wait_ge(s_st.h, fs)
                if dump == "xbc":
                    g.dma_start(out=dbg[:], in_=xbc_hbm[:]).then_inc(s_st.h, 16)
                elif dump == "x4f":
                    g.dma_start(out=dbg[:], in_=x4f_hbm[:]).then_inc(s_st.h, 16)
                elif dump == "yn":
                    g.dma_start(out=dbg[:], in_=yn_hbm[:]).then_inc(s_st.h, 16)
                elif dump == "dt":
                    g.dma_start(out=dbg[:], in_=dt_sb[:]).then_inc(s_st.h, 16)
                elif dump == "acs":
                    g.dma_start(out=dbg[:], in_=acs[:]).then_inc(s_st.h, 16)
                s_st.v += 16
            block.gpsimd(dmp)
        if ctx is None:
            final_st = s_st.v

            def fin(g):
                g.wait_ge(s_st.h, final_st)
            block.gpsimd(fin)

    return nc


def _mamba_maps(inputs):
    m = {}
    m["winT"] = np.ascontiguousarray(np.asarray(inputs["L1_attn_a_W_in"], dtype=np.float32).T)
    m["woutT"] = np.ascontiguousarray(np.asarray(inputs["L1_attn_a_W_out"], dtype=np.float32).T)
    cw = np.asarray(inputs["L1_attn_a_conv_w"], dtype=np.float32)[:, 0, :]
    cwp = np.zeros((128, 34 * 4), dtype=np.float32)
    cbp = np.zeros((128, 34), dtype=np.float32)
    cb = np.asarray(inputs["L1_attn_a_conv_b"], dtype=np.float32)
    for cc in range(34):
        cwp[:, cc * 4:(cc + 1) * 4] = cw[cc * 128:(cc + 1) * 128]
        cbp[:, cc] = cb[cc * 128:(cc + 1) * 128]
    m["convw"] = cwp
    m["convb"] = cbp
    m["dtb"] = np.ascontiguousarray(np.asarray(inputs["L1_attn_a_dt_bias"], dtype=np.float32).reshape(64, 1))
    m["alog"] = np.ascontiguousarray(np.asarray(inputs["L1_attn_a_A_log"], dtype=np.float32).reshape(64, 1))
    dv = np.asarray(inputs["L1_attn_a_D"], dtype=np.float32)
    hp = np.arange(MI)
    m["dcol"] = np.ascontiguousarray(dv[(hp // 64)].reshape(HPC, 128).T)
    m["nw"] = np.ascontiguousarray(np.asarray(inputs["L1_attn_a_norm_w"], dtype=np.float32).reshape(HPC, 128).T)
    jm = np.arange(512)[None, :] >= (np.arange(128)[:, None] + np.arange(4)[:, None, None] * 128)
    m["masks"] = np.ascontiguousarray(jm.transpose(1, 0, 2).reshape(128, 4 * 512).astype(bfdt))
    m["ident"] = np.eye(128, dtype=bfdt)
    m["identf"] = np.eye(128, dtype=np.float32)
    return m


def run_mamba(x_full, wmap):
    nc = _prog("mamba", build_mamba)
    in_maps = []
    for c in range(NCORES):
        b, h = c // 2, c % 2
        own = x_full[b, h * T:(h + 1) * T]
        other = x_full[b, (1 - h) * T:(2 - h) * T]
        xc = np.ascontiguousarray(np.concatenate([other.T, own.T], axis=1))
        gates = np.zeros((128, 4), dtype=np.float32)
        gates[:, 0] = float(h)
        in_maps.append(dict(x=xc, gates=gates, **wmap))
    res = run_bass_kernel_spmd(nc, in_maps, core_ids=list(range(NCORES)))
    out = x_full.copy()
    for c in range(NCORES):
        b, h = c // 2, c % 2
        out[b, h * T:(h + 1) * T] = res.results[c]["xo"].T
    return out



class _Ctx:
    pass


def _emit_barrier(ctx):
    vals = [(s.h, s.v) for s in ctx.all_sems if s.v > 0]
    epv = ctx.s_ep.v
    for sec in (ctx.block.tensor, ctx.block.vector, ctx.block.scalar,
                ctx.block.gpsimd, ctx.block.sync):
        def bw(e, vals=vals, epv=epv):
            for h, v in vals:
                e.wait_ge(h, v)
            ctx.s_ep.wait(e, epv)
        sec(bw)


def _emit_exchange(ctx, n):
    import contextlib
    nc = ctx.nc
    block = ctx.block
    s_ex, s_ec, s_ep = ctx.s_ex, ctx.s_ec, ctx.s_ep
    xov = ctx.x_own.rearrange("(k p) t -> p k t", p=128)
    civ = ctx.cc_in.rearrange("(k p) t -> p k t", p=128)
    cov = ctx.cc_out.rearrange("(k p) t -> p k t", p=128)
    xpv = ctx.x_past.rearrange("(k p) t -> p k t", p=128)
    es = contextlib.ExitStack()
    with es:
        sA = es.enter_context(nc.sbuf_tensor("ex%dA" % n, [128, 32768], BF16))
        sB = es.enter_context(nc.sbuf_tensor("ex%dB" % n, [128, 32768], BF16))
        exg = es.enter_context(nc.sbuf_tensor("ex%dg" % n, [128, 4], F32))
        sAv = sA[:, 0:DC * T].rearrange("p (k t) -> p k t", k=DC)
        sBv = sB[:, 0:DC * T].rearrange("p (k t) -> p k t", k=DC)

        def g1(g):
            s_ex.dma(g.dma_start(out=exg[:], in_=ctx.gates[:]))
            for q in range(4):
                s_ex.dma(g.dma_start(out=sAv[:, 4 * q:4 * q + 4, :],
                                     in_=xov[:, 4 * q:4 * q + 4, :]))
            g.wait_ge(s_ex.h, s_ex.v)
            for q in range(4):
                s_ex.dma(g.dma_start(out=civ[:, 4 * q:4 * q + 4, :],
                                     in_=sAv[:, 4 * q:4 * q + 4, :]))
            g.wait_ge(s_ex.h, s_ex.v)
            g.collective_compute(
                "AllGather", ALU.bypass,
                replica_groups=[[0, 1], [2, 3], [4, 5], [6, 7]],
                ins=[ctx.cc_in[:].opt()], outs=[ctx.cc_out[:].opt()],
            ).then_inc(s_ec.h)
            s_ec.v += 1
            g.wait_ge(s_ec.h, s_ec.v)
            for q in range(4):
                s_ex.dma(g.dma_start(out=sAv[:, 4 * q:4 * q + 4, :],
                                     in_=cov[:, 4 * q:4 * q + 4, :]))
                s_ex.dma(g.dma_start(out=sBv[:, 4 * q:4 * q + 4, :],
                                     in_=cov[:, DC + 4 * q:DC + 4 * q + 4, :]))
        block.gpsimd(g1)
        ldv = s_ex.v

        def v1(v):
            v.wait_ge(s_ex.h, ldv)
            for q in range(4):
                v.tensor_scalar_mul(sA[:, q * 8192:(q + 1) * 8192],
                                    sA[:, q * 8192:(q + 1) * 8192], exg[:, 0:1])
                s_ep.tick(v.scalar_tensor_tensor(
                    sA[:, q * 8192:(q + 1) * 8192], sB[:, q * 8192:(q + 1) * 8192],
                    exg[:, 2:3], sA[:, q * 8192:(q + 1) * 8192], ALU.mult, ALU.add))
        block.vector(v1)
        epv = s_ep.v

        def g2(g):
            s_ep.wait(g, epv)
            for q in range(4):
                s_ex.dma(g.dma_start(out=xpv[:, 4 * q:4 * q + 4, :],
                                     in_=sAv[:, 4 * q:4 * q + 4, :]))
        block.gpsimd(g2)


_FUSED = {}


def build_fused():
    import contextlib
    nc = bass.Bass("TRN2", target_bir_lowering=False, debug=False)
    ctx = _Ctx()
    ctx.nc = nc
    ctx.dram_cache = {}
    ctx.x0 = nc.dram_tensor("x0", [D, T], F32, kind="ExternalInput").ap()
    ctx.gates = nc.dram_tensor("gates", [128, 4], F32, kind="ExternalInput").ap()
    ctx.masks = nc.dram_tensor("masks", [128, 4 * 512], BF16, kind="ExternalInput").ap()
    ctx.ident = nc.dram_tensor("ident", [128, 128], BF16, kind="ExternalInput").ap()
    ctx.identf = nc.dram_tensor("identf", [128, 128], F32, kind="ExternalInput").ap()
    ctx.xo = nc.dram_tensor("xo", [D, T], F32, kind="ExternalOutput").ap()
    ctx.x_own = nc.dram_tensor("x_own", [D, T], F32, kind="Internal").ap()
    ctx.x_past = nc.dram_tensor("x_past", [D, T], BF16, kind="Internal").ap()
    ctx.cc_in = nc.dram_tensor("cc_in", [D, T], BF16, kind="Internal").ap()
    ctx.cc_out = nc.dram_tensor("cc_out", [2 * D, T], BF16, kind="Internal").ap()
    ctx.h_hbm = nc.dram_tensor("h_hbm", [IFF, T], BF16, kind="Internal").ap()

    es = contextlib.ExitStack()
    with es:
        ctx.all_sems = []

        def SEMs(name):
            s = Sem(es.enter_context(nc.semaphore(name)))
            ctx.all_sems.append(s)
            return s
        ctx.psA = es.enter_context(nc.psum_tensor("psA", [128, 2048], F32))
        ctx.psB = es.enter_context(nc.psum_tensor("psB", [128, 2048], F32))
        ctx.s_ld = SEMs("s_ld"); ctx.s_w = SEMs("s_w"); ctx.s_mm = SEMs("s_mm")
        ctx.s_ep = EpSem(es.enter_context(nc.semaphore("s_epv")),
                         es.enter_context(nc.semaphore("s_eps")))
        ctx.s_st = SEMs("s_st"); ctx.s_h = SEMs("s_h")
        ctx.s_c = SEMs("s_c"); ctx.s_g = SEMs("s_g")
        ctx.s_hs = SEMs("s_hs"); ctx.s_fs = SEMs("s_fs"); ctx.s_vs = SEMs("s_vs")
        ctx.s_vw = SEMs("s_vw"); ctx.s_xo = SEMs("s_xo")
        ctx.s_x4f = SEMs("s_x4f"); ctx.s_x4t = SEMs("s_x4t"); ctx.s_yn = SEMs("s_yn")
        ctx.s_cl = SEMs("s_cl"); ctx.s_x4l = SEMs("s_x4l"); ctx.s_zx = SEMs("s_zx")
        ctx.s_wp = [ctx.s_w, SEMs("s_w1")]
        ctx.s_hp = [ctx.s_h, SEMs("s_h1")]
        ctx.s_cp = [SEMs("s_cp0"), SEMs("s_cp1")]
        ctx.s_xop = [ctx.s_xo, SEMs("s_xo1")]
        ctx.s_stp = [ctx.s_st, SEMs("s_st1")]
        ctx.s_wo = [SEMs("s_wo0"), SEMs("s_wo1")]
        ctx.s_kv = [SEMs("s_kv0"), SEMs("s_kv1")]
        ctx.s_q = [SEMs("s_q0"), SEMs("s_q1")]
        ctx.s_ex = SEMs("s_ex"); ctx.s_ec = SEMs("s_ec")
        block = es.enter_context(nc.Block())
        ctx.block = block

        def ini(g):
            ctx.s_ex.dma(g.dma_start(out=ctx.x_own[:], in_=ctx.x0[:]))
        block.gpsimd(ini)
        _emit_barrier(ctx)

        def scoped(name, fn):
            with nc.named_scope(name):
                fn()
            _emit_barrier(ctx)
        scoped("ex0", lambda: _emit_exchange(ctx, 0))
        scoped("a0", lambda: build_attn("a0", ctx=ctx, pfx="a0"))
        scoped("ffnA", lambda: build_ffn(ctx=ctx, pfx="f0", uid="fA"))
        scoped("ex1", lambda: _emit_exchange(ctx, 1))
        scoped("b0", lambda: build_attn("b0", ctx=ctx, pfx="b0"))
        scoped("ffnB", lambda: build_ffn(ctx=ctx, pfx="f0", uid="fB"))
        scoped("ex2", lambda: _emit_exchange(ctx, 2))
        scoped("mamba", lambda: build_mamba(ctx=ctx, pfx="mb"))
        scoped("ffnC", lambda: build_ffn(ctx=ctx, pfx="f1", uid="fC"))
        scoped("ex3", lambda: _emit_exchange(ctx, 3))
        scoped("b1", lambda: build_attn("b1", ctx=ctx, pfx="b1"))
        scoped("ffnD", lambda: build_ffn(ctx=ctx, pfx="f1", uid="fD"))

        def fino(g):
            ctx.s_ex.dma(g.dma_start(out=ctx.xo[:], in_=ctx.x_own[:]))
            g.wait_ge(ctx.s_ex.h, ctx.s_ex.v)
        block.gpsimd(fino)
    return nc


def kernel(**inputs):
    x = np.asarray(inputs["L0_attn_a_x"], dtype=np.float32)
    if "nc" not in _FUSED:
        _FUSED["nc"] = build_fused()
    nc = _FUSED["nc"]
    wmap = {}
    for p, wm in [("f0", _ffn_weight_maps("L0_ffn", inputs)),
                  ("f1", _ffn_weight_maps("L1_ffn", inputs)),
                  ("a0", _attn_maps("a0", inputs)),
                  ("b0", _attn_maps("b0", inputs)),
                  ("b1", _attn_maps("b1", inputs)),
                  ("mb", _mamba_maps(inputs))]:
        for k, v in wm.items():
            wmap[p + k] = v
    masks = wmap.pop("a0masks"); ident = wmap.pop("a0ident")
    identf = wmap.pop("mbidentf")
    for p in ("b0", "b1", "mb"):
        wmap.pop(p + "masks"); wmap.pop(p + "ident")
    in_maps = []
    for c in range(NCORES):
        b, h = c // 2, c % 2
        x0 = np.ascontiguousarray(x[b, h * T:(h + 1) * T].T)
        gates = np.zeros((128, 4), dtype=np.float32)
        gates[:, 0] = float(h)
        gates[:, 1] = 0.0 if h == 1 else -60.0
        gates[:, 2] = 1.0 - float(h)
        in_maps.append(dict(x0=x0, gates=gates, masks=masks, ident=ident,
                            identf=identf, **wmap))
    res = run_bass_kernel_spmd(nc, in_maps, core_ids=list(range(NCORES)))
    out = np.empty((B * S, D), dtype=np.float32)
    for c in range(NCORES):
        out[c * T:(c + 1) * T] = res.results[c]["xo"].T
    return out.reshape(B, S, D)


# version 10
# speedup vs baseline: 1.1541x; 1.1541x over previous
import numpy as np
import ml_dtypes
import concourse.bass as bass
import concourse.mybir as mybir
from concourse.bass_utils import run_bass_kernel_spmd

F32 = mybir.dt.float32
BF16 = mybir.dt.bfloat16
I32 = mybir.dt.int32
AF = mybir.ActivationFunctionType
ALU = mybir.AluOpType
AX = mybir.AxisListType
bfdt = ml_dtypes.bfloat16

D = 2048
DC = D // 128
T = 2048
B, S = 4, 4096
NCORES = 8


class Sem:
    def __init__(self, h):
        self.h = h
        self.v = 0

    def dma(self, instr):
        instr.then_inc(self.h, 16)
        self.v += 16
        return self.v

    def tick(self, instr, n=1):
        instr.then_inc(self.h, n)
        self.v += n
        return self.v


ZEP = (0, 0)


def vmax(*vals):
    out = [0, 0]
    for v in vals:
        if v == 0:
            continue
        out[0] = max(out[0], v[0])
        out[1] = max(out[1], v[1])
    return (out[0], out[1])


class EpSem:
    def __init__(self, hv, hs):
        self.hv, self.hs = hv, hs
        self.nv = 0
        self.ns = 0

    def tick(self, instr, n=1):
        e = str(instr.ins.engine)
        if "DVE" in e or "Pool" in e:
            instr.then_inc(self.hv, n)
            self.nv += n
        else:
            instr.then_inc(self.hs, n)
            self.ns += n
        return self.v

    @property
    def v(self):
        return (self.nv, self.ns)

    def wait(self, eng, val):
        if val == 0:
            return
        nv, ns = val
        if nv:
            eng.wait_ge(self.hv, nv)
        if ns:
            eng.wait_ge(self.hs, ns)

    def reserve_s(self, n=1):
        self.ns += n
        return self.v


class ScalBatch:
    def __init__(self, block):
        self.block = block
        self.fns = []

    def add(self, fn):
        self.fns.append(fn)

    def flush(self):
        if self.fns:
            fns = tuple(self.fns)
            self.fns = []
            self.block.scalar(lambda s, fns=fns: [f(s) for f in fns])



def emit_layernorm(block, sems, xbf, psA, psB, ones1, onesc, sc, ntok=T):
    s_mm, s_ep = sems["mm"], sems["ep"]
    TGn = ntok // 512
    x2, mrow, vrow, trow = sc["x2"], sc["mrow"], sc["vrow"], sc["trow"]
    mbfr, rbfr, mbc, rbc = sc["mbfr"], sc["rbfr"], sc["mbc"], sc["rbc"]

    sb = ScalBatch(block)
    for tg in range(TGn):
        t0 = tg * 512
        for k in range(DC):
            wait_mm = s_mm.v

            def sq(s, k=k, t0=t0, wait_mm=wait_mm):
                s.wait_ge(s_mm.h, wait_mm)
                s.activation(out=x2[:], in_=xbf[:, k * ntok + t0:k * ntok + t0 + 512],
                             func=AF.Square).then_inc(s_ep.hs, 1)
            sb.add(sq)
            s_ep.reserve_s(1)
            sq_done = s_ep.v

            def mm(t, k=k, t0=t0, sq_done=sq_done):
                s_ep.wait(t, sq_done)
                t.matmul(psA[0:1, 0:512], onesc[:], xbf[:, k * ntok + t0:k * ntok + t0 + 512],
                         start=(k == 0), stop=(k == DC - 1))
                s_mm.tick(t.matmul(psB[0:1, 0:512], onesc[:], x2[:],
                                   start=(k == 0), stop=(k == DC - 1)))
            block.tensor(mm)
        stats_done = s_mm.v

        def rowmath(v, stats_done=stats_done):
            v.wait_ge(s_mm.h, stats_done)
            v.tensor_scalar_mul(mrow[:], psA[0:1, 0:512], 1.0 / D)
            v.tensor_scalar_mul(vrow[:], psB[0:1, 0:512], 1.0 / D)
            v.tensor_mul(trow[:], mrow[:], mrow[:])
            v.tensor_sub(vrow[:], vrow[:], trow[:])
            s_ep.tick(v.tensor_scalar_add(vrow[:], vrow[:], 1e-5))
        block.vector(rowmath)
        var_done = s_ep.v

        def rstd(s, var_done=var_done):
            s_ep.wait(s, var_done)
            s.activation(out=trow[:], in_=vrow[:], func=AF.Ln)
            s.activation(out=trow[:], in_=trow[:], func=AF.Exp, scale=-0.5).then_inc(s_ep.hs, 1)
        sb.add(rstd)
        s_ep.reserve_s(1)
        rstd_done = s_ep.v

        def tobf(v, rstd_done=rstd_done):
            s_ep.wait(v, rstd_done)
            v.tensor_copy(mbfr[:], mrow[:])
            s_ep.tick(v.tensor_copy(rbfr[:], trow[:]))
        block.vector(tobf)
        bf_done = s_ep.v

        def bcast(t, bf_done=bf_done):
            s_ep.wait(t, bf_done)
            t.matmul(psA[0:128, 0:512], ones1[:], mbfr[:], start=True, stop=True)
            s_mm.tick(t.matmul(psB[0:128, 0:512], ones1[:], rbfr[:], start=True, stop=True))
        block.tensor(bcast)
        bc_done = s_mm.v

        def xform(v, t0=t0, bc_done=bc_done):
            v.wait_ge(s_mm.h, bc_done)
            v.tensor_copy(mbc[:], psA[0:128, 0:512])
            v.tensor_copy(rbc[:], psB[0:128, 0:512])
            for k in range(DC):
                xk = xbf[:, k * ntok + t0:k * ntok + t0 + 512]
                v.tensor_sub(xk, xk, mbc[:])
                last = v.tensor_mul(xk, xk, rbc[:])
            s_ep.tick(last)
        block.vector(xform)
    sb.flush()



IFF = 8192
ICn = IFF // 128


def build_ffn(ctx=None, pfx="", uid=None):
    uid = pfx if uid is None else uid
    if ctx is None:
        nc = bass.Bass("TRN2", target_bir_lowering=False, debug=False)
        x = nc.dram_tensor("x", [D, T], F32, kind="ExternalInput").ap()
        xo = nc.dram_tensor("xo", [D, T], F32, kind="ExternalOutput").ap()
        xv = x.rearrange("(k p) t -> p k t", p=128)
        xov = xo.rearrange("(k p) t -> p k t", p=128)
        h_hbm = nc.dram_tensor("h_hbm", [IFF, T], BF16, kind="Internal").ap()
    else:
        nc = ctx.nc
        xv = ctx.x_own.rearrange("(k p) t -> p k t", p=128)
        xov = xv
        h_hbm = ctx.h_hbm
    def DT(name, shape, dt):
        if ctx is not None and name in ctx.dram_cache:
            return ctx.dram_cache[name]
        ap = nc.dram_tensor(name, shape, dt, kind="ExternalInput").ap()
        if ctx is not None:
            ctx.dram_cache[name] = ap
        return ap
    gw = DT(pfx + "gw", [D, IFF], I32)
    uw = DT(pfx + "uw", [D, IFF], I32)
    dw = DT(pfx + "dw", [IFF, D], I32)
    gsc = DT(pfx + "gsc", [128, ICn], F32)
    usc = DT(pfx + "usc", [128, ICn], F32)
    dsc = DT(pfx + "dsc", [128, DC], F32)

    gwv = gw.rearrange("(k p) i -> p k i", p=128)
    uwv = uw.rearrange("(k p) i -> p k i", p=128)
    dwv = dw.rearrange("(c p) d -> p c d", p=128)
    hv = h_hbm.rearrange("(c p) t -> p c t", p=128)

    import contextlib
    es = contextlib.ExitStack()
    def SB(name, shape, dt):
        return es.enter_context(nc.sbuf_tensor(uid + name, shape, dt))
    def PS(name, shape, dt):
        return es.enter_context(nc.psum_tensor(uid + name, shape, dt))
    def SEM(name):
        return es.enter_context(nc.semaphore(uid + name))
    with es:
        big = SB("big", [128, ICn * 1024], BF16)
        xbf = big
        arena = big
        wgA = SB("wgA", [128, DC * 128], BF16); wgB = SB("wgB", [128, DC * 128], BF16)
        wuA = SB("wuA", [128, DC * 128], BF16); wuB = SB("wuB", [128, DC * 128], BF16)
        gss = SB("gss", [128, ICn], F32); uss = SB("uss", [128, ICn], F32); dss = SB("dss", [128, DC], F32)
        gsil = SB("gsil", [128, T], BF16); usca = SB("usca", [128, T], BF16); htile = SB("htile", [128, T], BF16)
        wdintA = SB("wdintA", [128, 1024], I32); wdintB = SB("wdintB", [128, 1024], I32)
        hstrA = SB("hstrA", [128, 512], BF16); hstrB = SB("hstrB", [128, 512], BF16)
        xoldA = SB("xoldA", [128, 512], F32); xoldB = SB("xoldB", [128, 512], F32)
        xnewA = SB("xnewA", [128, 512], F32); xnewB = SB("xnewB", [128, 512], F32)
        ones1 = SB("ones1", [1, 128], BF16)
        onesc = SB("onesc", [128, 1], BF16)
        x2 = SB("x2", [128, 512], BF16)
        mrow = SB("mrow", [1, 512], F32); vrow = SB("vrow", [1, 512], F32); trow = SB("trow", [1, 512], F32)
        mbfr = SB("mbfr", [1, 512], BF16); rbfr = SB("rbfr", [1, 512], BF16)
        mbc = SB("mbc", [128, 512], BF16); rbc = SB("rbc", [128, 512], BF16)
        if ctx is None:
            psA = PS("psA", [128, 2048], F32); psB = PS("psB", [128, 2048], F32)
            _s_ld = SEM("s_ld"); _s_w = SEM("s_w"); _s_mm = SEM("s_mm")
            _s_epv = SEM("s_epv"); _s_eps = SEM("s_eps"); _s_st = SEM("s_st"); _s_h = SEM("s_h")
            _s_c = SEM("s_c"); _s_c1 = SEM("s_c1"); _s_hs = SEM("s_hs"); _s_xo = SEM("s_xo")
            _s_w1 = SEM("s_w1"); _s_h1 = SEM("s_h1"); _s_xo1 = SEM("s_xo1"); _s_st1 = SEM("s_st1")
            block = es.enter_context(nc.Block())
        if ctx is None:
            s_ld = Sem(_s_ld); s_w = Sem(_s_w); s_mm = Sem(_s_mm)
            s_ep = EpSem(_s_epv, _s_eps); s_st = Sem(_s_st); s_h = Sem(_s_h)
            s_hs = Sem(_s_hs)
            s_wp = [s_w, Sem(_s_w1)]
            s_hp = [s_h, Sem(_s_h1)]
            s_cp = [Sem(_s_c), Sem(_s_c1)]
            s_xop = [Sem(_s_xo), Sem(_s_xo1)]
            s_stp = [s_st, Sem(_s_st1)]
            s_g = Sem(es.enter_context(nc.semaphore("s_g")))
        else:
            block = ctx.block
            psA = ctx.psA; psB = ctx.psB
            s_ld = ctx.s_ld; s_mm = ctx.s_mm; s_ep = ctx.s_ep
            s_wp = ctx.s_wp; s_hp = ctx.s_hp; s_cp = ctx.s_cp
            s_xop = ctx.s_xop; s_stp = ctx.s_stp; s_hs = ctx.s_hs; s_g = ctx.s_g
        sems = dict(mm=s_mm, ep=s_ep)
        sc = dict(x2=x2, mrow=mrow, vrow=vrow, trow=trow, mbfr=mbfr, rbfr=rbfr, mbc=mbc, rbc=rbc)

        def ld0(g):
            for q in range(4):
                s_ld.dma(g.dma_start(out=xbf[:, 0:DC * T].rearrange("p (k t) -> p k t", k=DC)[:, 4 * q:4 * q + 4, :],
                                     in_=xv[:, 4 * q:4 * q + 4, :]))
            s_ld.dma(g.dma_start(out=gss[:], in_=gsc[:]))
            s_ld.dma(g.dma_start(out=uss[:], in_=usc[:]))
            s_ld.dma(g.dma_start(out=dss[:], in_=dsc[:]))
            g.memset(ones1[:], 1.0)
            s_g.tick(g.memset(onesc[:], 1.0))
        block.gpsimd(ld0)
        ld_done = s_ld.v
        g_done = s_g.v

        for sec in (block.tensor, block.vector, block.scalar):
            sec(lambda e, ld_done=ld_done, g_done=g_done: (
                e.wait_ge(s_ld.h, ld_done), e.wait_ge(s_g.h, g_done)))

        emit_layernorm(block, sems, xbf, psA, psB, ones1, onesc, sc)

        sb_gu = ScalBatch(block)
        wg = [wgA, wgB]
        wu = [wuA, wuB]
        mm_done = {}
        psum_free = s_ep.v
        gu_free = 0
        h_store = {}

        def wload(g, ic, need_mm):
            g.wait_ge(s_mm.h, need_mm)
            sw = s_wp[ic % 2]
            g.memset(wg[ic % 2][:], -128.0)
            g.memset(wu[ic % 2][:], -128.0)
            sw.dma(g.dma_start(out=wg[ic % 2][:].rearrange("p (k i) -> p k i", k=DC),
                               in_=gwv[:, :, ic * 128:(ic + 1) * 128], accum_op=ALU.add))
            sw.dma(g.dma_start(out=wu[ic % 2][:].rearrange("p (k i) -> p k i", k=DC),
                               in_=uwv[:, :, ic * 128:(ic + 1) * 128], accum_op=ALU.add))
            return sw.v

        w_arrv = {}
        block.gpsimd(lambda g: w_arrv.__setitem__(0, wload(g, 0, 0)))
        block.gpsimd(lambda g: w_arrv.__setitem__(1, wload(g, 1, 0)))

        for ic in range(ICn):
            buf = ic % 2

            def mmg(t, ic=ic, buf=buf, wt=w_arrv[ic], psum_free=psum_free):
                t.wait_ge(s_wp[buf].h, wt)
                s_ep.wait(t, psum_free)
                for k in range(DC):
                    for q in range(4):
                        mmi = t.matmul(psA[:, q * 512:(q + 1) * 512], wg[buf][:, k * 128:(k + 1) * 128],
                                       xbf[:, k * T + q * 512:k * T + (q + 1) * 512],
                                       start=(k == 0), stop=(k == DC - 1))
                s_mm.tick(mmi)
                for k in range(DC):
                    for q in range(4):
                        mmi = t.matmul(psB[:, q * 512:(q + 1) * 512], wu[buf][:, k * 128:(k + 1) * 128],
                                       xbf[:, k * T + q * 512:k * T + (q + 1) * 512],
                                       start=(k == 0), stop=(k == DC - 1))
                s_mm.tick(mmi)
            block.tensor(mmg)
            g_done, u_done = s_mm.v - 1, s_mm.v
            mm_done[ic] = s_mm.v

            if ic + 2 < ICn:
                block.gpsimd(lambda g, ic=ic: w_arrv.__setitem__(
                    ic + 2, wload(g, ic + 2, mm_done[ic])))

            def epi_s(s, ic=ic, g_done=g_done, gu_free=gu_free):
                s.wait_ge(s_mm.h, g_done)
                s_ep.wait(s, gu_free)
                s.activation(out=gsil[:], in_=psA[:, 0:2048], func=AF.Silu,
                             scale=gss[:, ic:ic + 1]).then_inc(s_ep.hs, 1)
            sb_gu.add(epi_s)
            s_ep.reserve_s(1)
            sil_done = s_ep.v

            def epi_v(v, ic=ic, sil_done=sil_done, u_done=u_done, h_st=h_store.get(ic - 1, 0)):
                s_ep.wait(v, sil_done)
                v.wait_ge(s_mm.h, u_done)
                v.wait_ge(s_hs.h, h_st)
                s_ep.tick(v.scalar_tensor_tensor(htile[:], psB[:, 0:2048], uss[:, ic:ic + 1],
                                                 gsil[:], ALU.mult, ALU.mult))
            block.vector(epi_v)
            psum_free = s_ep.v
            gu_free = s_ep.v
            h_done = s_ep.v

            def hstore(sy, ic=ic, h_done=h_done):
                s_ep.wait(sy, h_done)
                s_hs.dma(sy.dma_start(out=hv[:, ic, :], in_=htile[:]))
            block.sync(hstore)
            h_store[ic] = s_hs.v

        sb_gu.flush()
        gateup_st = s_hs.v
        down_psum_free = psum_free

        wdint = [wdintA, wdintB]
        hstr = [hstrA, hstrB]
        xold = [xoldA, xoldB]
        xnew = [xnewA, xnewB]
        conv_tick = {}
        dmm_tick = {}
        epi_tick = {}
        xst_tick = {}
        arena_free = s_mm.v

        for dmh in range(2):
            for icc in range(ICn):
                bufw = icc % 2

                def wdload(g, icc=icc, dmh=dmh, bufw=bufw, arena_free=arena_free):
                    g.wait_ge(s_mm.h, arena_free)
                    g.memset(arena[:, icc * 1024:(icc + 1) * 1024], -128.0)
                    s_cp[bufw].dma(g.dma_start(out=arena[:, icc * 1024:(icc + 1) * 1024],
                                               in_=dwv[:, icc, dmh * 1024:(dmh + 1) * 1024],
                                               accum_op=ALU.add))
                block.gpsimd(wdload)
            wd_done = (s_cp[0].v, s_cp[1].v)

            for tp in range(4):
                for icc in range(ICn):
                    bufh = icc % 2
                    h_free = dmm_tick.get((dmh, tp, icc - 2),
                                          dmm_tick.get((dmh, tp - 1, ICn - 2 + icc),
                                                       dmm_tick.get((dmh - 1, 3, ICn - 2 + icc), 0)))

                    def hload(g, icc=icc, tp=tp, bufh=bufh, h_free=h_free, need=gateup_st):
                        g.wait_ge(s_hs.h, need)
                        g.wait_ge(s_mm.h, h_free)
                        s_hp[bufh].dma(g.dma_start(out=hstr[bufh][:], in_=hv[:, icc, tp * 512:(tp + 1) * 512]))
                    block.gpsimd(hload)
                    h_arr = s_hp[bufh].v

                    dpsum_free = epi_tick.get((dmh, tp - 1), epi_tick.get((dmh - 1, 3), down_psum_free))

                    def dmm(t, icc=icc, bufh=bufh, h_arr=h_arr, wd_done=wd_done, dpsum_free=dpsum_free):
                        t.wait_ge(s_cp[0].h, wd_done[0])
                        t.wait_ge(s_cp[1].h, wd_done[1])
                        t.wait_ge(s_hp[bufh].h, h_arr)
                        if icc == 0:
                            s_ep.wait(t, dpsum_free)
                        for j in range(8):
                            ps = psA if j < 4 else psB
                            jj = j % 4
                            mmi = t.matmul(ps[:, jj * 512:(jj + 1) * 512],
                                           arena[:, icc * 1024 + j * 128:icc * 1024 + (j + 1) * 128],
                                           hstr[bufh][:], start=(icc == 0), stop=(icc == ICn - 1))
                        s_mm.tick(mmi)
                    block.tensor(dmm)
                    dmm_tick[(dmh, tp, icc)] = s_mm.v
                dmm_all = s_mm.v

                for j in range(8):
                    dmc = dmh * 8 + j
                    bufe = j % 2
                    xold_free = epi_tick.get((dmh, tp, j - 2), epi_tick.get((dmh, tp - 1, 6 + j),
                                             epi_tick.get((dmh - 1, 3, 6 + j), 0)))

                    def xload(sy, dmc=dmc, tp=tp, bufe=bufe, xold_free=xold_free):
                        s_ep.wait(sy, xold_free)
                        s_xop[bufe].dma(sy.dma_start(out=xold[bufe][:], in_=xv[:, dmc, tp * 512:(tp + 1) * 512]))
                    block.sync(xload)
                    x_arr = s_xop[bufe].v

                    xnew_free = xst_tick.get((dmh, tp, j - 2), xst_tick.get((dmh, tp - 1, 6 + j),
                                             xst_tick.get((dmh - 1, 3, 6 + j), 0)))

                    def epi(v, j=j, dmc=dmc, bufe=bufe, x_arr=x_arr, dmm_all=dmm_all, xnew_free=xnew_free):
                        v.wait_ge(s_mm.h, dmm_all)
                        v.wait_ge(s_xop[bufe].h, x_arr)
                        v.wait_ge(s_stp[bufe].h, xnew_free)
                        ps = psA if j < 4 else psB
                        jj = j % 4
                        s_ep.tick(v.scalar_tensor_tensor(
                            xnew[bufe][:], ps[:, jj * 512:(jj + 1) * 512], dss[:, dmc:dmc + 1],
                            xold[bufe][:], ALU.mult, ALU.add))
                    block.vector(epi)
                    epi_tick[(dmh, tp, j)] = s_ep.v
                    e_done = s_ep.v

                    def xstore(sy, dmc=dmc, tp=tp, bufe=bufe, e_done=e_done):
                        s_ep.wait(sy, e_done)
                        s_stp[bufe].dma(sy.dma_start(out=xov[:, dmc, tp * 512:(tp + 1) * 512], in_=xnew[bufe][:]))
                    block.sync(xstore)
                    xst_tick[(dmh, tp, j)] = s_stp[bufe].v
                epi_tick[(dmh, tp)] = s_ep.v
            arena_free = s_mm.v

        if ctx is None:
            f0, f1 = s_stp[0].v, s_stp[1].v

            def fin(g):
                g.wait_ge(s_stp[0].h, f0)
                g.wait_ge(s_stp[1].h, f1)
            block.gpsimd(fin)

    return nc



_PROGS = {}


def _prog(name, builder):
    if name not in _PROGS:
        _PROGS[name] = builder()
    return _PROGS[name]


def _ffn_weight_maps(pfx, inputs):
    gw = np.ascontiguousarray(np.asarray(inputs[pfx + "_gate_w"]).T)
    uw = np.ascontiguousarray(np.asarray(inputs[pfx + "_up_w"]).T)
    dw = np.ascontiguousarray(np.asarray(inputs[pfx + "_down_w"]).T)
    gsc = np.ascontiguousarray(np.asarray(inputs[pfx + "_gate_scale"]).reshape(ICn, 128).T)
    usc = np.ascontiguousarray(np.asarray(inputs[pfx + "_up_scale"]).reshape(ICn, 128).T)
    dsc = np.ascontiguousarray(np.asarray(inputs[pfx + "_down_scale"]).reshape(DC, 128).T)
    return dict(gw=gw, uw=uw, dw=dw, gsc=gsc, usc=usc, dsc=dsc)


def run_ffn(x_full, wmap):
    nc = _prog("ffn", build_ffn)
    flat = x_full.reshape(B * S, D)
    in_maps = []
    for c in range(NCORES):
        xc = np.ascontiguousarray(flat[c * T:(c + 1) * T].T)
        in_maps.append(dict(x=xc, **wmap))
    res = run_bass_kernel_spmd(nc, in_maps, core_ids=list(range(NCORES)))
    out = np.empty_like(flat)
    for c in range(NCORES):
        out[c * T:(c + 1) * T] = res.results[c]["xo"].T
    return out.reshape(B, S, D)


def kernel_unfused(**inputs):
    x = np.asarray(inputs["L0_attn_a_x"], dtype=np.float32)
    w_ffn0 = _ffn_weight_maps("L0_ffn", inputs)
    w_ffn1 = _ffn_weight_maps("L1_ffn", inputs)
    m_a0 = _attn_maps("a0", inputs)
    m_b0 = _attn_maps("b0", inputs)
    m_b1 = _attn_maps("b1", inputs)
    m_mb = _mamba_maps(inputs)
    x = run_attn("a0", x, m_a0)
    x = run_ffn(x, w_ffn0)
    x = run_attn("b0", x, m_b0)
    x = run_ffn(x, w_ffn0)
    x = run_mamba(x, m_mb)
    x = run_ffn(x, w_ffn1)
    x = run_attn("b1", x, m_b1)
    x = run_ffn(x, w_ffn1)
    return x



SC128 = float(1.0 / np.sqrt(128.0))

ATTN_CFG = {
    "a0": dict(EK=2048, EV=2048, NQH=16, NKV=16, HPG=1, power=False, bias=True, fused_qkv=True),
    "b0": dict(EK=512, EV=512, NQH=16, NKV=4, HPG=4, power=False, bias=False, fused_qkv=False),
    "b1": dict(EK=1024, EV=1024, NQH=16, NKV=8, HPG=2, power=True, bias=True, fused_qkv=False),
}


def build_attn(kind, ctx=None, pfx=""):
    cfg = ATTN_CFG[kind]
    EK, EV, NQH, NKV, HPG = cfg["EK"], cfg["EV"], cfg["NQH"], cfg["NKV"], cfg["HPG"]
    power, has_bias = cfg["power"], cfg["bias"]
    EKC = EK // 128
    T2 = 2 * T

    if ctx is None:
        nc = bass.Bass("TRN2", target_bir_lowering=False, debug=False)
        x = nc.dram_tensor("x", [D, T2], F32, kind="ExternalInput").ap()
        gates = nc.dram_tensor("gates", [128, 4], F32, kind="ExternalInput").ap()
        masks = nc.dram_tensor("masks", [128, 4 * 512], BF16, kind="ExternalInput").ap()
        ident = nc.dram_tensor("ident", [128, 128], BF16, kind="ExternalInput").ap()
        xo = nc.dram_tensor("xo", [D, T], F32, kind="ExternalOutput").ap()
        xv = x.rearrange("(k p) t -> p k t", p=128)
        xov = xo.rearrange("(k p) t -> p k t", p=128)

        def xsrc(sl, xcol0):
            return xv[:, sl, xcol0:xcol0 + T]

        def xown_cols(sl, c0, c1):
            return xv[:, sl, T + c0:T + c1]
    else:
        nc = ctx.nc
        gates = ctx.gates; masks = ctx.masks; ident = ctx.ident
        xv_own = ctx.x_own.rearrange("(k p) t -> p k t", p=128)
        xv_past = ctx.x_past.rearrange("(k p) t -> p k t", p=128)
        xov = xv_own

        def xsrc(sl, xcol0):
            return (xv_past if xcol0 == 0 else xv_own)[:, sl, :]

        def xown_cols(sl, c0, c1):
            return xv_own[:, sl, c0:c1]
    if kind == "a0":
        wqkv = nc.dram_tensor(pfx + "wq", [D, 6144], F32, kind="ExternalInput").ap()
        bqr = nc.dram_tensor(pfx + "bqr", [1, 6144], F32, kind="ExternalInput").ap()
        bqc = nc.dram_tensor(pfx + "bqc", [128, 48], F32, kind="ExternalInput").ap()
    elif kind == "b0":
        wqm = nc.dram_tensor(pfx + "wq", [D, 2048], F32, kind="ExternalInput").ap()
        wkv = nc.dram_tensor(pfx + "wkv", [D, 1024], F32, kind="ExternalInput").ap()
    else:
        wqm = nc.dram_tensor(pfx + "wq", [D, 2048], F32, kind="ExternalInput").ap()
        wk = nc.dram_tensor(pfx + "wk", [D, 1024], F32, kind="ExternalInput").ap()
        wvm = nc.dram_tensor(pfx + "wv", [D, 1024], F32, kind="ExternalInput").ap()
        wgg = nc.dram_tensor(pfx + "wgg", [D, 8], F32, kind="ExternalInput").ap()
    wo = nc.dram_tensor(pfx + "wo", [D, D], F32, kind="ExternalInput").ap()
    if has_bias:
        boc = nc.dram_tensor(pfx + "boc", [128, DC], F32, kind="ExternalInput").ap()

    q_hbm = nc.dram_tensor(pfx + "q_hbm", [D, T], BF16, kind="Internal").ap()
    k_hbm = nc.dram_tensor(pfx + "k_hbm", [EK, T2], BF16, kind="Internal").ap()
    v_hbm = nc.dram_tensor(pfx + "v_hbm", [T2, EV], BF16, kind="Internal").ap()
    qv = q_hbm.rearrange("(h p) t -> p h t", p=128)
    kv_ = k_hbm.rearrange("(h p) t -> p h t", p=128)
    vv = v_hbm.rearrange("(c p) e -> p c e", p=128)
    wov = wo.rearrange("(e p) d -> p e d", p=128)

    import contextlib
    es = contextlib.ExitStack()
    SB = lambda n, s, d: es.enter_context(nc.sbuf_tensor(pfx + n, s, d))
    PS = lambda n, s, d: es.enter_context(nc.psum_tensor(pfx + n, s, d))
    SEM = lambda n: es.enter_context(nc.semaphore(pfx + n))
    with es:
        big = SB("big", [128, 32768], BF16)
        wtA = SB("wtA", [128, DC * 128], BF16); wtB = SB("wtB", [128, DC * 128], BF16)
        vwA = SB("vwA", [128, DC * 512], BF16)
        ptA = SB("ptA", [128, T], BF16); ptB = SB("ptB", [128, T], BF16)
        vtA = SB("vtA", [128, 512], BF16); vtB = SB("vtB", [128, 512], BF16)
        khA = SB("khA", [128, T2], BF16); khB = SB("khB", [128, T2], BF16)
        vhA = SB("vhA", [128, 32 * 128], BF16); vhB = SB("vhB", [128, 32 * 128], BF16)
        qtA = SB("qtA", [128, 512], BF16); qtB = SB("qtB", [128, 512], BF16)
        pT0 = SB("pT0", [128, 512], BF16); pT1 = SB("pT1", [128, 512], BF16)
        msk = SB("msk", [128, 4 * 512], BF16)
        idn = SB("idn", [128, 128], BF16)
        gat = SB("gat", [128, 4], F32)
        woT = SB("woT", [128, DC * 128], BF16); woU = SB("woU", [128, DC * 128], BF16)
        xoldA = SB("xoldA", [128, T], F32)
        xnewA = SB("xnewA", [128, T], F32)
        ones1 = SB("ones1", [1, 128], BF16)
        onesc = SB("onesc", [128, 1], BF16)
        lrow = SB("lrow", [1, 512], F32)
        linv = SB("linv", [1, 512], F32)
        linb = SB("linb", [1, 512], BF16)
        lbc = SB("lbc", [128, 512], F32)
        x2 = SB("x2", [128, 512], BF16)
        mrow = SB("mrow", [1, 512], F32); vrow = SB("vrow", [1, 512], F32); trow = SB("trow", [1, 512], F32)
        mbfr = SB("mbfr", [1, 512], BF16); rbfr = SB("rbfr", [1, 512], BF16)
        mbc = SB("mbc", [128, 512], BF16); rbc = SB("rbc", [128, 512], BF16)
        if has_bias and kind == "a0":
            bqrs = SB("bqrs", [1, 6144], BF16)
            bqcs = SB("bqcs", [128, 48], F32)
        if has_bias:
            bocs = SB("bocs", [128, DC], F32)
        if power:
            g_sb = SB("g_sb", [8, T2], F32)
            ga = big[0:8, 0:2 * T2].bitcast(F32)
            gb2 = big[0:8, 2 * T2:4 * T2].bitcast(F32)
            gones = big[0:8, 4 * T2:6 * T2].bitcast(F32)
            a_bf = SB("a_bf", [8, T2], BF16)
            aT = SB("aT", [128, 8 * 32], F32)
        if ctx is None:
            psA = PS("psA", [128, 2048], F32)
            psB = PS("psB", [128, 2048], F32)
            s_ld = Sem(SEM("s_ld")); s_w = Sem(SEM("s_w")); s_mm = Sem(SEM("s_mm"))
            s_ep = EpSem(SEM("s_epv"), SEM("s_eps")); s_st = Sem(SEM("s_st")); s_h = Sem(SEM("s_h"))
            s_c = Sem(SEM("s_c"))
            s_g = Sem(SEM("s_g"))
            s_fs = Sem(SEM("s_fs"))
            s_vs = Sem(SEM("s_vs"))
            s_xo = Sem(SEM("s_xo"))
            s_wp = [s_w, Sem(SEM("s_w1"))]
            s_vw = Sem(SEM("s_vw"))
            s_wo = [Sem(SEM("s_wo0")), Sem(SEM("s_wo1"))]
            s_kv = [Sem(SEM("s_kv0")), Sem(SEM("s_kv1"))]
            s_q = [Sem(SEM("s_q0")), Sem(SEM("s_q1"))]
            block = es.enter_context(nc.Block())
        else:
            psA = ctx.psA; psB = ctx.psB; block = ctx.block
            s_ld = ctx.s_ld; s_w = ctx.s_w; s_mm = ctx.s_mm; s_ep = ctx.s_ep
            s_st = ctx.s_st; s_h = ctx.s_h; s_c = ctx.s_c; s_g = ctx.s_g
            s_fs = ctx.s_fs; s_vs = ctx.s_vs; s_xo = ctx.s_xo
            s_wp = ctx.s_wp; s_vw = ctx.s_vw; s_wo = ctx.s_wo
            s_kv = ctx.s_kv; s_q = ctx.s_q
        sems = dict(mm=s_mm, ep=s_ep)
        sc = dict(x2=x2, mrow=mrow, vrow=vrow, trow=trow, mbfr=mbfr, rbfr=rbfr, mbc=mbc, rbc=rbc)
        xbf = big

        def ld0(sy):
            s_ld.dma(sy.dma_start(out=msk[:], in_=masks[:]))
            s_ld.dma(sy.dma_start(out=idn[:], in_=ident[:]))
            s_ld.dma(sy.dma_start(out=gat[:], in_=gates[:]))
            if has_bias:
                s_ld.dma(sy.dma_start(out=bocs[:], in_=boc[:]))
            if has_bias and kind == "a0":
                s_ld.dma(sy.dma_start(out=bqcs[:], in_=bqc[:]))
        block.sync(ld0)

        def ld0g(g):
            if has_bias and kind == "a0":
                s_ld.dma(g.dma_start(out=bqrs[:], in_=bqr[:]))
            g.memset(ones1[:], 1.0)
            s_ld.tick(g.memset(onesc[:], 1.0))
        block.gpsimd(ld0g)
        consts_done = s_ld.v

        for sec in (block.tensor, block.vector, block.scalar):
            sec(lambda e: e.wait_ge(s_ld.h, consts_done))

        if kind == "a0":
            wqv = wqkv.rearrange("(k p) e -> p k e", p=128)
        else:
            wqv = wqm.rearrange("(k p) e -> p k e", p=128)
            if kind == "b0":
                wkvv = wkv.rearrange("(k p) e -> p k e", p=128)
            else:
                wkv_b1 = wk.rearrange("(k p) e -> p k e", p=128)
                wvv_b1 = wvm.rearrange("(k p) e -> p k e", p=128)
                wgv = wgg.rearrange("(k p) e -> p k e", p=128)

        wt = [wtA, wtB]
        vw = [vwA, vwA]
        pt = [ptA, ptB]
        vt = [vtA, vtB]
        fm_mm = {}
        fm_ep = {}
        fm_st = {}
        vt_mm = {}
        vt_ep = {}
        vt_st = {}
        jglob = [0]
        vglob = [0]
        half_mm_done = 0

        def fm_wload(g, wsrc, ec, j):
            g.wait_ge(s_mm.h, fm_mm.get(j - 2, 0))
            s_wp[j % 2].dma(g.dma_start(out=wt[j % 2][:].rearrange("p (k i) -> p k i", k=DC),
                                        in_=wsrc[:, :, ec * 128:(ec + 1) * 128]))

        def emit_fm_job(wsrc, ec, dst_view, dst_chunk, dst_col0, bias_ec, xln_ready):
            j = jglob[0]; jglob[0] += 1
            pp = j % 2
            ps = psA if pp == 0 else psB
            block.gpsimd(lambda g: fm_wload(g, wsrc, ec, j))
            w_arr = s_wp[j % 2].v

            def mmj(t, j=j, pp=pp, ps=ps, w_arr=w_arr, xln_ready=xln_ready):
                t.wait_ge(s_wp[j % 2].h, w_arr)
                s_ep.wait(t, vmax(fm_ep.get(j - 2, 0), xln_ready))
                for k in range(DC):
                    for q in range(4):
                        mmi = t.matmul(ps[:, q * 512:(q + 1) * 512], wt[pp][:, k * 128:(k + 1) * 128],
                                       xbf[:, k * T + q * 512:k * T + (q + 1) * 512],
                                       start=(k == 0), stop=(k == DC - 1))
                s_mm.tick(mmi)
            block.tensor(mmj)
            fm_mm[j] = s_mm.v
            mmv = s_mm.v

            def epij(s, j=j, pp=pp, ps=ps, bias_ec=bias_ec, mmv=mmv, fst=fm_st.get(j - 2, 0)):
                s.wait_ge(s_mm.h, mmv)
                s.wait_ge(s_fs.h, fst)
                if bias_ec is not None:
                    s.activation(out=pt[pp][:], in_=ps[:, 0:2048], func=AF.Identity,
                                 bias=bqcs[:, bias_ec:bias_ec + 1]).then_inc(s_ep.hs, 1)
                else:
                    s.activation(out=pt[pp][:], in_=ps[:, 0:2048],
                                 func=AF.Copy).then_inc(s_ep.hs, 1)
            sb_fm.add(epij)
            s_ep.reserve_s(1)
            fm_ep[j] = s_ep.v
            epv = s_ep.v

            def stj(sy, j=j, pp=pp, dst_view=dst_view, dst_chunk=dst_chunk, dst_col0=dst_col0, epv=epv):
                s_ep.wait(sy, epv)
                s_fs.dma(sy.dma_start(out=dst_view[:, dst_chunk, dst_col0:dst_col0 + T], in_=pt[pp][:]))
            block.sync(stj)
            fm_st[j] = s_fs.v

        def emit_tm_vjob(wsrc, src_col0, ncols, dst_row_chunk0, dst_col0, bias_row_col0, xln_ready):
            jv = vglob[0]; vglob[0] += 1
            pp = jv % 2
            ep_now = s_ep.v
            block.gpsimd(lambda g, jv=jv, pp=pp: (
                g.wait_ge(s_mm.h, vt_mm.get(jv - 1, 0)),
                s_vw.dma(g.dma_start(out=vw[pp][:, 0:DC * ncols].rearrange("p (k i) -> p k i", k=DC),
                                     in_=wsrc[:, :, src_col0:src_col0 + ncols]))))
            w_arr = s_vw.v
            for tc in range(16):
                def mmv(t, jv=jv, pp=pp, tc=tc, w_arr=w_arr, ncols=ncols,
                        bias_row_col0=bias_row_col0, xln_ready=xln_ready, ep_now=ep_now):
                    ps = psA if tc % 2 == 0 else psB
                    t.wait_ge(s_vw.h, w_arr)
                    s_ep.wait(t, vmax(vt_ep.get((jv, tc - 2), vt_ep.get((jv - 1, 14 + tc), 0)),
                                          xln_ready, ep_now if tc < 2 else 0))
                    has_b = bias_row_col0 is not None
                    for k in range(DC):
                        mmi = t.matmul(ps[:, 0:ncols], xbf[:, k * T + tc * 128:k * T + (tc + 1) * 128],
                                       vw[pp][:, k * ncols:(k + 1) * ncols],
                                       start=(k == 0), stop=(k == DC - 1 and not has_b))
                    if has_b:
                        mmi = t.matmul(ps[:, 0:ncols], ones1[:],
                                       bqrs[0:1, bias_row_col0:bias_row_col0 + ncols],
                                       start=False, stop=True)
                    s_mm.tick(mmi)
                block.tensor(mmv)
                vt_mm[jv] = s_mm.v
                mmvv = s_mm.v

                def epiv(v, jv=jv, tc=tc, ncols=ncols, mmvv=mmvv):
                    ps = psA if tc % 2 == 0 else psB
                    pe = (jv * 16 + tc) % 2
                    v.wait_ge(s_mm.h, mmvv)
                    v.wait_ge(s_vs.h, vt_st.get((jv, tc - 2), vt_st.get((jv - 1, 14 + tc), 0)))
                    s_ep.tick(v.tensor_copy(vt[pe][:, 0:ncols], ps[:, 0:ncols]))
                block.vector(epiv)
                vt_ep[(jv, tc)] = s_ep.v
                epvv = s_ep.v

                def stv(sy, jv=jv, tc=tc, ncols=ncols, dst_row_chunk0=dst_row_chunk0,
                        dst_col0=dst_col0, epvv=epvv):
                    pe = (jv * 16 + tc) % 2
                    s_ep.wait(sy, epvv)
                    s_vs.dma(sy.dma_start(out=vv[:, dst_row_chunk0 + tc, dst_col0:dst_col0 + ncols],
                                          in_=vt[pe][:, 0:ncols]))
                block.sync(stv)
                vt_st[(jv, tc)] = s_vs.v

        sb_fm = ScalBatch(block)
        power_g_ep = {}
        for half in range(2):
            xcol0 = half * T
            xgate = s_mm.v

            def xload(g, xcol0=xcol0, xgate=xgate):
                g.wait_ge(s_mm.h, xgate)
                for qq in range(4):
                    s_ld.dma(g.dma_start(
                        out=xbf[:, 0:DC * T].rearrange("p (k t) -> p k t", k=DC)[:, 4 * qq:4 * qq + 4, :],
                        in_=xsrc(slice(4 * qq, 4 * qq + 4), xcol0)))
            block.gpsimd(xload)
            xld = s_ld.v
            for sec in (block.tensor, block.vector, block.scalar):
                sec(lambda e, xld=xld: e.wait_ge(s_ld.h, xld))
            emit_layernorm(block, sems, xbf, psA, psB, ones1, onesc, sc)
            xln_ready = s_ep.v

            if kind == "a0":
                if half == 1:
                    for ec in range(16):
                        emit_fm_job(wqv, ec, qv, ec, 0, ec, xln_ready)
                for ec in range(16):
                    emit_fm_job(wqv, 16 + ec, kv_, ec, xcol0, 16 + ec, xln_ready)
                for gvi in range(4):
                    emit_tm_vjob(wqv, 4096 + gvi * 512, 512, 16 * half, gvi * 512, 4096 + gvi * 512, xln_ready)
            elif kind == "b0":
                if half == 1:
                    for ec in range(16):
                        emit_fm_job(wqv, ec, qv, ec, 0, None, xln_ready)
                for ec in range(4):
                    emit_fm_job(wkvv, ec, kv_, ec, xcol0, None, xln_ready)
                emit_tm_vjob(wkvv, 512, 512, 16 * half, 0, None, xln_ready)
            else:
                if half == 1:
                    for ec in range(16):
                        emit_fm_job(wqv, ec, qv, ec, 0, None, xln_ready)
                for ec in range(8):
                    emit_fm_job(wkv_b1, ec, kv_, ec, xcol0, None, xln_ready)
                for gvi in range(2):
                    emit_tm_vjob(wvv_b1, gvi * 512, 512, 16 * half, gvi * 512, None, xln_ready)
                block.gpsimd(lambda g: (
                    g.wait_ge(s_mm.h, fm_mm.get(jglob[0] - 1, 0)),
                    s_vw.dma(g.dma_start(out=wt[0][:, 0:DC * 8].rearrange("p (k i) -> p k i", k=DC),
                                         in_=wgv[:, :, 0:8]))))
                gw_arr = s_vw.v

                for tgc in range(4):
                    def gmm(t, tgc=tgc, gw_arr=gw_arr, xln_ready=xln_ready):
                        t.wait_ge(s_vw.h, gw_arr)
                        s_ep.wait(t, vmax(power_g_ep.get((half, tgc - 1), power_g_ep.get((half - 1, 3), 0)), xln_ready))
                        for k in range(DC):
                            mmi = t.matmul(psB[0:8, 1024:1536], wt[0][:, k * 8:(k + 1) * 8],
                                           xbf[:, k * T + tgc * 512:k * T + (tgc + 1) * 512],
                                           start=(k == 0), stop=(k == DC - 1))
                        s_mm.tick(mmi)
                    block.tensor(gmm)
                    gm = s_mm.v

                    def gev(v, half=half, tgc=tgc, gm=gm):
                        v.wait_ge(s_mm.h, gm)
                        s_ep.tick(v.tensor_copy(g_sb[0:8, half * T + tgc * 512:half * T + (tgc + 1) * 512],
                                                psB[0:8, 1024:1536]))
                    block.vector(gev)
                    power_g_ep[(half, tgc)] = s_ep.v
            half_mm_done = s_mm.v
            sb_fm.flush()

        proj_done_mm = s_mm.v
        proj_fs = s_fs.v
        proj_vs = s_vs.v
        proj_done_ep = s_ep.v

        p2_ep = 0
        if power:
            GO = 6.906768

            def p2v(v):
                s_ep.wait(v, proj_done_ep)
                v.wait_ge(s_mm.h, proj_done_mm)
                v.memset(gones[:, :], 1.0)
                v.tensor_scalar_mul(g_sb[0:8, 0:T], g_sb[0:8, 0:T], gat[0:8, 0:1])
                s_ep.tick(v.tensor_scalar(ga[:, :], g_sb[0:8, :], -1.0, -GO, ALU.mult, ALU.add))
            block.vector(p2v)
            t_done = s_ep.v

            def p2s(s, t_done=t_done):
                s_ep.wait(s, t_done)
                s.activation(out=gb2[:, :], in_=ga[:, :], func=AF.Abs)
                s.activation(out=gb2[:, :], in_=gb2[:, :], func=AF.Exp, scale=-1.0)
                s_ep.tick(s.activation(out=gb2[:, :], in_=gb2[:, :], func=AF.Ln, bias=1.0))
            block.scalar(p2s)
            sp_done = s_ep.v

            def p2v2(v, sp_done=sp_done):
                s_ep.wait(v, sp_done)
                v.tensor_relu(ga[:, :], ga[:, :])
                v.tensor_add(ga[:, :], ga[:, :], gb2[:, :])
                v.tensor_scalar_mul(ga[:, :], ga[:, :], -1.0)
                s_ep.tick(v.tensor_tensor_scan(g_sb[0:8, :], gones[:, :], ga[:, :], 0.0,
                                               ALU.mult, ALU.add))
            block.vector(p2v2)
            cs_done = s_ep.v

            def p2s2(s, cs_done=cs_done):
                s_ep.wait(s, cs_done)
                s_ep.tick(s.activation(out=ga[:, :], in_=g_sb[0:8, :], func=AF.Exp, scale=-1.0,
                                       bias=g_sb[0:8, T2 - 1:T2]))
            block.scalar(p2s2)
            a_done = s_ep.v

            def p2v3(v, a_done=a_done):
                s_ep.wait(v, a_done)
                v.tensor_scalar_mul(ga[:, 0:T], ga[:, 0:T], gat[0:8, 0:1])
                s_ep.tick(v.tensor_copy(a_bf[0:8, :], ga[:, :]))
            block.vector(p2v3)
            abf_done = s_ep.v

            pstr = [psA[:, 1536:1540].bitcast(BF16), psB[:, 1536:1540].bitcast(BF16)]
            trc = {}
            for kc in range(32):
                def ptr2(t, kc=kc, abf_done=abf_done):
                    s_ep.wait(t, abf_done if kc < 2 else trc[kc - 2])
                    s_mm.tick(t.transpose(pstr[kc % 2], a_bf[0:8, kc * 128:(kc + 1) * 128], idn[0:8, 0:8]))
                block.tensor(ptr2)
                trm = s_mm.v

                def ptrc(v, kc=kc, trm=trm):
                    v.wait_ge(s_mm.h, trm)
                    s_ep.tick(v.tensor_copy(aT[:, kc * 8:(kc + 1) * 8], pstr[kc % 2]))
                block.vector(ptrc)
                trc[kc] = s_ep.v
            p2_ep = s_ep.v

        kh = [khA, khB]
        vh = [vhA, vhB]
        qt = [qtA, qtB]
        pT = [pT0, pT1]
        sc_scale = SC128
        st_ps = [psA[:, 0:512], psA[:, 512:1024]]
        po_ps = psA[:, 1024:1536]
        sum_ps = psA[0:1, 1536:2048]
        eps_ps = psB[0:1, 1024:1536]
        pb_ps = psB[:, 0:512]

        sb_p3 = ScalBatch(block)
        kv_arr = {}
        kv_mm = {}
        st_free = {}
        pt_free = {}
        q_arr = {}
        norm_ep = {}
        norm_eps_free = [0]

        attn_fs = proj_fs
        attn_vs = proj_vs
        p3gate_ep = vmax(proj_done_ep, p2_ep)

        for kvg in range(NKV):
            def kvload(sy, kvg=kvg):
                sy.wait_ge(s_fs.h, attn_fs)
                sy.wait_ge(s_vs.h, attn_vs)
                sy.wait_ge(s_mm.h, kv_mm.get(kvg - 2, 0))
                skv = s_kv[kvg % 2]
                skv.dma(sy.dma_start(out=kh[kvg % 2][:], in_=kv_[:, kvg, :]))
                for qq in range(4):
                    skv.dma(sy.dma_start(
                        out=vh[kvg % 2][:].rearrange("p (c e) -> p c e", e=128)[:, 8 * qq:8 * qq + 8, :],
                        in_=vv[:, 8 * qq:8 * qq + 8, kvg * 128:(kvg + 1) * 128]))
            block.sync(kvload)
            kv_arr[kvg] = s_kv[kvg % 2].v

            for qh in range(kvg * HPG, (kvg + 1) * HPG):
                for tg in range(4):
                    def qload(sy, qh=qh, tg=tg):
                        sy.wait_ge(s_fs.h, attn_fs)
                        sy.wait_ge(s_mm.h, q_arr.get((qh, tg - 2), q_arr.get((qh - 1, 2 + tg), 0)))
                        s_q[tg % 2].dma(sy.dma_start(out=qt[tg % 2][:], in_=qv[:, qh, tg * 512:(tg + 1) * 512]))
                    block.sync(qload)
                    q_here = s_q[tg % 2].v

                    nown = 4 * (tg + 1)
                    kcs = list(range(16 + nown))
                    po_gate = norm_ep.get((qh, tg - 1), norm_ep.get((qh - 1, 3), 0))
                    for i, kc in enumerate(kcs):
                        pp = i % 2
                        stf = st_free.get(pp, 0)

                        def smm(t, i=i, kc=kc, pp=pp, kvg=kvg, tg=tg, q_here=q_here,
                                po_gate=po_gate, stf=stf):
                            if i == 0:
                                t.wait_ge(s_kv[kvg % 2].h, kv_arr[kvg])
                                t.wait_ge(s_q[tg % 2].h, q_here)
                                s_ep.wait(t, vmax(stf, p3gate_ep, po_gate))
                            else:
                                s_ep.wait(t, stf)
                            s_mm.tick(t.matmul(st_ps[pp], kh[kvg % 2][:, kc * 128:(kc + 1) * 128],
                                               qt[tg % 2][:], start=True, stop=True))
                        block.tensor(smm)
                        st_here = s_mm.v
                        ptf_gate = pt_free.get(pp, 0)

                        def sexp(s, i=i, kc=kc, pp=pp, st_here=st_here, ptf_gate=ptf_gate):
                            s.wait_ge(s_mm.h, max(st_here, ptf_gate))
                            if power:
                                s.activation(out=pT[pp][:], in_=st_ps[pp], func=AF.Square,
                                             scale=sc_scale).then_inc(s_ep.hs, 1)
                            elif kc < 16:
                                s.activation(out=pT[pp][:], in_=st_ps[pp], func=AF.Exp,
                                             scale=sc_scale, bias=gat[:, 1:2]).then_inc(s_ep.hs, 1)
                            else:
                                s.activation(out=pT[pp][:], in_=st_ps[pp], func=AF.Exp,
                                             scale=sc_scale).then_inc(s_ep.hs, 1)
                        sb_p3.add(sexp)
                        s_ep.reserve_s(1)
                        st_free[pp] = s_ep.v
                        exp_here = s_ep.v

                        jdiag = kc - 16 - 4 * tg
                        need_mask = 0 <= jdiag < 4
                        need_pow = power
                        if need_mask or need_pow:
                            def ptf(v, pp=pp, kc=kc, qh=qh, jdiag=jdiag, exp_here=exp_here,
                                    need_mask=need_mask, need_pow=need_pow):
                                s_ep.wait(v, exp_here)
                                last = None
                                if need_pow:
                                    last = v.tensor_scalar_mul(pT[pp][:], pT[pp][:],
                                                               aT[:, kc * 8 + qh // HPG:kc * 8 + qh // HPG + 1])
                                if need_mask:
                                    last = v.tensor_mul(pT[pp][:], pT[pp][:],
                                                        msk[:, jdiag * 512:(jdiag + 1) * 512])
                                s_ep.tick(last)
                            block.vector(ptf)
                        pt_ready = s_ep.v

                        def avmm(t, i=i, kc=kc, pp=pp, kvg=kvg, nkc=len(kcs), pt_ready=pt_ready):
                            s_ep.wait(t, pt_ready)
                            t.matmul(po_ps, vh[kvg % 2][:, kc * 128:(kc + 1) * 128], pT[pp][:],
                                     start=(i == 0), stop=(i == nkc - 1))
                            s_mm.tick(t.matmul(sum_ps, onesc[:], pT[pp][:],
                                               start=(i == 0), stop=(i == nkc - 1)))
                        block.tensor(avmm)
                        pt_free[pp] = s_mm.v
                    sb_p3.flush()
                    kv_mm[kvg] = s_mm.v
                    q_arr[(qh, tg)] = s_mm.v
                    av_done = s_mm.v

                    if power:
                        def emm(t, qh=qh, tg=tg, nef=norm_eps_free[0]):
                            s_ep.wait(t, nef)
                            s_mm.tick(t.matmul(eps_ps, idn[0:8, (qh // HPG):(qh // HPG) + 1],
                                               a_bf[0:8, T + tg * 512:T + (tg + 1) * 512],
                                               start=True, stop=True))
                        block.tensor(emm)
                        eps_here = s_mm.v

                    def nrm(v, qh=qh, tg=tg, av_done=av_done):
                        v.wait_ge(s_mm.h, av_done)
                        if power:
                            v.wait_ge(s_mm.h, eps_here)
                            v.tensor_scalar_mul(lrow[:], eps_ps, 1e-6)
                            v.tensor_add(lrow[:], lrow[:], sum_ps)
                        else:
                            v.tensor_copy(lrow[:], sum_ps)
                        v.reciprocal(linv[:], lrow[:])
                        s_ep.tick(v.tensor_copy(linb[:], linv[:]))
                    block.vector(nrm)
                    if power:
                        norm_eps_free[0] = s_ep.v
                    li_done = s_ep.v

                    def nbc(t, li_done=li_done):
                        s_ep.wait(t, li_done)
                        s_mm.tick(t.matmul(pb_ps, ones1[:], linb[:], start=True, stop=True))
                    block.tensor(nbc)
                    bc_done = s_mm.v

                    def nctx(v, qh=qh, tg=tg, bc_done=bc_done):
                        v.wait_ge(s_mm.h, bc_done)
                        v.tensor_copy(lbc[:], pb_ps)
                        s_ep.tick(v.tensor_mul(big[:, qh * 2048 + tg * 512:qh * 2048 + (tg + 1) * 512],
                                               po_ps, lbc[:]))
                    block.vector(nctx)
                    norm_ep[(qh, tg)] = s_ep.v

        attn_done_ep = s_ep.v

        woB = [woT, woU]
        xold = [xoldA, xoldA]
        xnew = [xnewA, xnewA]
        od_mm = {}
        od_ep = {}
        od_st = {}
        for dm in range(DC):
            def woload(g, dm=dm):
                g.wait_ge(s_mm.h, od_mm.get(dm - 2, 0))
                s_wo[dm % 2].dma(g.dma_start(out=woB[dm % 2][:].rearrange("p (e i) -> p e i", e=DC),
                                             in_=wov[:, :, dm * 128:(dm + 1) * 128]))
            block.gpsimd(woload)
            wo_arr = s_wo[dm % 2].v

            def xoload(sy, dm=dm):
                s_ep.wait(sy, od_ep.get(dm - 1, 0))
                s_xo.dma(sy.dma_start(out=xold[dm % 2][:], in_=xown_cols(dm, 0, T)))
            block.sync(xoload)
            xo_arr = s_xo.v

            def omm(t, dm=dm, wo_arr=wo_arr, attn_done_ep=attn_done_ep):
                t.wait_ge(s_wo[dm % 2].h, wo_arr)
                s_ep.wait(t, vmax(attn_done_ep, od_ep.get(dm - 2, 0)))
                ps = psA if dm % 2 == 0 else psB
                for ec in range(DC):
                    for tgq in range(4):
                        mmi = t.matmul(ps[:, tgq * 512:(tgq + 1) * 512], woB[dm % 2][:, ec * 128:(ec + 1) * 128],
                                       big[:, ec * 2048 + tgq * 512:ec * 2048 + (tgq + 1) * 512],
                                       start=(ec == 0), stop=(ec == DC - 1))
                s_mm.tick(mmi)
            block.tensor(omm)
            od_mm[dm] = s_mm.v
            om = s_mm.v

            def oepi(v, dm=dm, om=om, xo_arr=xo_arr):
                v.wait_ge(s_mm.h, om)
                v.wait_ge(s_xo.h, xo_arr)
                v.wait_ge(s_st.h, od_st.get(dm - 1, 0))
                ps = psA if dm % 2 == 0 else psB
                if has_bias:
                    s_ep.tick(v.scalar_tensor_tensor(xnew[dm % 2][:], ps[:, 0:2048],
                                                     bocs[:, dm:dm + 1], xold[dm % 2][:],
                                                     ALU.add, ALU.add))
                else:
                    s_ep.tick(v.scalar_tensor_tensor(xnew[dm % 2][:], ps[:, 0:2048],
                                                     1.0, xold[dm % 2][:], ALU.mult, ALU.add))
            block.vector(oepi)
            od_ep[dm] = s_ep.v
            oe = s_ep.v

            def ost(sy, dm=dm, oe=oe):
                s_ep.wait(sy, oe)
                s_st.dma(sy.dma_start(out=xov[:, dm, :], in_=xnew[dm % 2][:]))
            block.sync(ost)
            od_st[dm] = s_st.v

        if ctx is None:
            final_st = s_st.v

            def fin(g):
                g.wait_ge(s_st.h, final_st)
            block.gpsimd(fin)

    return nc


def _attn_maps(kind, inputs):
    m = {}
    if kind == "a0":
        m["wq"] = np.ascontiguousarray(np.asarray(inputs["L0_attn_a_W_qkv"], dtype=np.float32))
        b = np.asarray(inputs["L0_attn_a_b_qkv"], dtype=np.float32)
        m["bqr"] = np.ascontiguousarray(b.reshape(1, 6144))
        m["bqc"] = np.ascontiguousarray(b.reshape(48, 128).T)
        m["wo"] = np.ascontiguousarray(np.asarray(inputs["L0_attn_a_W_out"], dtype=np.float32))
        m["boc"] = np.ascontiguousarray(np.asarray(inputs["L0_attn_a_b_out"], dtype=np.float32).reshape(16, 128).T)
    elif kind == "b0":
        m["wq"] = np.ascontiguousarray(np.asarray(inputs["L0_attn_b_w_q"], dtype=np.float32))
        m["wkv"] = np.ascontiguousarray(np.asarray(inputs["L0_attn_b_w_kv"], dtype=np.float32))
        m["wo"] = np.ascontiguousarray(np.asarray(inputs["L0_attn_b_w_dense"], dtype=np.float32))
    else:
        m["wq"] = np.ascontiguousarray(np.asarray(inputs["L1_attn_b_Wq"], dtype=np.float32))
        m["wk"] = np.ascontiguousarray(np.asarray(inputs["L1_attn_b_Wk"], dtype=np.float32))
        m["wv"] = np.ascontiguousarray(np.asarray(inputs["L1_attn_b_Wv"], dtype=np.float32))
        m["wgg"] = np.ascontiguousarray(np.asarray(inputs["L1_attn_b_Wg"], dtype=np.float32))
        m["wo"] = np.ascontiguousarray(np.asarray(inputs["L1_attn_b_Wc"], dtype=np.float32))
        m["boc"] = np.ascontiguousarray(np.asarray(inputs["L1_attn_b_bc"], dtype=np.float32).reshape(16, 128).T)
    jm = np.arange(512)[None, :] >= (np.arange(128)[:, None] + np.arange(4)[:, None, None] * 128)
    m["masks"] = np.ascontiguousarray(jm.transpose(1, 0, 2).reshape(128, 4 * 512).astype(bfdt))
    m["ident"] = np.eye(128, dtype=bfdt)
    return m


def run_attn(kind, x_full, wmap):
    nc = _prog("attn_" + kind, lambda: build_attn(kind))
    in_maps = []
    for c in range(NCORES):
        b, h = c // 2, c % 2
        own = x_full[b, h * T:(h + 1) * T]
        other = x_full[b, (1 - h) * T:(2 - h) * T]
        xc = np.ascontiguousarray(np.concatenate([other.T, own.T], axis=1))
        gates = np.zeros((128, 4), dtype=np.float32)
        gates[:, 0] = float(h)
        gates[:, 1] = 0.0 if h == 1 else -60.0
        in_maps.append(dict(x=xc, gates=gates, **wmap))
    res = run_bass_kernel_spmd(nc, in_maps, core_ids=list(range(NCORES)))
    out = x_full.copy()
    for c in range(NCORES):
        b, h = c // 2, c % 2
        out[b, h * T:(h + 1) * T] = res.results[c]["xo"].T
    return out



MI = 4096
MN = 128
MNH = 64
MP = 64
MCH = 256
NCH = 16
HPC = 32


def build_mamba(phases=99, dump=None, ctx=None, pfx=""):
    if ctx is None:
        nc = bass.Bass("TRN2", target_bir_lowering=False, debug=False)
        x = nc.dram_tensor("x", [D, 2 * T], F32, kind="ExternalInput").ap()
        gates = nc.dram_tensor("gates", [128, 4], F32, kind="ExternalInput").ap()
        masks = nc.dram_tensor("masks", [128, 4 * 512], BF16, kind="ExternalInput").ap()
        ident = nc.dram_tensor("ident", [128, 128], BF16, kind="ExternalInput").ap()
        identf = nc.dram_tensor("identf", [128, 128], F32, kind="ExternalInput").ap()
        xo = nc.dram_tensor("xo", [D, T], F32, kind="ExternalOutput").ap()
        xv = x.rearrange("(k p) t -> p k t", p=128)
        xov = xo.rearrange("(k p) t -> p k t", p=128)

        def xsrc(sl, xcol0):
            return xv[:, sl, xcol0:xcol0 + T]

        def xown_cols(sl, c0, c1):
            return xv[:, sl, T + c0:T + c1]
    else:
        nc = ctx.nc
        gates = ctx.gates; masks = ctx.masks; ident = ctx.ident; identf = ctx.identf
        xv_own = ctx.x_own.rearrange("(k p) t -> p k t", p=128)
        xv_past = ctx.x_past.rearrange("(k p) t -> p k t", p=128)
        xov = xv_own

        def xsrc(sl, xcol0):
            return (xv_past if xcol0 == 0 else xv_own)[:, sl, :]

        def xown_cols(sl, c0, c1):
            return xv_own[:, sl, c0:c1]
    winT = nc.dram_tensor(pfx + "winT", [D, 8512], F32, kind="ExternalInput").ap()
    woutT = nc.dram_tensor(pfx + "woutT", [MI, D], F32, kind="ExternalInput").ap()
    convw = nc.dram_tensor(pfx + "convw", [128, 34 * 4], F32, kind="ExternalInput").ap()
    convb = nc.dram_tensor(pfx + "convb", [128, 34], F32, kind="ExternalInput").ap()
    dtb = nc.dram_tensor(pfx + "dtb", [64, 1], F32, kind="ExternalInput").ap()
    alog = nc.dram_tensor(pfx + "alog", [64, 1], F32, kind="ExternalInput").ap()
    dcolv = nc.dram_tensor(pfx + "dcol", [128, HPC], F32, kind="ExternalInput").ap()
    nwv = nc.dram_tensor(pfx + "nw", [128, HPC], F32, kind="ExternalInput").ap()

    dbg = None
    if dump == "xbc":
        dbg = nc.dram_tensor("dbg", [4352, 2 * T], BF16, kind="ExternalOutput").ap()
    elif dump == "x4f":
        dbg = nc.dram_tensor("dbg", [MI, T], BF16, kind="ExternalOutput").ap()
    elif dump == "yn":
        dbg = nc.dram_tensor("dbg", [MI, T], BF16, kind="ExternalOutput").ap()
    elif dump == "dt":
        dbg = nc.dram_tensor("dbg", [64, 2 * T], F32, kind="ExternalOutput").ap()
    elif dump == "acs":
        dbg = nc.dram_tensor("dbg", [64, 2 * T], F32, kind="ExternalOutput").ap()
    elif dump == "y":
        dbg = nc.dram_tensor("dbg", [MI, T], BF16, kind="ExternalOutput").ap()
    xbc_hbm = nc.dram_tensor(pfx + "xbc_hbm", [4352, 2 * T], BF16, kind="Internal").ap()
    z_hbm = nc.dram_tensor(pfx + "z_hbm", [MI, T], BF16, kind="Internal").ap()
    x4t_hbm = nc.dram_tensor(pfx + "x4t_hbm", [2 * T, MI], BF16, kind="Internal").ap()
    x4f_hbm = nc.dram_tensor(pfx + "x4f_hbm", [MI, T], BF16, kind="Internal").ap()
    yn_hbm = nc.dram_tensor(pfx + "yn_hbm", [MI, T], BF16, kind="Internal").ap()

    winv = winT.rearrange("(k p) e -> p k e", p=128)
    woutv = woutT.rearrange("(c p) d -> p c d", p=128)
    xbcv = xbc_hbm.rearrange("(c p) t -> p c t", p=128)
    zv = z_hbm.rearrange("(c p) t -> p c t", p=128)
    x4tv = x4t_hbm.rearrange("(tc p) hp -> p tc hp", p=128)
    x4fv = x4f_hbm.rearrange("(c p) t -> p c t", p=128)
    ynv = yn_hbm.rearrange("(c p) t -> p c t", p=128)

    import contextlib
    es = contextlib.ExitStack()
    SB = lambda n, s, d: es.enter_context(nc.sbuf_tensor(pfx + n, s, d))
    PS = lambda n, s, d: es.enter_context(nc.psum_tensor(pfx + n, s, d))
    SEM = lambda n: es.enter_context(nc.semaphore(pfx + n))
    with es:
        big = SB("big", [128, 32768], BF16)
        Ysb = SB("Ysb", [128, HPC * MCH], BF16)
        x4ft = SB("x4ft", [128, HPC * MCH], BF16)
        zsil = big[:, 24576:32768]
        wtA = x4ft[:, 0:DC * 128]; wtB = x4ft[:, DC * 128:2 * DC * 128]
        ptA = Ysb[:, 0:T]; ptB = Ysb[:, T:2 * T]
        dt_sb = SB("dt_sb", [64, 2 * T], BF16)
        acs = SB("acs", [64, 2 * T], F32)
        acs_bf = SB("acs_bf", [64, 2 * T], BF16)
        acsT = SB("acsT", [128, 32 * 64], F32)
        lam_bf = SB("lam_bf", [64, NCH], BF16)
        lamhp = SB("lamhp", [128, HPC * NCH], F32)
        Bfm = SB("Bfm", [128, 2 * T], BF16)
        Cfm = SB("Cfm", [128, 2 * T], BF16)
        Sfp = SB("Sfp", [128, HPC * 128], F32)
        Sin = SB("Sin", [128, HPC * 128], BF16)
        slbf = SB("slbf", [128, MI], BF16)
        xpad = Ysb[:, 4093:8192]
        cvb = slbf
        x4s = Sin
        convws = SB("convws", [128, 34 * 4], F32)
        convbs = SB("convbs", [128, 34], F32)
        dtbs = SB("dtbs", [64, 1], F32)
        alogs = SB("alogs", [64, 1], F32)
        acol = SB("acol", [64, 1], F32)
        dcols = SB("dcols", [128, HPC], F32)
        nws = SB("nws", [128, HPC], F32)
        gat = SB("gat", [128, 4], F32)
        msk = SB("msk", [128, 4 * 512], BF16)
        idn = SB("idn", [128, 128], BF16)
        idnf = SB("idnf", [128, 128], F32)
        onesf = SB("onesf", [1, 128], F32)
        erowf = SB("erowf", [1, MCH], F32)
        w2 = SB("w2", [64, MCH], F32)
        w2b = SB("w2b", [64, MCH], BF16)
        wT1 = SB("wT1", [128, 128], F32)
        wT2 = SB("wT2", [128, 128], F32)
        eacs = SB("eacs", [64, MCH], BF16)
        erow = SB("erow", [1, MCH], BF16)
        GTsb = SB("GTsb", [128, 2 * MCH], BF16)
        wtsb = SB("wtsb", [128, MCH], BF16)
        decL = SB("decL", [128, MCH], BF16)
        diffs = SB("diffs", [128, MCH], F32)
        zeros2 = SB("zeros2", [128, MCH], F32)
        tmpf = SB("tmpf", [128, MCH], F32)
        Btm = SB("Btm", [128, 2 * 128], BF16)
        hstrA = x4ft[:, 4096:4608]; hstrB = x4ft[:, 4608:5120]
        xoldA = x4ft[:, 5120:6144].bitcast(F32)
        xnewA = x4ft[:, 6144:7168].bitcast(F32)
        ones1 = SB("ones1", [1, 128], BF16)
        onesc = SB("onesc", [128, 1], BF16)
        x2 = x4ft[:, 4096:4608]
        mbc = Sin[:, 0:512]; rbc = Sin[:, 512:1024]
        mrow = Sin[0:1, 1024:2048].bitcast(F32)
        vrow = Sin[0:1, 2048:3072].bitcast(F32)
        trow = Sin[0:1, 3072:4096].bitcast(F32)
        mbfr = slbf[0:1, 0:512]; rbfr = slbf[0:1, 512:1024]
        rrow = x4ft[0:1, 4608:5120].bitcast(F32)
        if ctx is None:
            psA = PS("psA", [128, 2048], F32)
            psB = PS("psB", [128, 2048], F32)
            s_ld = Sem(SEM("s_ld")); s_w = Sem(SEM("s_w")); s_mm = Sem(SEM("s_mm"))
            s_ep = EpSem(SEM("s_epv"), SEM("s_eps")); s_st = Sem(SEM("s_st")); s_h = Sem(SEM("s_h"))
            s_c = Sem(SEM("s_c"))
            s_g = Sem(SEM("s_g"))
            s_fs = Sem(SEM("s_fs"))
            s_x4f = Sem(SEM("s_x4f"))
            s_x4t = Sem(SEM("s_x4t"))
            s_yn = Sem(SEM("s_yn"))
            s_xo = Sem(SEM("s_xo"))
            s_cl = Sem(SEM("s_cl"))
            s_x4l = Sem(SEM("s_x4l"))
            s_zx = Sem(SEM("s_zx"))
            s_hp = [Sem(SEM("s_hl0")), Sem(SEM("s_hl1"))]
            block = es.enter_context(nc.Block())
        else:
            psA = ctx.psA; psB = ctx.psB; block = ctx.block
            s_ld = ctx.s_ld; s_w = ctx.s_w; s_mm = ctx.s_mm; s_ep = ctx.s_ep
            s_st = ctx.s_st; s_h = ctx.s_h; s_c = ctx.s_c; s_g = ctx.s_g
            s_fs = ctx.s_fs; s_x4f = ctx.s_x4f; s_x4t = ctx.s_x4t; s_yn = ctx.s_yn
            s_xo = ctx.s_xo; s_cl = ctx.s_cl; s_x4l = ctx.s_x4l; s_zx = ctx.s_zx
            s_hp = ctx.s_hp
        sems = dict(mm=s_mm, ep=s_ep)
        sc = dict(x2=x2, mrow=mrow, vrow=vrow, trow=trow, mbfr=mbfr, rbfr=rbfr, mbc=mbc, rbc=rbc)
        xbf = big

        def ld0(sy):
            s_c.dma(sy.dma_start(out=convws[:], in_=convw[:]))
            s_c.dma(sy.dma_start(out=convbs[:], in_=convb[:]))
            s_c.dma(sy.dma_start(out=dtbs[:], in_=dtb[:]))
            s_c.dma(sy.dma_start(out=alogs[:], in_=alog[:]))
            s_c.dma(sy.dma_start(out=dcols[:], in_=dcolv[:]))
            s_c.dma(sy.dma_start(out=nws[:], in_=nwv[:]))
            s_c.dma(sy.dma_start(out=gat[:], in_=gates[:]))
            s_c.dma(sy.dma_start(out=msk[:], in_=masks[:]))
            s_c.dma(sy.dma_start(out=idn[:], in_=ident[:]))
            s_c.dma(sy.dma_start(out=idnf[:], in_=identf[:]))
        block.sync(ld0)

        def ld0g(g):
            g.memset(ones1[:], 1.0)
            g.memset(onesc[:], 1.0)
            g.memset(onesf[:], 1.0)
            g.memset(zeros2[:], 0.0)
            g.memset(Sfp[:], 0.0)
            s_g.tick(g.memset(xpad[:, 0:3], 0.0))
        block.gpsimd(ld0g)
        consts_c = s_c.v
        consts_g = s_g.v
        for sec in (block.tensor, block.vector, block.scalar):
            sec(lambda e: (e.wait_ge(s_c.h, consts_c), e.wait_ge(s_g.h, consts_g)))

        wt = [wtA, wtB]
        pt = [ptA, ptB]
        fm_mm = {}
        fm_ep = {}
        fm_st = {}
        jg = [0]

        def emit_proj(dst_view, dst_chunk, dst_col0, src_ec, xln_ready, to_dt=None):
            j = jg[0]; jg[0] += 1
            pp = j % 2
            ps = psA if pp == 0 else psB
            ncols = 64 if to_dt is not None else 128

            def wl(g, j=j, pp=pp, src_ec=src_ec, ncols=ncols):
                g.wait_ge(s_mm.h, fm_mm.get(j - 2, 0))
                s_w.dma(g.dma_start(out=wt[pp][:, 0:DC * ncols].rearrange("p (k i) -> p k i", k=DC),
                                    in_=winv[:, :, src_ec * 128:src_ec * 128 + ncols]))
            block.gpsimd(wl)
            w_arr = s_w.v

            def mmj(t, j=j, pp=pp, ps=ps, w_arr=w_arr, xln_ready=xln_ready, ncols=ncols):
                t.wait_ge(s_w.h, w_arr)
                s_ep.wait(t, vmax(fm_ep.get(j - 2, 0), xln_ready))
                for k in range(DC):
                    for q in range(4):
                        mmi = t.matmul(ps[0:ncols, q * 512:(q + 1) * 512],
                                       wt[pp][:, k * ncols:(k + 1) * ncols],
                                       xbf[:, k * T + q * 512:k * T + (q + 1) * 512],
                                       start=(k == 0), stop=(k == DC - 1))
                s_mm.tick(mmi)
            block.tensor(mmj)
            fm_mm[j] = s_mm.v
            mmv = s_mm.v

            if to_dt is not None:
                def epid(v, ps=ps, mmv=mmv, to_dt=to_dt):
                    v.wait_ge(s_mm.h, mmv)
                    s_ep.tick(v.tensor_copy(dt_sb[0:64, to_dt:to_dt + T], ps[0:64, 0:2048]))
                block.vector(epid)
                fm_ep[j] = s_ep.v
            else:
                def epij(s, pp=pp, ps=ps, mmv=mmv, fst=fm_st.get(j - 2, 0)):
                    s.wait_ge(s_mm.h, mmv)
                    s.wait_ge(s_fs.h, fst)
                    s.activation(out=pt[pp][:], in_=ps[:, 0:2048], func=AF.Copy).then_inc(s_ep.hs, 1)
                sb1m.add(epij)
                s_ep.reserve_s(1)
                fm_ep[j] = s_ep.v
                epv = s_ep.v

                def stj(sy, pp=pp, dst_view=dst_view, dst_chunk=dst_chunk, dst_col0=dst_col0, epv=epv):
                    s_ep.wait(sy, epv)
                    s_fs.dma(sy.dma_start(out=dst_view[:, dst_chunk, dst_col0:dst_col0 + T], in_=pt[pp][:]))
                block.sync(stj)
                fm_st[j] = s_fs.v

        sb1m = ScalBatch(block)
        for half in range(2):
            xcol0 = half * T
            xgate = s_mm.v

            def xload(g, xcol0=xcol0, xgate=xgate):
                g.wait_ge(s_mm.h, xgate)
                for qq in range(4):
                    s_ld.dma(g.dma_start(
                        out=xbf[:, 0:DC * T].rearrange("p (k t) -> p k t", k=DC)[:, 4 * qq:4 * qq + 4, :],
                        in_=xsrc(slice(4 * qq, 4 * qq + 4), xcol0)))
            block.gpsimd(xload)
            xld = s_ld.v
            for sec in (block.tensor, block.vector, block.scalar):
                sec(lambda e, xld=xld: e.wait_ge(s_ld.h, xld))
            emit_layernorm(block, sems, xbf, psA, psB, ones1, onesc, sc)
            xln_ready = s_ep.v

            if half == 1:
                for ec in range(32):
                    emit_proj(zv, ec, 0, ec, xln_ready)
            for ec in range(34):
                emit_proj(xbcv, ec, xcol0, 32 + ec, xln_ready)
            emit_proj(None, 0, 0, 66, xln_ready, to_dt=xcol0)
            sb1m.flush()

        proj_mm = s_mm.v
        proj_ep = s_ep.v
        proj_st = s_fs.v

        if phases < 2:
            ff = s_st.v
            block.gpsimd(lambda g: g.wait_ge(s_st.h, ff))
            return nc
        def m2a(v):
            s_ep.wait(v, proj_ep)
            s_ep.tick(v.tensor_scalar_add(acs[0:64, :], dt_sb[0:64, :], dtbs[0:64, 0:1]))
        block.vector(m2a)
        t1 = s_ep.v
        if phases < 22:
            ff = s_st.v; fe = s_ep.v
            block.gpsimd(lambda g: (g.wait_ge(s_st.h, ff), s_ep.wait(g, fe)))
            return nc

        def m2b(s, t1=t1):
            s_ep.wait(s, t1)
            s.activation(out=dt_sb[0:64, :], in_=acs[0:64, :], func=AF.Abs)
            s.activation(out=dt_sb[0:64, :], in_=dt_sb[0:64, :], func=AF.Exp, scale=-1.0)
            s.activation(out=dt_sb[0:64, :], in_=dt_sb[0:64, :], func=AF.Ln, bias=1.0)
            s.activation(out=acol[0:64, :], in_=alogs[0:64, :], func=AF.Exp)
            s_ep.tick(s.mul(acol[0:64, :], acol[0:64, :], -1.0))
        block.scalar(m2b)
        t2 = s_ep.v

        def m2c(v, t2=t2):
            s_ep.wait(v, t2)
            v.tensor_relu(acs[0:64, :], acs[0:64, :])
            v.tensor_add(dt_sb[0:64, :], dt_sb[0:64, :], acs[0:64, :])
            v.tensor_scalar_min(dt_sb[0:64, :], dt_sb[0:64, :], 1.0e10)
            s_ep.tick(v.tensor_scalar_mul(acs[0:64, :], dt_sb[0:64, :], acol[0:64, 0:1]))
        block.vector(m2c)
        t3 = s_ep.v
        if phases < 23:
            ff = s_st.v; fe = s_ep.v
            block.gpsimd(lambda g: (g.wait_ge(s_st.h, ff), s_ep.wait(g, fe)))
            return nc

        def m2d(v, t3=t3):
            s_ep.wait(v, t3)
            gones = big[0:64, 0:512].bitcast(F32)
            v.wait_ge(s_mm.h, proj_mm)
            v.memset(gones, 1.0)
            for c in range(NCH):
                la = v.tensor_tensor_scan(acs[0:64, c * MCH:(c + 1) * MCH], gones,
                                          acs[0:64, c * MCH:(c + 1) * MCH], 0.0, ALU.mult, ALU.add)
            v.tensor_copy(acs_bf[0:64, :], acs[0:64, :])
            s_ep.tick(la)
        block.vector(m2d)
        t4 = s_ep.v
        if phases < 24:
            ff = s_st.v; fe = s_ep.v
            block.gpsimd(lambda g: (g.wait_ge(s_st.h, ff), s_ep.wait(g, fe)))
            return nc

        def m2e0(v, t4=t4):
            s_ep.wait(v, t4)
            s_ep.tick(v.tensor_copy(w2[0:64, 0:NCH],
                                    acs[0:64, :].rearrange("h (c l) -> h c l", l=MCH)[:, :, MCH - 1]))
        block.vector(m2e0)
        t4b = s_ep.v

        def m2e(s, t4b=t4b):
            s_ep.wait(s, t4b)
            s_ep.tick(s.activation(out=lam_bf[0:64, :], in_=w2[0:64, 0:NCH], func=AF.Exp))
        block.scalar(m2e)
        t5 = s_ep.v
        if phases < 25:
            ff = s_st.v; fe = s_ep.v
            block.gpsimd(lambda g: (g.wait_ge(s_st.h, ff), s_ep.wait(g, fe)))
            return nc

        trc = {}
        for kc in range(32):
            def ptr(t, kc=kc, t4=t4):
                s_ep.wait(t, t4 if kc < 2 else trc[kc - 2])
                pst = (psA if kc % 2 == 0 else psB)[:, 1536:1600]
                s_mm.tick(t.transpose(pst, acs[0:64, kc * 128:(kc + 1) * 128], idnf[0:64, 0:64]))
            block.tensor(ptr)
            trm = s_mm.v

            def ptc(v, kc=kc, trm=trm):
                v.wait_ge(s_mm.h, trm)
                pst = (psA if kc % 2 == 0 else psB)[:, 1536:1600]
                s_ep.tick(v.tensor_copy(acsT[:, kc * 64:(kc + 1) * 64], pst))
            block.vector(ptc)
            trc[kc] = s_ep.v
        acst_done = s_ep.v
        if phases < 26:
            ff = s_st.v; fe = s_ep.v
            block.gpsimd(lambda g: (g.wait_ge(s_st.h, ff), s_ep.wait(g, fe)))
            return nc

        lhp = {}
        lrowm = {}
        for h in range(64):
            base = 64 * (h % 2)

            def lex(t, h=h, base=base):
                s_ep.wait(t, vmax(acst_done, lrowm.get(h - 2, 0)))
                s_mm.tick(t.matmul(psA[0:1, 1536 + (h % 2) * 16:1552 + (h % 2) * 16],
                                   idn[0:64, h:h + 1], lam_bf[0:64, :], start=True, stop=True))
            block.tensor(lex)
            lem = s_mm.v

            def lrowc(v, h=h, lem=lem):
                v.wait_ge(s_mm.h, lem)
                s_ep.tick(v.tensor_copy(erow[0:1, (h % 2) * 16:(h % 2) * 16 + 16].bitcast(BF16),
                                        psA[0:1, 1536 + (h % 2) * 16:1552 + (h % 2) * 16]))
            block.vector(lrowc)
            lrowm[h] = s_ep.v
            lrw = s_ep.v

            def lbc(t, h=h, base=base, lrw=lrw):
                s_ep.wait(t, vmax(lrw, lhp.get(h - 2, 0)))
                s_mm.tick(t.matmul(psB[base:base + 64, 1536 + (h % 2) * 16:1552 + (h % 2) * 16],
                                   ones1[0:1, 0:64],
                                   erow[0:1, (h % 2) * 16:(h % 2) * 16 + 16].bitcast(BF16),
                                   start=True, stop=True))
            block.tensor(lbc)
            lbm = s_mm.v

            def lco(v, h=h, base=base, lbm=lbm):
                v.wait_ge(s_mm.h, lbm)
                hpc = h // 2
                s_ep.tick(v.tensor_copy(lamhp[base:base + 64, hpc * NCH:(hpc + 1) * NCH],
                                        psB[base:base + 64, 1536 + (h % 2) * 16:1552 + (h % 2) * 16]))
            block.vector(lco)
            lhp[h] = s_ep.v
        m2_done = s_ep.v

        if phases < 30:
            ff = s_st.v
            block.gpsimd(lambda g: (g.wait_ge(s_st.h, ff), s_ep.wait(g, m2_done)))
            return nc
        sb3m = ScalBatch(block)
        cv_ep = {}
        cv_mm = {}
        cvst = {}
        x4st = {}
        for ic in range(34):
            def cload(g, ic=ic):
                g.wait_ge(s_fs.h, proj_st)
                s_ep.wait(g, cv_ep.get(ic - 1, 0))
                s_cl.dma(g.dma_start(out=xpad[:, 3:3 + 2 * T], in_=xbcv[:, ic, :]))
            block.gpsimd(cload)
            c_arr = s_cl.v

            def conv(v, ic=ic, c_arr=c_arr):
                v.wait_ge(s_cl.h, c_arr)
                s_ep.wait(v, vmax(m2_done, cv_ep.get(ic - 1, 0)))
                v.wait_ge(s_mm.h, cv_mm.get(ic - 1, 0))
                v.wait_ge(s_x4f.h, cvst.get(ic - 1, 0))
                v.tensor_scalar_mul(xpad[:, 3:3 + T], xpad[:, 3:3 + T], gat[:, 0:1])
                for hf in range(2):
                    ps = (psA if hf == 0 else psB)[:, 0:2048]
                    v.tensor_scalar(ps, xpad[:, hf * 2048:hf * 2048 + 2048],
                                    convws[:, ic * 4:ic * 4 + 1], None, ALU.mult)
                    for j in range(1, 4):
                        la = v.scalar_tensor_tensor(ps, xpad[:, hf * 2048 + j:hf * 2048 + j + 2048],
                                                    convws[:, ic * 4 + j:ic * 4 + j + 1], ps,
                                                    ALU.mult, ALU.add)
                s_ep.tick(la)
            block.vector(conv)
            cvd = s_ep.v

            def csil(s, ic=ic, cvd=cvd):
                s_ep.wait(s, cvd)
                s.activation(out=cvb[:, 0:2048], in_=psA[:, 0:2048], func=AF.Silu,
                             bias=convbs[:, ic:ic + 1])
                s.activation(out=cvb[:, 2048:4096], in_=psB[:, 0:2048], func=AF.Silu,
                             bias=convbs[:, ic:ic + 1]).then_inc(s_ep.hs, 1)
            sb3m.add(csil)
            s_ep.reserve_s(1)
            cv_ep[ic] = s_ep.v
            sil_done = s_ep.v

            if ic < 32:
                def x4fst(sy, ic=ic, sil_done=sil_done):
                    s_ep.wait(sy, sil_done)
                    s_x4f.dma(sy.dma_start(out=x4fv[:, ic, :], in_=cvb[:, T:2 * T]))
                block.sync(x4fst)
                cvst[ic] = s_x4f.v
                for j in range(32):
                    def ctr(t, ic=ic, j=j, sil_done=sil_done):
                        s_ep.wait(t, sil_done if j < 2 else x4st[j - 2])
                        pst = (psA if j % 2 == 0 else psB)[:, 1792:1856].bitcast(BF16)
                        s_mm.tick(t.transpose(pst, cvb[:, j * 128:(j + 1) * 128], idn[:]))
                    block.tensor(ctr)
                    trm = s_mm.v

                    def ctc(v, j=j, trm=trm):
                        v.wait_ge(s_mm.h, trm)
                        pst = (psA if j % 2 == 0 else psB)[:, 1792:1856].bitcast(BF16)
                        s_ep.tick(v.tensor_copy(x4s[:, j * 128:(j + 1) * 128], pst))
                    block.vector(ctc)
                    x4st[j] = s_ep.v
                cv_mm[ic] = s_mm.v
                stg_done = s_ep.v

                def x4tst(sy, ic=ic, stg_done=stg_done):
                    s_ep.wait(sy, stg_done)
                    s_x4t.dma(sy.dma_start(out=x4tv[:, :, ic * 128:(ic + 1) * 128],
                                          in_=x4s[:].rearrange("p (tc i) -> p tc i", i=128)))
                block.sync(x4tst)
            elif ic == 32:
                def bcp(v, sil_done=sil_done):
                    s_ep.wait(v, sil_done)
                    s_ep.tick(v.tensor_copy(Bfm[:], cvb[:]))
                block.vector(bcp)
                cv_ep[ic] = s_ep.v
                cv_mm[ic] = s_mm.v
            else:
                def ccp(v, sil_done=sil_done):
                    s_ep.wait(v, sil_done)
                    s_ep.tick(v.tensor_copy(Cfm[:], cvb[:]))
                block.vector(ccp)
                cv_ep[ic] = s_ep.v
                cv_mm[ic] = s_mm.v
        m3_stf = s_x4f.v
        sb3m.flush()
        m3_stt = s_x4t.v
        m3_ep = s_ep.v

        if phases < 40:
            ff = s_st.v
            block.gpsimd(lambda g: (g.wait_ge(s_st.h, ff), s_ep.wait(g, m3_ep)))
            return nc
        x4t_t = big[:, 0:8192]
        xs1v = big[:, 8192:16384]
        xs2v = big[:, 16384:24576]
        decb = wtsb
        wts2 = SB("wts2", [128, MCH], BF16)
        w1b = SB("w1b", [64, MCH], BF16)
        w1T = SB("w1T", [128, 128], F32)
        sfpbA = SB("sfpbA", [128, 128], BF16)
        sfpbB = SB("sfpbB", [128, 128], BF16)
        sfpb = [sfpbA, sfpbB]
        tmpf2 = SB("tmpf2", [128, MCH], F32)
        erow2 = SB("erow2", [1, MCH], BF16)

        sbm = ScalBatch(block)
        ch_ep = m3_ep
        ch_mm = s_mm.v
        ch_st = m3_stt
        zx_gate = [0]
        yst_gate = [0]
        prev_wde = [0]
        ydst_hold = [0]
        for ci in range(NCH):
            own = ci >= 8
            def x4l(g, ci=ci, ch_ep=ch_ep, ch_mm=ch_mm):
                g.wait_ge(s_x4t.h, m3_stt)
                g.wait_ge(s_mm.h, ch_mm)
                s_ep.wait(g, ch_ep)
                s_x4l.dma(g.dma_start(out=x4t_t.rearrange("p (s hp) -> p s hp", s=2),
                                      in_=x4tv[:, 2 * ci:2 * ci + 2, :]))
            block.gpsimd(x4l)
            x4a = s_x4l.v

            def wrow_s(s, ci=ci, ch_ep=ch_ep):
                s_ep.wait(s, ch_ep)
                s.activation(out=w2[0:64, :], in_=acs[0:64, ci * MCH:(ci + 1) * MCH],
                             func=AF.Exp, scale=-1.0,
                             bias=acs[0:64, (ci + 1) * MCH - 1:(ci + 1) * MCH]).then_inc(s_ep.hs, 1)
            sbm.add(wrow_s)
            s_ep.reserve_s(1)
            w2a = s_ep.v

            def wrow_v(v, ci=ci, own=own, w2a=w2a):
                s_ep.wait(v, w2a)
                v.tensor_mul(w2[0:64, :], w2[0:64, :], dt_sb[0:64, ci * MCH:(ci + 1) * MCH])
                if ci < 8:
                    v.tensor_scalar_mul(w2[0:64, :], w2[0:64, :], gat[0:64, 0:1])
                if own:
                    v.tensor_copy(w1b[0:64, :], dt_sb[0:64, ci * MCH:(ci + 1) * MCH])
                s_ep.tick(v.tensor_copy(w2b[0:64, :], w2[0:64, :]))
            block.vector(wrow_v)
            w2c = s_ep.v


            def wtr(t, ci=ci, own=own, w2c=w2c):
                s_ep.wait(t, w2c)
                t.transpose(psA[:, 0:32].bitcast(BF16), w2b[0:64, 0:128], idn[0:64, 0:64])
                t.transpose(psA[:, 512:544].bitcast(BF16), w2b[0:64, 128:256], idn[0:64, 0:64])
                if own:
                    t.transpose(psA[:, 1024:1056].bitcast(BF16), w1b[0:64, 0:128], idn[0:64, 0:64])
                    t.transpose(psB[:, 1024:1056].bitcast(BF16), w1b[0:64, 128:256], idn[0:64, 0:64])
                t.transpose(psB[:, 0:64].bitcast(BF16), Bfm[:, ci * MCH:ci * MCH + 128], idn[:])
                s_mm.tick(t.transpose(psB[:, 512:576].bitcast(BF16),
                                      Bfm[:, ci * MCH + 128:ci * MCH + 256], idn[:]))
            block.tensor(wtr)
            wtm = s_mm.v

            def wtc(v, own=own, wtm=wtm):
                v.wait_ge(s_mm.h, wtm)
                v.tensor_copy(wT2[:, 0:64], psA[:, 0:32].bitcast(BF16))
                v.tensor_copy(wT2[:, 64:128], psA[:, 512:544].bitcast(BF16))
                if own:
                    v.tensor_copy(w1T[:, 0:64], psA[:, 1024:1056].bitcast(BF16))
                    v.tensor_copy(w1T[:, 64:128], psB[:, 1024:1056].bitcast(BF16))
                v.tensor_copy(Btm[:, 0:128], psB[:, 0:64].bitcast(BF16))
                s_ep.tick(v.tensor_copy(Btm[:, 128:256], psB[:, 512:576].bitcast(BF16)))
            block.vector(wtc)
            wtcd = s_ep.v

            def xsb(v, own=own, wtcd=wtcd, x4a=x4a):
                s_ep.wait(v, wtcd)
                v.wait_ge(s_x4l.h, x4a)
                for sub in range(2):
                    src3 = x4t_t[:, sub * 4096:(sub + 1) * 4096].rearrange("p (h q) -> p h q", h=64)
                    w2s = wT2[:, sub * 64:(sub + 1) * 64].broadcast_to([128, 64, 64])
                    v.tensor_tensor(xs2v[:, sub * 4096:(sub + 1) * 4096].rearrange("p (h q) -> p h q", h=64),
                                    src3, w2s, ALU.mult)
                    if own:
                        w1s = w1T[:, sub * 64:(sub + 1) * 64].broadcast_to([128, 64, 64])
                        la = v.tensor_tensor(xs1v[:, sub * 4096:(sub + 1) * 4096].rearrange("p (h q) -> p h q", h=64),
                                             src3, w1s, ALU.mult)
                    else:
                        la = None
                s_ep.tick(la if la is not None else v.tensor_copy(wts2[:, 0:1], wT2[:, 0:1]))
            block.vector(xsb)
            xsd = s_ep.v

            def slmm(t, xsd=xsd, slgate=ch_ep):
                s_ep.wait(t, xsd)
                for sub in range(2):
                    for nq in range(8):
                        ps = psA if nq < 4 else psB
                        q = nq % 4
                        mmi = t.matmul(ps[:, q * 512:(q + 1) * 512], Btm[:, sub * 128:(sub + 1) * 128],
                                       xs2v[:, sub * 4096 + nq * 512:sub * 4096 + (nq + 1) * 512],
                                       start=(sub == 0), stop=(sub == 1))
                s_mm.tick(mmi)
            block.tensor(slmm)
            slm = s_mm.v

            def slev(v, slm=slm):
                v.wait_ge(s_mm.h, slm)
                v.tensor_copy(slbf[:, 0:2048], psA[:, 0:2048])
                s_ep.tick(v.tensor_copy(slbf[:, 2048:4096], psB[:, 0:2048]))
            block.vector(slev)
            sld = s_ep.v

            if own:
                co = ci - 8
                sintr = {}
                strm = {}
                for hpc in range(HPC):
                    def sbc(v, hpc=hpc, sld=sld):
                        v.wait_ge(s_mm.h, strm.get(hpc - 2, 0))
                        if hpc == 0:
                            s_ep.wait(v, sld)
                        s_ep.tick(v.tensor_copy(sfpb[hpc % 2][:], Sfp[:, hpc * 128:(hpc + 1) * 128]))
                    block.vector(sbc)
                    sbd = s_ep.v

                    def str_(t, hpc=hpc, sbd=sbd, sld=sld):
                        s_ep.wait(t, vmax(sbd, sintr.get(hpc - 2, 0)))
                        pst = (psA if hpc % 2 == 0 else psB)[:, 1088:1152].bitcast(BF16)
                        s_mm.tick(t.transpose(pst, sfpb[hpc % 2][:], idn[:]))
                    block.tensor(str_)
                    strm[hpc] = s_mm.v
                    stm = s_mm.v

                    def stc(v, hpc=hpc, stm=stm):
                        v.wait_ge(s_mm.h, stm)
                        pst = (psA if hpc % 2 == 0 else psB)[:, 1088:1152].bitcast(BF16)
                        s_ep.tick(v.tensor_copy(Sin[:, hpc * 128:(hpc + 1) * 128], pst))
                    block.vector(stc)
                    sintr[hpc] = s_ep.v
                sin_done = s_ep.v

                def gtm(t, ci=ci, sin_done=sin_done):
                    s_ep.wait(t, sin_done)
                    t.matmul(psA[:, 0:256], Bfm[:, ci * MCH:ci * MCH + 128],
                             Cfm[:, ci * MCH:(ci + 1) * MCH], start=True, stop=True)
                    s_mm.tick(t.matmul(psA[:, 256:512], Bfm[:, ci * MCH + 128:ci * MCH + 256],
                                       Cfm[:, ci * MCH:(ci + 1) * MCH], start=True, stop=True))
                block.tensor(gtm)
                gtd = s_mm.v

                def gte(v, gtd=gtd):
                    v.wait_ge(s_mm.h, gtd)
                    v.tensor_mul(GTsb[:, 0:256], psA[:, 0:256], msk[:, 0:256])
                    s_ep.tick(v.tensor_mul(GTsb[:, 256:512], psA[:, 256:512], msk[:, 512:768]))
                block.vector(gte)
                gted = s_ep.v

                def eacs_s(s, ci=ci, gted=gted):
                    s_ep.wait(s, gted)
                    s.activation(out=eacs[0:64, :], in_=acs[0:64, ci * MCH:(ci + 1) * MCH],
                                 func=AF.Exp).then_inc(s_ep.hs, 1)
                sbm.add(eacs_s)
                s_ep.reserve_s(1)
                eacd = s_ep.v

                def zx4l(g, co=co, sld=sld, m3f=m3_stf):
                    g.wait_ge(s_x4f.h, m3f)
                    s_ep.wait(g, zx_gate[0])
                    for qq in range(4):
                        s_zx.dma(g.dma_start(
                            out=zsil[:].rearrange("p (c l) -> p c l", l=MCH)[:, 8 * qq:8 * qq + 8, :],
                            in_=zv[:, 8 * qq:8 * qq + 8, co * MCH:(co + 1) * MCH]))
                    for qq in range(4):
                        s_zx.dma(g.dma_start(
                            out=x4ft[:].rearrange("p (c l) -> p c l", l=MCH)[:, 8 * qq:8 * qq + 8, :],
                            in_=x4fv[:, 8 * qq:8 * qq + 8, co * MCH:(co + 1) * MCH]))
                block.gpsimd(zx4l)
                zx4a = s_zx.v

                hpc_ep = {}
                for hpc in range(HPC):
                    ypp = hpc % 2
                    ytile = psA[:, 512 + ypp * 256:512 + ypp * 256 + 256]
                    y2tile = psA[:, 1024 + ypp * 256:1024 + ypp * 256 + 256]
                    def dle(t, hpc=hpc, eacd=eacd, prv=None):
                        s_ep.wait(t, vmax(eacd, hpc_ep.get(hpc - 1, 0)))
                        s_mm.tick(t.matmul(psA[0:1, 1792:1792 + 256], idn[0:64, 2 * hpc:2 * hpc + 1],
                                           eacs[0:64, :], start=True, stop=True))
                    block.tensor(dle)
                    d1 = s_mm.v

                    def dlc(v, d1=d1):
                        v.wait_ge(s_mm.h, d1)
                        s_ep.tick(v.tensor_copy(erow[0:1, :], psA[0:1, 1792:1792 + 256]))
                    block.vector(dlc)
                    d2 = s_ep.v

                    def dle2(t, hpc=hpc, d2=d2):
                        s_ep.wait(t, d2)
                        t.matmul(psB[0:64, 0:256], ones1[0:1, 0:64], erow[0:1, :], start=True, stop=True)
                        s_mm.tick(t.matmul(psA[0:1, 1792:1792 + 256], idn[0:64, 2 * hpc + 1:2 * hpc + 2],
                                           eacs[0:64, :], start=True, stop=True))
                    block.tensor(dle2)
                    d3 = s_mm.v

                    def dlc2(v, d3=d3):
                        v.wait_ge(s_mm.h, d3)
                        s_ep.tick(v.tensor_copy(erow2[0:1, :], psA[0:1, 1792:1792 + 256]))
                    block.vector(dlc2)
                    d4 = s_ep.v

                    def dle3(t, d4=d4):
                        s_ep.wait(t, d4)
                        s_mm.tick(t.matmul(psB[64:128, 0:256], ones1[0:1, 0:64], erow2[0:1, :],
                                           start=True, stop=True))
                    block.tensor(dle3)
                    d5 = s_mm.v

                    def dlc3(v, d5=d5):
                        v.wait_ge(s_mm.h, d5)
                        s_ep.tick(v.tensor_copy(decL[:, 0:256], psB[:, 0:256]))
                    block.vector(dlc3)
                    dld = s_ep.v

                    def i2m(t, hpc=hpc, ci=ci, dld=dld, sin_done=sin_done):
                        s_ep.wait(t, vmax(hpc_ep.get(hpc - 2, 0), dld))
                        s_mm.tick(t.matmul(y2tile, Sin[:, hpc * 128:(hpc + 1) * 128],
                                           Cfm[:, ci * MCH:(ci + 1) * MCH], start=True, stop=True))
                    block.tensor(i2m)
                    i2d = s_mm.v

                    y1t = {}
                    for hh in range(2):
                        h = 2 * hpc + hh
                        def wex(t, h=h, ci=ci, i2d=i2d):
                            t.wait_ge(s_mm.h, i2d)
                            s_mm.tick(t.matmul(psA[0:1, 1792:1792 + 256], idnf[0:64, h:h + 1],
                                               acs[0:64, ci * MCH:(ci + 1) * MCH],
                                               start=True, stop=True))
                        block.tensor(wex)
                        we1 = s_mm.v

                        def wec(v, we1=we1):
                            v.wait_ge(s_mm.h, we1)
                            s_ep.tick(v.tensor_copy(erowf[0:1, :], psA[0:1, 1792:1792 + 256]))
                        block.vector(wec)
                        we2 = s_ep.v

                        def web(t, we2=we2):
                            s_ep.wait(t, we2)
                            s_mm.tick(t.matmul(psA[:, 1536:1792], onesf[:], erowf[0:1, :],
                                               start=True, stop=True))
                        block.tensor(web)
                        we3 = s_mm.v

                        for sub in range(2):
                            prev_y1 = s_mm.v

                            def wdf(v, ci=ci, h=h, sub=sub, we3=we3, pwde=prev_wde[0]):
                                v.wait_ge(s_mm.h, we3)
                                s_ep.wait(v, pwde)
                                s_ep.tick(v.scalar_tensor_tensor(
                                    diffs[:], psA[:, 1536:1792],
                                    acsT[:, (2 * ci + sub) * 64 + h:(2 * ci + sub) * 64 + h + 1],
                                    zeros2[:], ALU.subtract, ALU.min))
                            block.vector(wdf)
                            wd1 = s_ep.v

                            def wde(s, wd1=wd1):
                                s_ep.wait(s, wd1)
                                s.activation(out=wtsb[:], in_=diffs[:], func=AF.Exp).then_inc(s_ep.hs, 1)
                            sbm.add(wde)
                            s_ep.reserve_s(1)
                            wd2 = s_ep.v
                            prev_wde[0] = wd2

                            def wdm(v, sub=sub, wd2=wd2, prev_y1=prev_y1):
                                s_ep.wait(v, wd2)
                                v.wait_ge(s_mm.h, prev_y1)
                                s_ep.tick(v.tensor_mul(wts2[:], GTsb[:, sub * 256:(sub + 1) * 256], wtsb[:]))
                            block.vector(wdm)
                            wd3 = s_ep.v

                            def y1m(t, h=h, hh=hh, sub=sub, wd3=wd3, ytile=ytile):
                                s_ep.wait(t, wd3)
                                s_mm.tick(t.matmul(ytile[hh * 64:hh * 64 + 64, :],
                                                   xs1v[:, sub * 4096 + h * 64:sub * 4096 + (h + 1) * 64],
                                                   wts2[:], start=(sub == 0), stop=(sub == 1)))
                            block.tensor(y1m)
                            y1t[(hh, sub)] = s_mm.v
                    y1_done = s_mm.v

                    def yev(v, hpc=hpc, co=co, y1_done=y1_done, zx4a=zx4a, ytile=ytile, y2tile=y2tile,
                            ysg=yst_gate[0]):
                        v.wait_ge(s_mm.h, y1_done)
                        v.wait_ge(s_zx.h, zx4a)
                        v.wait_ge(s_yn.h, ysg)
                        v.tensor_mul(tmpf[:], y2tile, decL[:, 0:256])
                        v.scalar_tensor_tensor(tmpf2[:], x4ft[:, hpc * MCH:(hpc + 1) * MCH],
                                               dcols[:, hpc:hpc + 1], ytile, ALU.mult, ALU.add)
                        s_ep.tick(v.tensor_add(Ysb[:, hpc * MCH:(hpc + 1) * MCH], tmpf[:], tmpf2[:]))
                    block.vector(yev)
                    hpc_ep[hpc] = s_ep.v
                y_done = s_ep.v

                if dump == "y":
                    dbgv = dbg.rearrange("(c p) t -> p c t", p=128)
                    def ydmp(sy, co=co, y_done=y_done):
                        s_ep.wait(sy, y_done)
                        for qq in range(4):
                            s_st.dma(sy.dma_start(
                                out=dbgv[:, 8 * qq:8 * qq + 8, co * MCH:(co + 1) * MCH],
                                in_=Ysb[:].rearrange("p (c l) -> p c l", l=MCH)[:, 8 * qq:8 * qq + 8, :]))
                    block.sync(ydmp)
                    ydst_hold[0] = s_st.v

                def gat_s(s, y_done=y_done, zx4a=zx4a, _yd=(dump == "y")):
                    if _yd:
                        s.wait_ge(s_st.h, ydst_hold[0])
                    s_ep.wait(s, y_done)
                    s.wait_ge(s_zx.h, zx4a)
                    s.activation(out=zsil[:], in_=zsil[:], func=AF.Silu).then_inc(s_ep.hs, 1)
                sbm.add(gat_s)
                s_ep.reserve_s(1)
                zs1 = s_ep.v

                def gat_v(v, zs1=zs1):
                    s_ep.wait(v, zs1)
                    s_ep.tick(v.tensor_mul(Ysb[:], Ysb[:], zsil[:]))
                block.vector(gat_v)
                yf1 = s_ep.v

                sq_mm = {}
                for hpc in range(HPC):
                    def sqs(s, hpc=hpc, yf1=yf1, sqg=sq_mm.get(hpc - 1, 0)):
                        s_ep.wait(s, yf1)
                        s.wait_ge(s_mm.h, sqg)
                        s.activation(out=x2[:, 0:256], in_=Ysb[:, hpc * MCH:(hpc + 1) * MCH],
                                     func=AF.Square).then_inc(s_ep.hs, 1)
                    sbm.add(sqs)
                    s_ep.reserve_s(1)
                    sqd = s_ep.v

                    def sqm(t, hpc=hpc, sqd=sqd):
                        s_ep.wait(t, sqd)
                        s_mm.tick(t.matmul(psB[0:1, 256:512], onesc[:], x2[:, 0:256],
                                           start=(hpc == 0), stop=(hpc == HPC - 1)))
                    block.tensor(sqm)
                    sq_mm[hpc] = s_mm.v
                ssqd = s_mm.v

                def rstd_v(v, ssqd=ssqd):
                    v.wait_ge(s_mm.h, ssqd)
                    v.tensor_scalar_mul(rrow[:], psB[0:1, 256:512], 1.0 / MI)
                    s_ep.tick(v.tensor_scalar_add(rrow[:], rrow[:], 1e-5))
                block.vector(rstd_v)
                rs1 = s_ep.v

                def rstd_s(s, rs1=rs1):
                    s_ep.wait(s, rs1)
                    s.activation(out=rrow[:], in_=rrow[:], func=AF.Ln)
                    s.activation(out=rrow[:], in_=rrow[:], func=AF.Exp, scale=-0.5).then_inc(s_ep.hs, 1)
                sbm.add(rstd_s)
                s_ep.reserve_s(1)
                rs2 = s_ep.v

                def rstd_b(v, rs2=rs2):
                    s_ep.wait(v, rs2)
                    s_ep.tick(v.tensor_copy(erow[0:1, :], rrow[:]))
                block.vector(rstd_b)
                rs3 = s_ep.v

                def rstd_m(t, rs3=rs3):
                    s_ep.wait(t, rs3)
                    s_mm.tick(t.matmul(psB[:, 512:768], ones1[:], erow[0:1, :], start=True, stop=True))
                block.tensor(rstd_m)
                rs4 = s_mm.v

                def rstd_c(v, rs4=rs4):
                    v.wait_ge(s_mm.h, rs4)
                    s_ep.tick(v.tensor_copy(tmpf[:], psB[:, 512:768]))
                block.vector(rstd_c)
                rs5 = s_ep.v

                yn_ep = {}
                for hpc in range(HPC):
                    def ynv_(v, hpc=hpc, rs5=rs5):
                        s_ep.wait(v, rs5)
                        s_ep.tick(v.scalar_tensor_tensor(
                            Ysb[:, hpc * MCH:(hpc + 1) * MCH], Ysb[:, hpc * MCH:(hpc + 1) * MCH],
                            nws[:, hpc:hpc + 1], tmpf[:], ALU.mult, ALU.mult))
                    block.vector(ynv_)
                    yn_ep[hpc] = s_ep.v
                ynd = s_ep.v

                def ynst(sy, co=co, ynd=ynd):
                    s_ep.wait(sy, ynd)
                    for qq in range(4):
                        s_yn.dma(sy.dma_start(
                            out=ynv[:, 8 * qq:8 * qq + 8, co * MCH:(co + 1) * MCH],
                            in_=Ysb[:].rearrange("p (c l) -> p c l", l=MCH)[:, 8 * qq:8 * qq + 8, :]))
                block.sync(ynst)
                zx_gate[0] = s_ep.v
                yst_gate[0] = s_yn.v

            sbm.flush()
            rec_mm = {}
            rec_ep = {}
            base_ep = y_done if own else sld
            for hpc in range(HPC):
                def rtr(t, hpc=hpc, ci=ci, base_ep=base_ep):
                    s_ep.wait(t, vmax(base_ep, rec_ep.get(hpc - 2, 0)))
                    pst = (psA if hpc % 2 == 0 else psB)[:, 1792:1856].bitcast(BF16)
                    s_mm.tick(t.transpose(pst, slbf[:, hpc * 128:(hpc + 1) * 128], idn[:]))
                block.tensor(rtr)
                rtm = s_mm.v

                def rup(v, hpc=hpc, ci=ci, rtm=rtm):
                    v.wait_ge(s_mm.h, rtm)
                    pst = (psA if hpc % 2 == 0 else psB)[:, 1792:1856].bitcast(BF16)
                    s_ep.tick(v.scalar_tensor_tensor(
                        Sfp[:, hpc * 128:(hpc + 1) * 128], Sfp[:, hpc * 128:(hpc + 1) * 128],
                        lamhp[:, hpc * NCH + ci:hpc * NCH + ci + 1],
                        pst, ALU.mult, ALU.add))
                block.vector(rup)
                rec_ep[hpc] = s_ep.v
                rec_mm[hpc] = s_mm.v
            ch_ep = s_ep.v
            ch_mm = s_mm.v

        mamba_st = s_yn.v
        mamba_ep = s_ep.v
        mamba_mm = s_mm.v

        if phases < 50:
            ff = s_st.v
            fe = s_ep.v
            block.gpsimd(lambda g: (g.wait_ge(s_st.h, ff), s_ep.wait(g, fe)))
            return nc
        woh = big[:, 0:32768]
        hstr = [hstrA, hstrB]
        m6_mm = {}
        m6_ep = {}
        m6_st = {}
        hl_mm = {}
        for dmh in range(2):
            def wol(g, dmh=dmh, mamba_mm=mamba_mm):
                g.wait_ge(s_mm.h, max(mamba_mm, m6_mm.get((dmh - 1, 3), 0)))
                s_ep.wait(g, mamba_ep)
                for qq in range(4):
                    s_w.dma(g.dma_start(
                        out=woh.rearrange("p (c d) -> p c d", d=1024)[:, 8 * qq:8 * qq + 8, :],
                        in_=woutv[:, 8 * qq:8 * qq + 8, dmh * 1024:(dmh + 1) * 1024]))
            block.gpsimd(wol)
            wo_arr = s_w.v

            for tp in range(4):
                for icc in range(HPC):
                    bufh = icc % 2
                    def hload(g, icc=icc, tp=tp, bufh=bufh, mamba_st=mamba_st):
                        g.wait_ge(s_yn.h, mamba_st)
                        g.wait_ge(s_mm.h, hl_mm.get((dmh, tp, icc - 2),
                                                    hl_mm.get((dmh, tp - 1, HPC - 2 + icc),
                                                              hl_mm.get((dmh - 1, 3, HPC - 2 + icc), 0))))
                        s_hp[bufh].dma(g.dma_start(out=hstr[bufh][:], in_=ynv[:, icc, tp * 512:(tp + 1) * 512]))
                    block.gpsimd(hload)
                    h_arr = s_hp[bufh].v

                    dpf = m6_ep.get((dmh, tp - 1), m6_ep.get((dmh - 1, 3), 0))

                    def dmm(t, icc=icc, bufh=bufh, h_arr=h_arr, wo_arr=wo_arr, dpf=dpf):
                        t.wait_ge(s_w.h, wo_arr)
                        t.wait_ge(s_hp[bufh].h, h_arr)
                        if icc == 0:
                            s_ep.wait(t, dpf)
                        for j in range(8):
                            ps = psA if j < 4 else psB
                            jj = j % 4
                            mmi = t.matmul(ps[:, jj * 512:(jj + 1) * 512],
                                           woh[:, icc * 1024 + j * 128:icc * 1024 + (j + 1) * 128],
                                           hstr[bufh][:], start=(icc == 0), stop=(icc == HPC - 1))
                        s_mm.tick(mmi)
                    block.tensor(dmm)
                    hl_mm[(dmh, tp, icc)] = s_mm.v
                dmm_all = s_mm.v

                for j in range(8):
                    dmc = dmh * 8 + j
                    xof = m6_ep.get((dmh, tp, j - 1),
                                    m6_ep.get((dmh, tp - 1, 7),
                                              m6_ep.get((dmh - 1, 3, 7), 0)))

                    def xload(sy, dmc=dmc, tp=tp, xof=xof):
                        s_ep.wait(sy, xof)
                        s_xo.dma(sy.dma_start(out=xoldA[:], in_=xown_cols(dmc, tp * 512, (tp + 1) * 512)))
                    block.sync(xload)
                    x_arr = s_xo.v

                    xnf = m6_st.get((dmh, tp, j - 1), m6_st.get((dmh, tp - 1, 7), m6_st.get((dmh - 1, 3, 7), 0)))

                    def epi(v, j=j, x_arr=x_arr, dmm_all=dmm_all, xnf=xnf):
                        v.wait_ge(s_mm.h, dmm_all)
                        v.wait_ge(s_xo.h, x_arr)
                        v.wait_ge(s_st.h, xnf)
                        ps = psA if j < 4 else psB
                        jj = j % 4
                        s_ep.tick(v.scalar_tensor_tensor(xnewA[:], ps[:, jj * 512:(jj + 1) * 512],
                                                         1.0, xoldA[:], ALU.mult, ALU.add))
                    block.vector(epi)
                    m6_ep[(dmh, tp, j)] = s_ep.v
                    e_done = s_ep.v

                    def xst(sy, dmc=dmc, tp=tp, e_done=e_done):
                        s_ep.wait(sy, e_done)
                        s_st.dma(sy.dma_start(out=xov[:, dmc, tp * 512:(tp + 1) * 512], in_=xnewA[:]))
                    block.sync(xst)
                    m6_st[(dmh, tp, j)] = s_st.v
                m6_ep[(dmh, tp)] = s_ep.v
            m6_mm[(dmh, 3)] = s_mm.v

        if dump is not None and dump != "y":
            fe = s_ep.v; fs = s_st.v
            def dmp(g, fe=fe, fs=fs):
                s_ep.wait(g, fe)
                g.wait_ge(s_st.h, fs)
                if dump == "xbc":
                    g.dma_start(out=dbg[:], in_=xbc_hbm[:]).then_inc(s_st.h, 16)
                elif dump == "x4f":
                    g.dma_start(out=dbg[:], in_=x4f_hbm[:]).then_inc(s_st.h, 16)
                elif dump == "yn":
                    g.dma_start(out=dbg[:], in_=yn_hbm[:]).then_inc(s_st.h, 16)
                elif dump == "dt":
                    g.dma_start(out=dbg[:], in_=dt_sb[:]).then_inc(s_st.h, 16)
                elif dump == "acs":
                    g.dma_start(out=dbg[:], in_=acs[:]).then_inc(s_st.h, 16)
                s_st.v += 16
            block.gpsimd(dmp)
        if ctx is None:
            final_st = s_st.v

            def fin(g):
                g.wait_ge(s_st.h, final_st)
            block.gpsimd(fin)

    return nc


def _mamba_maps(inputs):
    m = {}
    m["winT"] = np.ascontiguousarray(np.asarray(inputs["L1_attn_a_W_in"], dtype=np.float32).T)
    m["woutT"] = np.ascontiguousarray(np.asarray(inputs["L1_attn_a_W_out"], dtype=np.float32).T)
    cw = np.asarray(inputs["L1_attn_a_conv_w"], dtype=np.float32)[:, 0, :]
    cwp = np.zeros((128, 34 * 4), dtype=np.float32)
    cbp = np.zeros((128, 34), dtype=np.float32)
    cb = np.asarray(inputs["L1_attn_a_conv_b"], dtype=np.float32)
    for cc in range(34):
        cwp[:, cc * 4:(cc + 1) * 4] = cw[cc * 128:(cc + 1) * 128]
        cbp[:, cc] = cb[cc * 128:(cc + 1) * 128]
    m["convw"] = cwp
    m["convb"] = cbp
    m["dtb"] = np.ascontiguousarray(np.asarray(inputs["L1_attn_a_dt_bias"], dtype=np.float32).reshape(64, 1))
    m["alog"] = np.ascontiguousarray(np.asarray(inputs["L1_attn_a_A_log"], dtype=np.float32).reshape(64, 1))
    dv = np.asarray(inputs["L1_attn_a_D"], dtype=np.float32)
    hp = np.arange(MI)
    m["dcol"] = np.ascontiguousarray(dv[(hp // 64)].reshape(HPC, 128).T)
    m["nw"] = np.ascontiguousarray(np.asarray(inputs["L1_attn_a_norm_w"], dtype=np.float32).reshape(HPC, 128).T)
    jm = np.arange(512)[None, :] >= (np.arange(128)[:, None] + np.arange(4)[:, None, None] * 128)
    m["masks"] = np.ascontiguousarray(jm.transpose(1, 0, 2).reshape(128, 4 * 512).astype(bfdt))
    m["ident"] = np.eye(128, dtype=bfdt)
    m["identf"] = np.eye(128, dtype=np.float32)
    return m


def run_mamba(x_full, wmap):
    nc = _prog("mamba", build_mamba)
    in_maps = []
    for c in range(NCORES):
        b, h = c // 2, c % 2
        own = x_full[b, h * T:(h + 1) * T]
        other = x_full[b, (1 - h) * T:(2 - h) * T]
        xc = np.ascontiguousarray(np.concatenate([other.T, own.T], axis=1))
        gates = np.zeros((128, 4), dtype=np.float32)
        gates[:, 0] = float(h)
        in_maps.append(dict(x=xc, gates=gates, **wmap))
    res = run_bass_kernel_spmd(nc, in_maps, core_ids=list(range(NCORES)))
    out = x_full.copy()
    for c in range(NCORES):
        b, h = c // 2, c % 2
        out[b, h * T:(h + 1) * T] = res.results[c]["xo"].T
    return out



class _Ctx:
    pass


def _emit_barrier(ctx):
    vals = [(s.h, s.v) for s in ctx.all_sems if s.v > 0]
    epv = ctx.s_ep.v
    for sec in (ctx.block.tensor, ctx.block.vector, ctx.block.scalar,
                ctx.block.gpsimd, ctx.block.sync):
        def bw(e, vals=vals, epv=epv):
            for h, v in vals:
                e.wait_ge(h, v)
            ctx.s_ep.wait(e, epv)
        sec(bw)


def _emit_exchange(ctx, n):
    import contextlib
    nc = ctx.nc
    block = ctx.block
    s_ex, s_ec, s_ep = ctx.s_ex, ctx.s_ec, ctx.s_ep
    xov = ctx.x_own.rearrange("(k p) t -> p k t", p=128)
    civ = ctx.cc_in.rearrange("(k p) t -> p k t", p=128)
    cov = ctx.cc_out.rearrange("(k p) t -> p k t", p=128)
    xpv = ctx.x_past.rearrange("(k p) t -> p k t", p=128)
    es = contextlib.ExitStack()
    with es:
        sA = es.enter_context(nc.sbuf_tensor("ex%dA" % n, [128, 32768], BF16))
        sB = es.enter_context(nc.sbuf_tensor("ex%dB" % n, [128, 32768], BF16))
        exg = es.enter_context(nc.sbuf_tensor("ex%dg" % n, [128, 4], F32))
        sAv = sA[:, 0:DC * T].rearrange("p (k t) -> p k t", k=DC)
        sBv = sB[:, 0:DC * T].rearrange("p (k t) -> p k t", k=DC)

        def g1(g):
            s_ex.dma(g.dma_start(out=exg[:], in_=ctx.gates[:]))
            for q in range(4):
                s_ex.dma(g.dma_start(out=civ[:, 4 * q:4 * q + 4, :],
                                     in_=xov[:, 4 * q:4 * q + 4, :]))
            g.wait_ge(s_ex.h, s_ex.v)
            g.collective_compute(
                "AllGather", ALU.bypass,
                replica_groups=[[0, 1], [2, 3], [4, 5], [6, 7]],
                ins=[ctx.cc_in[:].opt()], outs=[ctx.cc_out[:].opt()],
            ).then_inc(s_ec.h)
            s_ec.v += 1
            g.wait_ge(s_ec.h, s_ec.v)
            for q in range(4):
                s_ex.dma(g.dma_start(out=sAv[:, 4 * q:4 * q + 4, :],
                                     in_=cov[:, 4 * q:4 * q + 4, :]))
                s_ex.dma(g.dma_start(out=sBv[:, 4 * q:4 * q + 4, :],
                                     in_=cov[:, DC + 4 * q:DC + 4 * q + 4, :]))
        block.gpsimd(g1)
        ldv = s_ex.v

        def v1(v):
            v.wait_ge(s_ex.h, ldv)
            for q in range(4):
                v.tensor_scalar_mul(sA[:, q * 8192:(q + 1) * 8192],
                                    sA[:, q * 8192:(q + 1) * 8192], exg[:, 0:1])
                s_ep.tick(v.scalar_tensor_tensor(
                    sA[:, q * 8192:(q + 1) * 8192], sB[:, q * 8192:(q + 1) * 8192],
                    exg[:, 2:3], sA[:, q * 8192:(q + 1) * 8192], ALU.mult, ALU.add))
        block.vector(v1)
        epv = s_ep.v

        def g2(g):
            s_ep.wait(g, epv)
            for q in range(4):
                s_ex.dma(g.dma_start(out=xpv[:, 4 * q:4 * q + 4, :],
                                     in_=sAv[:, 4 * q:4 * q + 4, :]))
        block.gpsimd(g2)


_FUSED = {}


def build_fused():
    import contextlib
    nc = bass.Bass("TRN2", target_bir_lowering=False, debug=False)
    ctx = _Ctx()
    ctx.nc = nc
    ctx.dram_cache = {}
    ctx.x0 = nc.dram_tensor("x0", [D, T], F32, kind="ExternalInput").ap()
    ctx.gates = nc.dram_tensor("gates", [128, 4], F32, kind="ExternalInput").ap()
    ctx.masks = nc.dram_tensor("masks", [128, 4 * 512], BF16, kind="ExternalInput").ap()
    ctx.ident = nc.dram_tensor("ident", [128, 128], BF16, kind="ExternalInput").ap()
    ctx.identf = nc.dram_tensor("identf", [128, 128], F32, kind="ExternalInput").ap()
    ctx.xo = nc.dram_tensor("xo", [D, T], F32, kind="ExternalOutput").ap()
    ctx.x_own = nc.dram_tensor("x_own", [D, T], F32, kind="Internal").ap()
    ctx.x_past = nc.dram_tensor("x_past", [D, T], BF16, kind="Internal").ap()
    ctx.cc_in = nc.dram_tensor("cc_in", [D, T], BF16, kind="Internal").ap()
    ctx.cc_out = nc.dram_tensor("cc_out", [2 * D, T], BF16, kind="Internal").ap()
    ctx.h_hbm = nc.dram_tensor("h_hbm", [IFF, T], BF16, kind="Internal").ap()

    es = contextlib.ExitStack()
    with es:
        ctx.all_sems = []

        def SEMs(name):
            s = Sem(es.enter_context(nc.semaphore(name)))
            ctx.all_sems.append(s)
            return s
        ctx.psA = es.enter_context(nc.psum_tensor("psA", [128, 2048], F32))
        ctx.psB = es.enter_context(nc.psum_tensor("psB", [128, 2048], F32))
        ctx.s_ld = SEMs("s_ld"); ctx.s_w = SEMs("s_w"); ctx.s_mm = SEMs("s_mm")
        ctx.s_ep = EpSem(es.enter_context(nc.semaphore("s_epv")),
                         es.enter_context(nc.semaphore("s_eps")))
        ctx.s_st = SEMs("s_st"); ctx.s_h = SEMs("s_h")
        ctx.s_c = SEMs("s_c"); ctx.s_g = SEMs("s_g")
        ctx.s_hs = SEMs("s_hs"); ctx.s_fs = SEMs("s_fs"); ctx.s_vs = SEMs("s_vs")
        ctx.s_vw = SEMs("s_vw"); ctx.s_xo = SEMs("s_xo")
        ctx.s_x4f = SEMs("s_x4f"); ctx.s_x4t = SEMs("s_x4t"); ctx.s_yn = SEMs("s_yn")
        ctx.s_cl = SEMs("s_cl"); ctx.s_x4l = SEMs("s_x4l"); ctx.s_zx = SEMs("s_zx")
        ctx.s_wp = [ctx.s_w, SEMs("s_w1")]
        ctx.s_hp = [ctx.s_h, SEMs("s_h1")]
        ctx.s_cp = [SEMs("s_cp0"), SEMs("s_cp1")]
        ctx.s_xop = [ctx.s_xo, SEMs("s_xo1")]
        ctx.s_stp = [ctx.s_st, SEMs("s_st1")]
        ctx.s_wo = [SEMs("s_wo0"), SEMs("s_wo1")]
        ctx.s_kv = [SEMs("s_kv0"), SEMs("s_kv1")]
        ctx.s_q = [SEMs("s_q0"), SEMs("s_q1")]
        ctx.s_ex = SEMs("s_ex"); ctx.s_ec = SEMs("s_ec")
        block = es.enter_context(nc.Block())
        ctx.block = block

        def ini(g):
            ctx.s_ex.dma(g.dma_start(out=ctx.x_own[:], in_=ctx.x0[:]))
        block.gpsimd(ini)
        _emit_barrier(ctx)

        def scoped(name, fn):
            with nc.named_scope(name):
                fn()
            _emit_barrier(ctx)
        scoped("ex0", lambda: _emit_exchange(ctx, 0))
        scoped("a0", lambda: build_attn("a0", ctx=ctx, pfx="a0"))
        scoped("ffnA", lambda: build_ffn(ctx=ctx, pfx="f0", uid="fA"))
        scoped("ex1", lambda: _emit_exchange(ctx, 1))
        scoped("b0", lambda: build_attn("b0", ctx=ctx, pfx="b0"))
        scoped("ffnB", lambda: build_ffn(ctx=ctx, pfx="f0", uid="fB"))
        scoped("ex2", lambda: _emit_exchange(ctx, 2))
        scoped("mamba", lambda: build_mamba(ctx=ctx, pfx="mb"))
        scoped("ffnC", lambda: build_ffn(ctx=ctx, pfx="f1", uid="fC"))
        scoped("ex3", lambda: _emit_exchange(ctx, 3))
        scoped("b1", lambda: build_attn("b1", ctx=ctx, pfx="b1"))
        scoped("ffnD", lambda: build_ffn(ctx=ctx, pfx="f1", uid="fD"))

        def fino(g):
            ctx.s_ex.dma(g.dma_start(out=ctx.xo[:], in_=ctx.x_own[:]))
            g.wait_ge(ctx.s_ex.h, ctx.s_ex.v)
        block.gpsimd(fino)
    return nc


def kernel(**inputs):
    x = np.asarray(inputs["L0_attn_a_x"], dtype=np.float32)
    if "nc" not in _FUSED:
        _FUSED["nc"] = build_fused()
    nc = _FUSED["nc"]
    wmap = {}
    for p, wm in [("f0", _ffn_weight_maps("L0_ffn", inputs)),
                  ("f1", _ffn_weight_maps("L1_ffn", inputs)),
                  ("a0", _attn_maps("a0", inputs)),
                  ("b0", _attn_maps("b0", inputs)),
                  ("b1", _attn_maps("b1", inputs)),
                  ("mb", _mamba_maps(inputs))]:
        for k, v in wm.items():
            wmap[p + k] = v
    masks = wmap.pop("a0masks"); ident = wmap.pop("a0ident")
    identf = wmap.pop("mbidentf")
    for p in ("b0", "b1", "mb"):
        wmap.pop(p + "masks"); wmap.pop(p + "ident")
    in_maps = []
    for c in range(NCORES):
        b, h = c // 2, c % 2
        x0 = np.ascontiguousarray(x[b, h * T:(h + 1) * T].T)
        gates = np.zeros((128, 4), dtype=np.float32)
        gates[:, 0] = float(h)
        gates[:, 1] = 0.0 if h == 1 else -60.0
        gates[:, 2] = 1.0 - float(h)
        in_maps.append(dict(x0=x0, gates=gates, masks=masks, ident=ident,
                            identf=identf, **wmap))
    res = run_bass_kernel_spmd(nc, in_maps, core_ids=list(range(NCORES)))
    out = np.empty((B * S, D), dtype=np.float32)
    for c in range(NCORES):
        out[c * T:(c + 1) * T] = res.results[c]["xo"].T
    return out.reshape(B, S, D)
